# version 12
# speedup vs baseline: 1.2551x; 1.0505x over previous
import contextlib
import numpy as np
import ml_dtypes
import concourse.bass as bass
import concourse.mybir as mybir
from concourse.bass_utils import run_bass_kernel_spmd

F32 = mybir.dt.float32
BF16 = mybir.dt.bfloat16
I32 = mybir.dt.int32
AF = mybir.ActivationFunctionType
ALU = mybir.AluOpType
AX = mybir.AxisListType

NCORES = 8
D = 2048
T = 2048
NH = 16
DH = 128
NC = 16
TB = 512
NTB = T // TB

ISQRT_DH = float(1.0 / np.sqrt(128.0))
MASKNEG = -30000.0
LN_EPS = 1e-5

ENGS = ("y", "g", "t", "v", "s")
ENGMAP = {"y": "sync", "g": "gpsimd", "t": "tensor", "v": "vector", "s": "scalar"}


class Plan:

    def __init__(self):
        self.ops = {e: [] for e in ENGS}
        self.cnt = {}
        self.names = []
        self.pending = {e: [] for e in ENGS}

    def sem(self, *names):
        for n in names:
            if n not in self.cnt:
                self.cnt[n] = 0
                self.names.append(n)

    def add(self, eng, fn, waits=(), incs=()):
        w = self.pending[eng] + [(nm, v) for nm, v in waits if v > 0]
        self.pending[eng] = []
        wd = {}
        for nm, v in w:
            wd[nm] = max(wd.get(nm, 0), v)
        for nm, amt in incs:
            self.cnt[nm] += amt
        if fn is None and incs and not wd:
            for j in range(len(self.ops[eng]) - 1, -1, -1):
                if self.ops[eng][j][0] is not None:
                    if not self.ops[eng][j][2]:
                        self.ops[eng][j][2].extend(incs)
                    else:
                        self.ops[eng].append(
                            (lambda e, S: e.drain(), [], list(incs)))
                    return
        self.ops[eng].append((fn, sorted(wd.items()), list(incs)))

    def now(self, nm):
        return self.cnt[nm]

    def tok(self, nm):
        return (nm, self.cnt[nm])

    def barrier(self, engs=("t", "v", "s")):
        vals = {}
        for e in engs:
            nm = "s" + e
            self.add(e, None, incs=[(nm, 1)])
            vals[nm] = self.cnt[nm]
        wl = list(vals.items())
        for e in ENGS:
            self.pending[e] = self.pending[e] + wl
        return wl

    def emit(self, nc, sems):
        with nc.Block() as block:
            for ename in ENGS:
                ops = self.ops[ename]
                if not ops:
                    continue

                def body(e, ops=ops):
                    for fn, waits, incs in ops:
                        for nm, val in waits:
                            e.wait_ge(sems[nm], val)
                        inst = fn(e, sems) if fn is not None else None
                        if incs:
                            if inst is None:
                                inst = e.nop()
                            for nm, amt in incs:
                                inst = inst.then_inc(sems[nm], amt)

                getattr(block, ENGMAP[ename])(body)


def _bf(x):
    return np.ascontiguousarray(x).astype(ml_dtypes.bfloat16)



def prep_l0attn_host(x_b, W_K, W_Q, W_V, W_O):
    xp = np.ascontiguousarray(np.asarray(x_b, np.float32).T
                              .reshape(NC, 128, T).transpose(1, 0, 2))

    def wqkv(W):
        return np.ascontiguousarray(
            np.asarray(W, np.float32).reshape(NH, DH, NC, 128).transpose(0, 3, 2, 1))

    wo = np.ascontiguousarray(
        np.asarray(W_O, np.float32).reshape(NC, 128, NH, DH).transpose(0, 3, 2, 1))
    p = np.arange(128)[:, None, None]
    d_ = np.arange(4)[None, :, None]
    j2 = np.arange(TB)[None, None, :]
    cm = np.where(d_ * 128 + p <= j2, 0.0, MASKNEG).astype(np.float32)
    return dict(xin=xp, wq=wqkv(W_Q), wk=wqkv(W_K), wv=wqkv(W_V), wo=wo, cm=cm)


def pack_fm(x_b):
    return np.ascontiguousarray(np.asarray(x_b, np.float32).T
                                .reshape(NC, 128, T).transpose(1, 0, 2))


def unpack_fm(r):
    return np.ascontiguousarray(
        np.asarray(r).transpose(1, 0, 2).reshape(D, T).T)



def build_l0attn():
    nc = bass.Bass("TRN2", target_bir_lowering=False, debug=False)
    xin = nc.dram_tensor("xin", [128, NC, T], F32, kind="ExternalInput").ap()
    wq_d = nc.dram_tensor("wq", [NH, 128, NC, DH], F32, kind="ExternalInput").ap()
    wk_d = nc.dram_tensor("wk", [NH, 128, NC, DH], F32, kind="ExternalInput").ap()
    wv_d = nc.dram_tensor("wv", [NH, 128, NC, DH], F32, kind="ExternalInput").ap()
    wo_d = nc.dram_tensor("wo", [NC, 128, NH, DH], F32, kind="ExternalInput").ap()
    cm_d = nc.dram_tensor("cm", [128, 4, TB], F32, kind="ExternalInput").ap()
    r1_d = nc.dram_tensor("r1", [128, NC, T], F32, kind="ExternalOutput").ap()
    zscr_d = nc.dram_tensor("zscr", [NH, 128, T], BF16, kind="Internal").ap()

    P = Plan()
    P.sem("dg", "dy", "st", "sv", "ss", "sg",
          "dr", "dw0", "dw1", "dzs0", "dzs1", "dzq", "dwo0", "dwo1", "da0", "da1", "ds0", "ds1", "dcm")

    with contextlib.ExitStack() as ctx:
        E = ctx.enter_context
        rb = E(nc.sbuf_tensor("rb", [128, NC, T], BF16))
        q_s = E(nc.sbuf_tensor("q_s", [128, T], BF16))
        k_s = E(nc.sbuf_tensor("k_s", [128, T], BF16))
        vt_s = E(nc.sbuf_tensor("vt_s", [128, NC, 128], BF16))
        attnT = E(nc.sbuf_tensor("attnT", [128, NC, TB], BF16))
        wqb = [E(nc.sbuf_tensor(f"wqb{i}", [128, NC, DH], BF16)) for i in range(2)]
        wkb = [E(nc.sbuf_tensor(f"wkb{i}", [128, NC, DH], BF16)) for i in range(2)]
        wvb = [E(nc.sbuf_tensor(f"wvb{i}", [128, NC, DH], BF16)) for i in range(2)]
        wob = [E(nc.sbuf_tensor(f"wob{i}", [128, NH, DH], BF16)) for i in range(2)]
        zqb = [E(nc.sbuf_tensor("zqb0", [128, NH, TB], BF16))]
        cm_s = E(nc.sbuf_tensor("cm_s", [128, 4, TB], F32))
        rrep = E(nc.sbuf_tensor("rrep", [128, T], F32))
        nmu_bf = E(nc.sbuf_tensor("nmu_bf", [1, T], BF16))
        mhat_bf = E(nc.sbuf_tensor("mhat_bf", [1, T], BF16))
        lrec_bf = E(nc.sbuf_tensor("lrec_bf", [1, TB], BF16))
        rowa = E(nc.sbuf_tensor("rowa", [1, T], F32))
        rowb = E(nc.sbuf_tensor("rowb", [1, T], F32))
        kmax_s = E(nc.sbuf_tensor("kmax_s", [1, 8], F32))
        rstd_cols = E(nc.sbuf_tensor("rstd_cols", [128, NC], F32))
        wbar = E(nc.sbuf_tensor("wbar", [1, 384], BF16))
        scf = [E(nc.sbuf_tensor(f"scf{i}", [128, TB], F32)) for i in range(2)]
        rep_sb = E(nc.sbuf_tensor("rep_sb", [128, TB], F32))
        mrep = E(nc.sbuf_tensor("mrep", [128, TB], F32))
        zbs = [E(nc.sbuf_tensor(f"zbs{i}", [128, TB], BF16)) for i in range(2)]
        r0t = [E(nc.sbuf_tensor(f"r0t{i}", [128, TB], F32)) for i in range(2)]
        r1t = [E(nc.sbuf_tensor(f"r1t{i}", [128, TB], F32)) for i in range(2)]
        cstf = E(nc.sbuf_tensor("cstf", [128, 8], F32))
        onec_b = E(nc.sbuf_tensor("onec_b", [128, 1], BF16))
        onec_f = E(nc.sbuf_tensor("onec_f", [128, 1], F32))
        oner_b = E(nc.sbuf_tensor("oner_b", [1, 128], BF16))
        oner_f = E(nc.sbuf_tensor("oner_f", [1, 128], F32))
        pA = E(nc.psum_tensor("pA", [128, TB], F32))
        pB = E(nc.psum_tensor("pB", [128, TB], F32))
        PS = [pA, pB]
        sems = {n: E(nc.semaphore(n)) for n in P.names}

        def dma(q, out_ap, in_ap, waits=(), sem="dg"):
            P.add(q, lambda e, S, o=out_ap, i=in_ap: e.dma_start(out=o, in_=i),
                  waits=waits, incs=[(sem, 16)])
            return (sem, P.now(sem))

        for i in range(4):
            rb_ok = dma("g", rb[:, 4 * i:4 * (i + 1), :], xin[:, 4 * i:4 * (i + 1), :], sem="dr")
        cm_ok = dma("y", cm_s[:], cm_d[:], sem="dcm")
        P.add("g", lambda e, S: e.memset(onec_b[:], 1.0))
        P.add("g", lambda e, S: e.memset(onec_f[:], 1.0))
        P.add("g", lambda e, S: e.memset(oner_b[:], 1.0))
        P.add("g", lambda e, S: e.memset(oner_f[:], 1.0))
        for j, val in enumerate([LN_EPS, -0.5, 1.0 / 128.0, 0.5]):
            P.add("g", lambda e, S, j=j, val=val: e.memset(cstf[:, j:j + 1], val),
                  incs=([("sg", 1)] if j == 3 else []))
        consts_ok = P.tok("sg")
        wld = {0: [dma("g", wqb[0][:], wq_d[0], sem="dw0"),
                   dma("g", wkb[0][:], wk_d[0], sem="dw0"),
                   dma("g", wvb[0][:], wv_d[0], sem="dw0")]}

        for tb in range(NTB):
            sl = slice(tb * TB, (tb + 1) * TB)
            for dc in range(NC):
                P.add("v", lambda e, S, dc=dc, sl=sl: e.tensor_tensor(
                    attnT[:, dc, :], rb[:, dc, sl], rb[:, dc, sl], ALU.mult),
                    waits=[rb_ok, P.tok("st")] if dc == 0 else [])
            P.add("v", None, incs=[("sv", 1)])
            sq_ok = P.tok("sv")
            for dc in range(NC):
                P.add("t", lambda e, S, dc=dc, sl=sl: e.matmul(
                    pA[0:1, :], onec_b[:], rb[:, dc, sl],
                    start=(dc == 0), stop=(dc == NC - 1)),
                    waits=[rb_ok, consts_ok, P.tok("ss")] if dc == 0 else [])
            for dc in range(NC):
                P.add("t", lambda e, S, dc=dc: e.matmul(
                    pB[0:1, :], onec_b[:], attnT[:, dc, :],
                    start=(dc == 0), stop=(dc == NC - 1)),
                    waits=[sq_ok] if dc == 0 else [])
            P.add("t", None, incs=[("st", 1)])
            mm_ok = P.tok("st")
            P.add("s", lambda e, S, sl=sl: e.mul(rowa[0:1, sl], pA[0:1, :], 1.0 / D),
                  waits=[mm_ok])
            P.add("s", lambda e, S, sl=sl: e.mul(rowb[0:1, sl], pB[0:1, :], 1.0 / D),
                  incs=[("ss", 1)])
        P.barrier(("t", "v", "s"))
        P.add("v", lambda e, S: e.tensor_scalar_mul(nmu_bf[0:1, :], rowa[0:1, :], -1.0))
        P.add("v", lambda e, S: e.tensor_tensor(
            rowa[0:1, :], rowa[0:1, :], rowa[0:1, :], ALU.mult))
        P.add("v", lambda e, S: e.tensor_tensor(
            rowb[0:1, :], rowb[0:1, :], rowa[0:1, :], ALU.subtract), incs=[("sv", 1)])
        P.add("s", lambda e, S: e.activation(rowa[0:1, :], rowb[0:1, :], AF.Ln,
                                             bias=cstf[0:1, 0:1]),
              waits=[P.tok("sv"), consts_ok])
        P.add("s", lambda e, S: e.activation(rowb[0:1, :], rowa[0:1, :], AF.Exp,
                                             scale=cstf[0:1, 1:2]),
              incs=[("ss", 1)])
        rstd_ok = P.tok("ss")
        for tb in range(NTB):
            sl = slice(tb * TB, (tb + 1) * TB)
            pX = PS[tb % 2]
            P.add("t", lambda e, S, sl=sl, pX=pX: e.matmul(
                pX[:], oner_f[:], rowb[0:1, sl], start=True, stop=True),
                waits=[rstd_ok], incs=[("st", 1)])
            P.add("v", lambda e, S, sl=sl, pX=pX: e.tensor_copy(rrep[:, sl], pX[:]),
                  waits=[P.tok("st")], incs=[("sv", 1)])
        P.add("t", None, waits=[P.tok("sv")])
        for tc in range(NC):
            P.add("t", lambda e, S, tc=tc: e.matmul(
                pA[:, tc:tc + 1], rowb[0:1, tc * 128:(tc + 1) * 128], oner_f[0:1, 0:1],
                start=True, stop=True))
        P.add("t", None, incs=[("st", 1)])
        P.add("v", lambda e, S: e.tensor_copy(rstd_cols[:], pA[:, 0:NC]),
              waits=[P.tok("st")], incs=[("sv", 1)])
        P.barrier(("t", "v", "s"))

        bankfree = [P.tok("sv"), P.tok("sv")]
        attnT_free = P.tok("st")
        zb_stored = [("dzs0", 0), ("dzs1", 0)]

        w_consumed = {}
        for h in range(NH):
            buf = h % 2
            wq, wk, wv = wqb[buf], wkb[buf], wvb[buf]
            if h > 0:
                wwait = [w_consumed[h - 2]] if h >= 2 else []
                wld[h] = [dma("g", wq[:], wq_d[h], waits=wwait, sem=f"dw{buf}"),
                          dma("g", wk[:], wk_d[h], sem=f"dw{buf}"),
                          dma("g", wv[:], wv_d[h], sem=f"dw{buf}")]
            w_ok = wld[h][-1]

            P.add("t", None, waits=[w_ok, bankfree[0]])
            for j, wt in enumerate((wq, wk, wv)):
                for dc in range(NC):
                    P.add("t", lambda e, S, j=j, wt=wt, dc=dc: e.matmul(
                        pA[0:1, j * 128:(j + 1) * 128], onec_b[:], wt[:, dc, :],
                        start=(dc == 0), stop=(dc == NC - 1)))
            P.add("t", None, incs=[("st", 1)])
            P.add("s", lambda e, S: e.copy(wbar[0:1, :], pA[0:1, 0:384]),
                  waits=[P.tok("st")], incs=[("ss", 1)])
            bankfree[0] = P.tok("ss")
            wbar_ok = P.tok("ss")

            for wb_i, wt, dst in ((0, wq, q_s), (1, wk, k_s)):
                for tb in range(NTB):
                    sl = slice(tb * TB, (tb + 1) * TB)
                    bk = tb % 2
                    pX = PS[bk]
                    P.add("t", None, waits=[bankfree[bk], wbar_ok])
                    for dc in range(NC):
                        P.add("t", lambda e, S, dc=dc, sl=sl, wt=wt, pX=pX: e.matmul(
                            pX[:], wt[:, dc, :], rb[:, dc, sl],
                            start=(dc == 0), stop=False))
                    P.add("t", lambda e, S, sl=sl, pX=pX, wb_i=wb_i: e.matmul(
                        pX[:], wbar[0:1, wb_i * 128:(wb_i + 1) * 128], nmu_bf[0:1, sl],
                        start=False, stop=True), incs=[("st", 1)])
                    P.add("v", lambda e, S, sl=sl, pX=pX, dst=dst: e.tensor_tensor(
                        dst[:, sl], pX[:], rrep[:, sl], ALU.mult),
                        waits=[P.tok("st")], incs=[("sv", 1)])
                    bankfree[bk] = P.tok("sv")

            for tc in range(NC):
                csl = slice(tc * 128, (tc + 1) * 128)
                bk = tc % 2
                pX = PS[bk]
                P.add("t", None, waits=[bankfree[bk]])
                for dc in range(NC):
                    P.add("t", lambda e, S, dc=dc, csl=csl, pX=pX, wv=wv: e.matmul(
                        pX[:, 0:128], rb[:, dc, csl], wv[:, dc, :],
                        start=(dc == 0), stop=False))
                P.add("t", lambda e, S, csl=csl, pX=pX: e.matmul(
                    pX[:, 0:128], nmu_bf[0:1, csl], wbar[0:1, 256:384],
                    start=False, stop=True), incs=[("st", 1)])
                P.add("v", lambda e, S, tc=tc, pX=pX: e.tensor_scalar(
                    vt_s[:, tc, :], pX[:, 0:128], rstd_cols[:, tc:tc + 1], None, ALU.mult),
                    waits=[P.tok("st")], incs=[("sv", 1)])
                bankfree[bk] = P.tok("sv")
            w_consumed[h] = P.tok("st")

            for src, row in ((q_s, rowa), (k_s, rowb)):
                for tb in range(NTB):
                    sl = slice(tb * TB, (tb + 1) * TB)
                    bk = tb % 2
                    pX = PS[bk]
                    P.add("v", lambda e, S, sl=sl, src=src: e.tensor_tensor(
                        scf[0][:], src[:, sl], src[:, sl], ALU.mult), incs=[("sv", 1)])
                    P.add("t", lambda e, S, pX=pX: e.matmul(
                        pX[0:1, :], onec_f[:], scf[0][:], start=True, stop=True),
                        waits=[P.tok("sv"), bankfree[bk]], incs=[("st", 1)])
                    P.add("s", lambda e, S, sl=sl, pX=pX, row=row: e.copy(
                        row[0:1, sl], pX[0:1, :]),
                        waits=[P.tok("st")], incs=[("ss", 1)])
                    bankfree[bk] = P.tok("ss")
                    P.add("v", None, waits=[P.tok("st")])
            P.barrier(("t", "v", "s"))
            P.add("v", lambda e, S: e.reduce_max(
                kmax_s[0:1, 0:1], rowb[0:1, :], axis=AX.X), incs=[("sv", 1)])
            P.add("s", lambda e, S: e.activation(
                kmax_s[0:1, 1:2], kmax_s[0:1, 0:1], AF.Ln, scale=cstf[0:1, 2:3]),
                waits=[P.tok("sv")])
            P.add("s", lambda e, S: e.activation(
                kmax_s[0:1, 2:3], kmax_s[0:1, 1:2], AF.Exp, scale=cstf[0:1, 3:4]))
            P.add("s", lambda e, S: e.activation(rowa[0:1, :], rowa[0:1, :], AF.Ln))
            P.add("s", lambda e, S: e.activation(
                rowa[0:1, :], rowa[0:1, :], AF.Exp, scale=cstf[0:1, 3:4]),
                incs=[("ss", 1)])
            P.add("v", lambda e, S: e.tensor_scalar(
                mhat_bf[0:1, :], rowa[0:1, :], kmax_s[0:1, 2:3], 0.05, ALU.mult, ALU.add),
                waits=[P.tok("ss")], incs=[("sv", 1)])
            mhat_ok = P.tok("sv")

            for qb in range(NTB):
                sl = slice(qb * TB, (qb + 1) * TB)
                nkc = 4 * qb + 4
                P.add("t", lambda e, S, sl=sl: e.matmul(
                    pB[:], oner_b[:], mhat_bf[0:1, sl], start=True, stop=True),
                    waits=[mhat_ok, bankfree[1]], incs=[("st", 1)])
                P.add("s", lambda e, S: e.copy(mrep[:], pB[:]),
                      waits=[P.tok("st")], incs=[("ss", 1)])
                bankfree[1] = P.tok("ss")
                mrep_ok = P.tok("ss")
                scf_free = [P.tok("ss"), P.tok("ss")]
                for kc in range(nkc):
                    bk = kc % 2
                    pX = PS[bk]
                    P.add("t", lambda e, S, kc=kc, sl=sl, pX=pX: e.matmul(
                        pX[:], k_s[:, kc * 128:(kc + 1) * 128], q_s[:, sl],
                        start=True, stop=True),
                        waits=[bankfree[bk]], incs=[("st", 1)])
                    mm_tok = P.tok("st")
                    sb = scf[kc % 2]
                    P.add("v", lambda e, S, pX=pX, sb=sb: e.scalar_tensor_tensor(
                        sb[:], pX[:], ISQRT_DH, mrep[:], ALU.mult, ALU.subtract),
                        waits=[mm_tok, mrep_ok, scf_free[kc % 2]], incs=[("sv", 1)])
                    bankfree[bk] = P.tok("sv")
                    dd = kc - 4 * qb
                    if dd >= 0:
                        P.add("v", lambda e, S, sb=sb, dd=dd: e.tensor_tensor(
                            sb[:], sb[:], cm_s[:, dd, :], ALU.add),
                            waits=[cm_ok], incs=[("sv", 1)])
                    P.add("s", lambda e, S, kc=kc, sb=sb: e.activation(
                        attnT[:, kc, :], sb[:], AF.Exp),
                        waits=[P.tok("sv"), attnT_free], incs=[("ss", 1)])
                    scf_free[kc % 2] = P.tok("ss")
                exp_ok = P.tok("ss")
                P.add("t", None, waits=[exp_ok, bankfree[0], bankfree[1]])
                for kc in range(nkc):
                    P.add("t", lambda e, S, kc=kc, nkc=nkc: e.matmul(
                        pA[:], vt_s[:, kc, :], attnT[:, kc, :],
                        start=(kc == 0), stop=(kc == nkc - 1)))
                for kc in range(nkc):
                    P.add("t", lambda e, S, kc=kc, nkc=nkc: e.matmul(
                        pB[0:1, :], onec_b[:], attnT[:, kc, :],
                        start=(kc == 0), stop=(kc == nkc - 1)))
                P.add("t", None, incs=[("st", 1)])
                zl_ok = P.tok("st")
                attnT_free = zl_ok
                P.add("v", lambda e, S: e.reciprocal(rowb[0:1, 0:TB], pB[0:1, :]),
                      waits=[zl_ok])
                P.add("v", lambda e, S: e.tensor_copy(lrec_bf[0:1, :], rowb[0:1, 0:TB]),
                      incs=[("sv", 1)])
                P.add("t", lambda e, S: e.matmul(
                    pB[:], oner_b[:], lrec_bf[0:1, :], start=True, stop=True),
                    waits=[P.tok("sv")], incs=[("st", 1)])
                P.add("s", lambda e, S: e.copy(rep_sb[:], pB[:]),
                      waits=[P.tok("st")], incs=[("ss", 1)])
                bankfree[1] = P.tok("ss")
                zb = zbs[qb % 2]
                P.add("v", lambda e, S, zb=zb: e.tensor_tensor(
                    zb[:], pA[:], rep_sb[:], ALU.mult),
                    waits=[P.tok("ss"), zb_stored[qb % 2]], incs=[("sv", 1)])
                bankfree[0] = P.tok("sv")
                zb_stored[qb % 2] = dma("y", zscr_d[h][:, sl], zb[:],
                                        waits=[P.tok("sv")], sem=f"dzs{qb % 2}")
            P.barrier(("t", "v", "s"))

        zq_used = {}
        wo_last = [("st", 0), ("st", 0)]
        r0_last = [("sv", 0), ("sv", 0)]
        r1_last = [("ds0", 0), ("ds1", 0)]
        for qb in range(NTB):
            sl = slice(qb * TB, (qb + 1) * TB)
            zbuf = zqb[0]
            zsrc = zscr_d[:, :, sl].rearrange("h p t -> p h t")
            zw = [dma("g", zbuf[:, 4 * i:4 * (i + 1), :], zsrc[:, 4 * i:4 * (i + 1), :],
                      waits=[P.tok("dzs0"), P.tok("dzs1")] + ([zq_used[qb - 1]] if qb >= 1 else []),
                      sem="dzq")
                  for i in range(4)]
            z_ok = zw[-1]
            for dc in range(NC):
                par = dc % 2
                wbuf = wob[par]
                wo_ok = dma("g", wbuf[:], wo_d[dc], waits=[wo_last[par]], sem=f"dwo{par}")
                r0buf = r0t[par]
                r0_ok = dma("y", r0buf[:], xin[:, dc, sl], waits=[r0_last[par]], sem=f"da{par}")
                pX = PS[par]
                P.add("t", None, waits=[z_ok, wo_ok, bankfree[par]])
                for hh in range(NH):
                    P.add("t", lambda e, S, hh=hh, wbuf=wbuf, zbuf=zbuf, pX=pX: e.matmul(
                        pX[:], wbuf[:, hh, :], zbuf[:, hh, :],
                        start=(hh == 0), stop=(hh == NH - 1)))
                P.add("t", None, incs=[("st", 1)])
                wo_last[par] = P.tok("st")
                r1b = r1t[par]
                P.add("v", lambda e, S, pX=pX, r0buf=r0buf, r1b=r1b: e.tensor_tensor(
                    r1b[:], pX[:], r0buf[:], ALU.add),
                    waits=[P.tok("st"), r0_ok, r1_last[par]], incs=[("sv", 1)])
                bankfree[par] = P.tok("sv")
                r0_last[par] = P.tok("sv")
                r1_last[par] = dma("y", r1_d[:, dc, sl], r1b[:], waits=[P.tok("sv")], sem=f"ds{par}")
            zq_used[qb] = P.tok("st")

        P.add("y", None, waits=[P.tok(s) for s in
              ("dr", "dw0", "dw1", "dzs0", "dzs1", "dzq", "dwo0", "dwo1", "da0", "da1", "ds0", "ds1", "dcm")])
        P.emit(nc, sems)
    return nc




def _had128():
    i = np.arange(128)
    pc = np.zeros((128, 128), np.int64)
    H = np.ones((128, 128), np.float32)
    for b in range(7):
        bi = ((i >> b) & 1)
        H *= np.where(np.outer(bi, bi) == 1, -1.0, 1.0)
    return H


def _h16sign(a, b):
    return (-1.0) ** bin(a & b).count("1")


def prep_l0ffn_host(SU_ug, SU_dp, Wscale_ug, Wscale_dp, W_up, W_gate, W_down):
    h128 = _had128()
    su_ug = np.ascontiguousarray(np.asarray(SU_ug, np.float32).reshape(NC, 128).T)
    su_dp = np.ascontiguousarray(np.asarray(SU_dp, np.float32).reshape(64, 128).T)
    wsc_up = np.ascontiguousarray(np.asarray(Wscale_ug[:8192], np.float32).reshape(64, 128).T)
    wsc_gt = np.ascontiguousarray(np.asarray(Wscale_ug[8192:], np.float32).reshape(64, 128).T)
    wsc_dn = np.ascontiguousarray(np.asarray(Wscale_dp, np.float32).reshape(NC, 128).T)
    wup = np.ascontiguousarray(np.asarray(W_up, np.float32)
                               .reshape(64, 128, NC, 128).transpose(0, 3, 2, 1))
    wgt = np.ascontiguousarray(np.asarray(W_gate, np.float32)
                               .reshape(64, 128, NC, 128).transpose(0, 3, 2, 1))
    wdn = np.ascontiguousarray(np.asarray(W_down, np.float32)
                               .reshape(NC, 128, 64, 128).transpose(0, 3, 2, 1))
    return dict(h128=h128, su_ug=su_ug, su_dp=su_dp, wsc_up=wsc_up, wsc_gt=wsc_gt,
                wsc_dn=wsc_dn, wup=wup, wgt=wgt, wdn=wdn)


def build_l0ffn():
    nc = bass.Bass("TRN2", target_bir_lowering=False, debug=False)
    r1i = nc.dram_tensor("r1i", [128, NC, T], F32, kind="ExternalInput").ap()
    h128_d = nc.dram_tensor("h128", [128, 128], F32, kind="ExternalInput").ap()
    suug_d = nc.dram_tensor("su_ug", [128, NC], F32, kind="ExternalInput").ap()
    sudp_d = nc.dram_tensor("su_dp", [128, 64], F32, kind="ExternalInput").ap()
    wscu_d = nc.dram_tensor("wsc_up", [128, 64], F32, kind="ExternalInput").ap()
    wscg_d = nc.dram_tensor("wsc_gt", [128, 64], F32, kind="ExternalInput").ap()
    wscd_d = nc.dram_tensor("wsc_dn", [128, NC], F32, kind="ExternalInput").ap()
    wup_d = nc.dram_tensor("wup", [64, 128, NC, 128], F32, kind="ExternalInput").ap()
    wgt_d = nc.dram_tensor("wgt", [64, 128, NC, 128], F32, kind="ExternalInput").ap()
    wdn_d = nc.dram_tensor("wdn", [NC, 128, 64, 128], F32, kind="ExternalInput").ap()
    r2_d = nc.dram_tensor("r2", [128, NC, T], F32, kind="ExternalOutput").ap()

    P = Plan()
    P.sem("st", "sv", "ss", "sg", "dcst", "dr1", "dwu0", "dwu1", "dwg0", "dwg1",
          "dwd0", "dwd1", "ds0", "ds1")

    S1 = float(1.0 / np.sqrt(2048.0))
    S2 = float(1.0 / np.sqrt(8192.0))

    with contextlib.ExitStack() as ctx:
        E = ctx.enter_context
        r1sup = E(nc.sbuf_tensor("r1sup", [128, NC, TB], F32))
        xln = E(nc.sbuf_tensor("xln", [128, NC, TB], BF16))
        t1 = E(nc.sbuf_tensor("t1", [128, NC, TB], BF16))
        hbuf = E(nc.sbuf_tensor("hbuf", [128, 64, TB], BF16))
        wub = [E(nc.sbuf_tensor(f"wub{i}", [128, NC, 128], BF16)) for i in range(2)]
        wgb = [E(nc.sbuf_tensor(f"wgb{i}", [128, NC, 128], BF16)) for i in range(2)]
        wdb = [E(nc.sbuf_tensor(f"wdb{i}", [128, 64, 128], BF16)) for i in range(2)]
        h128p = E(nc.sbuf_tensor("h128p", [128, 128], BF16))
        h128n = E(nc.sbuf_tensor("h128n", [128, 128], BF16))
        su_ug = E(nc.sbuf_tensor("su_ug_s", [128, NC], F32))
        su_dp = E(nc.sbuf_tensor("su_dp_s", [128, 64], F32))
        wscu = E(nc.sbuf_tensor("wscu_s", [128, 64], F32))
        wscg = E(nc.sbuf_tensor("wscg_s", [128, 64], F32))
        wscd = E(nc.sbuf_tensor("wscd_s", [128, NC], F32))
        sqs = E(nc.sbuf_tensor("sqs", [128, TB], BF16))
        rowm = E(nc.sbuf_tensor("rowm", [1, TB], F32))
        rowe = E(nc.sbuf_tensor("rowe", [1, TB], F32))
        nmurep = E(nc.sbuf_tensor("nmurep", [128, TB], F32))
        rstdrep = E(nc.sbuf_tensor("rstdrep", [128, TB], F32))
        dAb = [nmurep, rstdrep]
        tmpf = E(nc.sbuf_tensor("tmpf", [128, TB], F32))
        upt = E(nc.sbuf_tensor("upt", [128, TB], F32))
        silt = E(nc.sbuf_tensor("silt", [128, TB], F32))
        btmp = E(nc.sbuf_tensor("btmp", [128, TB], BF16))
        r2t = [E(nc.sbuf_tensor(f"r2t{i}", [128, TB], F32)) for i in range(2)]
        onec_b = E(nc.sbuf_tensor("onec_b", [128, 1], BF16))
        onec_f = E(nc.sbuf_tensor("onec_f", [128, 1], F32))
        oner_f = E(nc.sbuf_tensor("oner_f", [1, 128], F32))
        cstf = E(nc.sbuf_tensor("cstf", [128, 8], F32))
        pA = E(nc.psum_tensor("pA", [128, TB], F32))
        pB = E(nc.psum_tensor("pB", [128, TB], F32))
        PS = [pA, pB]
        sems = {n: E(nc.semaphore(n)) for n in P.names}

        def dma(q, out_ap, in_ap, sem, waits=()):
            P.add(q, lambda e, S, o=out_ap, i=in_ap: e.dma_start(out=o, in_=i),
                  waits=waits, incs=[(sem, 16)])
            return (sem, P.now(sem))

        for t_s, t_d in ((h128p, h128_d), (su_ug, suug_d), (su_dp, sudp_d),
                         (wscu, wscu_d), (wscg, wscg_d), (wscd, wscd_d)):
            cst_ok = dma("g", t_s[:], t_d[:], "dcst")
        P.add("g", lambda e, S: e.memset(onec_b[:], 1.0))
        P.add("g", lambda e, S: e.memset(onec_f[:], 1.0))
        P.add("g", lambda e, S: e.memset(oner_f[:], 1.0))
        P.add("g", lambda e, S: e.memset(cstf[:, 0:1], LN_EPS))
        P.add("g", lambda e, S: e.memset(cstf[:, 1:2], -0.5), incs=[("sg", 1)])
        consts_ok = P.tok("sg")
        P.add("v", lambda e, S: e.tensor_scalar_mul(h128n[:], h128p[:], -1.0),
              waits=[cst_ok, consts_ok])
        P.add("v", lambda e, S: e.tensor_scalar_mul(wscu[:], wscu[:], S1))
        P.add("v", lambda e, S: e.tensor_scalar_mul(wscg[:], wscg[:], S1))
        P.add("v", lambda e, S: e.tensor_scalar_mul(wscd[:], wscd[:], S2),
              incs=[("sv", 1)])
        P.barrier(("v",))

        bankfree = [P.tok("sv"), P.tok("sv")]
        wu_last = [("st", 0), ("st", 0)]
        wg_last = [("st", 0), ("st", 0)]
        wd_last = [("st", 0), ("st", 0)]
        r2_last = [("ds0", 0), ("ds1", 0)]
        r1_used = ("sv", 0)

        for ts in range(NTB):
            sl = slice(ts * TB, (ts + 1) * TB)
            r1_ok = dma("g", r1sup[:], r1i[:, :, sl], "dr1", waits=[r1_used])

            for dc in range(NC):
                P.add("v", lambda e, S, dc=dc: e.tensor_tensor(
                    sqs[:], r1sup[:, dc, :], r1sup[:, dc, :], ALU.mult),
                    waits=[r1_ok, P.tok("st")], incs=[("sv", 1)])
                P.add("t", lambda e, S, dc=dc: e.matmul(
                    pB[0:1, :], onec_b[:], sqs[:],
                    start=(dc == 0), stop=(dc == NC - 1)),
                    waits=[P.tok("sv")] + ([bankfree[1]] if dc == 0 else []),
                    incs=[("st", 1)])
            sq_mm_ok = P.tok("st")
            P.add("t", None, waits=[bankfree[0], r1_ok])
            for dc in range(NC):
                P.add("t", lambda e, S, dc=dc: e.matmul(
                    pA[0:1, :], onec_f[:], r1sup[:, dc, :],
                    start=(dc == 0), stop=(dc == NC - 1)))
            P.add("t", None, incs=[("st", 1)])
            sx_ok = P.tok("st")
            P.add("s", lambda e, S: e.mul(rowm[0:1, :], pA[0:1, :], 1.0 / D),
                  waits=[sx_ok])
            P.add("s", lambda e, S: e.mul(rowe[0:1, :], pB[0:1, :], 1.0 / D),
                  incs=[("ss", 1)])
            st_ok = P.tok("ss")
            P.add("v", lambda e, S: e.tensor_tensor(
                tmpf[0:1, 0:TB], rowm[0:1, :], rowm[0:1, :], ALU.mult), waits=[st_ok])
            P.add("v", lambda e, S: e.tensor_tensor(
                rowe[0:1, :], rowe[0:1, :], tmpf[0:1, 0:TB], ALU.subtract))
            P.add("v", lambda e, S: e.tensor_scalar_mul(rowm[0:1, :], rowm[0:1, :], -1.0),
                  incs=[("sv", 1)])
            P.add("s", lambda e, S: e.activation(rowe[0:1, :], rowe[0:1, :], AF.Ln,
                                                 bias=cstf[0:1, 0:1]), waits=[P.tok("sv")])
            P.add("s", lambda e, S: e.activation(rowe[0:1, :], rowe[0:1, :], AF.Exp,
                                                 scale=cstf[0:1, 1:2]), incs=[("ss", 1)])
            P.add("t", lambda e, S: e.matmul(pA[:], oner_f[:], rowm[0:1, :],
                                             start=True, stop=True),
                  waits=[P.tok("ss"), bankfree[0]], incs=[("st", 1)])
            P.add("v", lambda e, S: e.tensor_copy(nmurep[:], pA[:]),
                  waits=[P.tok("st")], incs=[("sv", 1)])
            bankfree[0] = P.tok("sv")
            P.add("t", lambda e, S: e.matmul(pB[:], oner_f[:], rowe[0:1, :],
                                             start=True, stop=True),
                  waits=[P.tok("ss"), bankfree[1], sq_mm_ok], incs=[("st", 1)])
            P.add("v", lambda e, S: e.tensor_copy(rstdrep[:], pB[:]),
                  waits=[P.tok("st")], incs=[("sv", 1)])
            bankfree[1] = P.tok("sv")
            rep_ok = P.tok("sv")

            for dc in range(NC):
                P.add("v", lambda e, S, dc=dc: e.tensor_tensor(
                    tmpf[:], r1sup[:, dc, :], nmurep[:], ALU.add), waits=[rep_ok] if dc == 0 else [])
                P.add("v", lambda e, S, dc=dc: e.scalar_tensor_tensor(
                    xln[:, dc, :], tmpf[:], su_ug[:, dc:dc + 1], rstdrep[:],
                    ALU.mult, ALU.mult))
            P.add("v", None, incs=[("sv", 1)])
            xln_ok = P.tok("sv")

            for ec in range(NC):
                bk = ec % 2
                pX = PS[bk]
                P.add("t", None, waits=[xln_ok, bankfree[bk]])
                for jc in range(NC):
                    hs = h128p if _h16sign(ec, jc) > 0 else h128n
                    P.add("t", lambda e, S, jc=jc, hs=hs, pX=pX: e.matmul(
                        pX[:], hs[:], xln[:, jc, :],
                        start=(jc == 0), stop=(jc == NC - 1)))
                P.add("t", None, incs=[("st", 1)])
                P.add("v", lambda e, S, ec=ec, pX=pX: e.tensor_copy(t1[:, ec, :], pX[:]),
                      waits=[P.tok("st")], incs=[("sv", 1)])
                bankfree[bk] = P.tok("sv")
            t1_ok = P.tok("sv")

            for oc in range(64):
                par = oc % 2
                wu_ok = dma("g", wub[par][:], wup_d[oc], f"dwu{par}", waits=[wu_last[par]])
                wg_ok = dma("g", wgb[par][:], wgt_d[oc], f"dwg{par}", waits=[wg_last[par]])
                P.add("t", None, waits=[t1_ok, wu_ok, wg_ok, bankfree[0], bankfree[1]])
                for dc in range(NC):
                    P.add("t", lambda e, S, dc=dc, par=par: e.matmul(
                        pA[:], wub[par][:, dc, :], t1[:, dc, :],
                        start=(dc == 0), stop=(dc == NC - 1)))
                for dc in range(NC):
                    P.add("t", lambda e, S, dc=dc, par=par: e.matmul(
                        pB[:], wgb[par][:, dc, :], t1[:, dc, :],
                        start=(dc == 0), stop=(dc == NC - 1)))
                P.add("t", None, incs=[("st", 1)])
                mm_ok = P.tok("st")
                wu_last[par] = mm_ok
                wg_last[par] = mm_ok
                P.add("v", lambda e, S, oc=oc: e.tensor_scalar(
                    upt[:], pA[:], wscu[:, oc:oc + 1], None, ALU.mult),
                    waits=[mm_ok], incs=[("sv", 1)])
                bankfree[0] = P.tok("sv")
                P.add("s", lambda e, S, oc=oc: e.activation(
                    silt[:], pB[:], AF.Silu, scale=wscg[:, oc:oc + 1]),
                    waits=[mm_ok, P.tok("sv")], incs=[("ss", 1)])
                bankfree[1] = P.tok("ss")
                P.add("v", lambda e, S, oc=oc: e.scalar_tensor_tensor(
                    hbuf[:, oc, :], silt[:], su_dp[:, oc:oc + 1], upt[:],
                    ALU.mult, ALU.mult),
                    waits=[P.tok("ss")], incs=[("sv", 1)])
            h_ok = P.tok("sv")

            for stg in range(6):
                stride = 32 >> stg
                for base in range(0, 64, 2 * stride):
                    for i in range(base, base + stride):
                        j = i + stride
                        P.add("v", lambda e, S, i=i, j=j: e.tensor_tensor(
                            btmp[:], hbuf[:, i, :], hbuf[:, j, :], ALU.subtract))
                        P.add("v", lambda e, S, i=i, j=j: e.tensor_tensor(
                            hbuf[:, i, :], hbuf[:, i, :], hbuf[:, j, :], ALU.add))
                        P.add("v", lambda e, S, j=j: e.tensor_copy(
                            hbuf[:, j, :], btmp[:]))
            P.add("v", None, incs=[("sv", 1)])
            bfl_ok = P.tok("sv")
            for oc in range(64):
                bk = oc % 2
                pX = PS[bk]
                P.add("t", lambda e, S, oc=oc, pX=pX: e.matmul(
                    pX[:], h128p[:], hbuf[:, oc, :], start=True, stop=True),
                    waits=[bfl_ok, bankfree[bk]], incs=[("st", 1)])
                P.add("v", lambda e, S, oc=oc, pX=pX: e.tensor_copy(
                    hbuf[:, oc, :], pX[:]),
                    waits=[P.tok("st")], incs=[("sv", 1)])
                bankfree[bk] = P.tok("sv")
            t2_ok = P.tok("sv")

            for dco in range(NC):
                par = dco % 2
                wd_ok = dma("g", wdb[par][:], wdn_d[dco], f"dwd{par}", waits=[wd_last[par]])
                pX = PS[par]
                P.add("t", None, waits=[t2_ok, wd_ok, bankfree[par]])
                for ic in range(64):
                    P.add("t", lambda e, S, ic=ic, par=par, pX=pX: e.matmul(
                        pX[:], wdb[par][:, ic, :], hbuf[:, ic, :],
                        start=(ic == 0), stop=(ic == 63)))
                P.add("t", None, incs=[("st", 1)])
                wd_last[par] = P.tok("st")
                rb2 = r2t[par]
                P.add("v", lambda e, S, dco=dco, pX=pX, rb2=rb2: e.scalar_tensor_tensor(
                    rb2[:], pX[:], wscd[:, dco:dco + 1], r1sup[:, dco, :],
                    ALU.mult, ALU.add),
                    waits=[P.tok("st"), r2_last[par]], incs=[("sv", 1)])
                bankfree[par] = P.tok("sv")
                r2_last[par] = dma("y", r2_d[:, dco, sl], rb2[:], f"ds{par}",
                                   waits=[P.tok("sv")])
            r1_used = P.tok("sv")

        P.add("y", None, waits=[P.tok("ds0"), P.tok("ds1")])
        P.emit(nc, sems)
    return nc



DI = 4096
DIC = 32
DSTATE = 16


def prep_l1attn_host(W_in, conv_w, conv_b, W_x, W_dt, dt_bias, A_log, Dv, W_out):
    win = np.ascontiguousarray(np.asarray(W_in, np.float32)
                               .reshape(64, 128, NC, 128).transpose(0, 3, 2, 1))
    convw = np.ascontiguousarray(np.asarray(conv_w, np.float32)
                                 .reshape(DIC, 128, 4).transpose(1, 0, 2))
    convb = np.ascontiguousarray(np.asarray(conv_b, np.float32).reshape(DIC, 128).T)
    wx = np.asarray(W_x, np.float32)
    wxdt = np.ascontiguousarray(wx[:128].reshape(128, DIC, 128).transpose(2, 1, 0))
    wxB = np.ascontiguousarray(wx[128:144].reshape(16, DIC, 128).transpose(2, 1, 0))
    wxC = np.ascontiguousarray(wx[144:160].reshape(16, DIC, 128).transpose(2, 1, 0))
    wdt = np.ascontiguousarray(np.asarray(W_dt, np.float32).T.reshape(128, DIC, 128))
    dtb = np.ascontiguousarray(np.asarray(dt_bias, np.float32).reshape(DIC, 128).T)
    alog = np.ascontiguousarray(np.asarray(A_log, np.float32)
                                .reshape(DIC, 128, 16).transpose(1, 0, 2))
    dvec = np.ascontiguousarray(np.asarray(Dv, np.float32).reshape(DIC, 128).T)
    wout = np.ascontiguousarray(np.asarray(W_out, np.float32)
                                .reshape(NC, 128, DIC, 128).transpose(0, 3, 2, 1))
    e16 = np.zeros((16, 16, 128), np.float32)
    for n in range(16):
        e16[n, n, :] = 1.0
    e16 = np.ascontiguousarray(e16.transpose(1, 0, 2).reshape(16, 16 * 128))
    return dict(win=win, convw=convw, convb=convb, wxdt=wxdt, wxB=wxB, wxC=wxC,
                wdt=wdt, dtb=dtb, alog=alog, dvec=dvec, wout=wout,
                h128=_had128(), e16=e16)


def build_l1attn():
    nc = bass.Bass("TRN2", target_bir_lowering=False, debug=False)
    r2i = nc.dram_tensor("r2i", [128, NC, T], F32, kind="ExternalInput").ap()
    win_d = nc.dram_tensor("win", [64, 128, NC, 128], F32, kind="ExternalInput").ap()
    convw_d = nc.dram_tensor("convw", [128, DIC, 4], F32, kind="ExternalInput").ap()
    convb_d = nc.dram_tensor("convb", [128, DIC], F32, kind="ExternalInput").ap()
    wxdt_d = nc.dram_tensor("wxdt", [128, DIC, 128], F32, kind="ExternalInput").ap()
    wxB_d = nc.dram_tensor("wxB", [128, DIC, 16], F32, kind="ExternalInput").ap()
    wxC_d = nc.dram_tensor("wxC", [128, DIC, 16], F32, kind="ExternalInput").ap()
    wdt_d = nc.dram_tensor("wdt", [128, DIC, 128], F32, kind="ExternalInput").ap()
    dtb_d = nc.dram_tensor("dtb", [128, DIC], F32, kind="ExternalInput").ap()
    alog_d = nc.dram_tensor("alog", [128, DIC, 16], F32, kind="ExternalInput").ap()
    dvec_d = nc.dram_tensor("dvec", [128, DIC], F32, kind="ExternalInput").ap()
    wout_d = nc.dram_tensor("wout", [NC, 128, DIC, 128], F32, kind="ExternalInput").ap()
    h128_d = nc.dram_tensor("h128", [128, 128], F32, kind="ExternalInput").ap()
    e16_d = nc.dram_tensor("e16", [16, 16 * 128], F32, kind="ExternalInput").ap()
    r3_d = nc.dram_tensor("r3", [128, NC, T], F32, kind="ExternalOutput").ap()

    P = Plan()
    P.sem("st", "sv", "ss", "sg", "dcst", "dwx", "dwi0", "dwi1",
          "dwo0", "dwo1", "ds0", "ds1", "drc0", "drc1")

    with contextlib.ExitStack() as ctx:
        E = ctx.enter_context
        xln = E(nc.sbuf_tensor("xln", [128, NC, TB], BF16))
        xconv = E(nc.sbuf_tensor("xconv", [128, DIC, 3 + TB], BF16))
        zsil = E(nc.sbuf_tensor("zsil", [128, DIC, TB], BF16))
        ubuf = E(nc.sbuf_tensor("ubuf", [128, DIC, TB], BF16))
        crep = E(nc.sbuf_tensor("crep", [128, 16, TB], BF16))
        scn = E(nc.sbuf_tensor("scn", [128, 16, TB], BF16))
        brep = xln
        winb = [E(nc.sbuf_tensor(f"winb{i}", [128, NC, 128], BF16)) for i in range(2)]
        woutb = E(nc.sbuf_tensor("woutb", [128, DIC, 128], BF16))
        wxw = E(nc.sbuf_tensor("wxw_s", [128, DIC, 128], BF16))
        wxB = E(nc.sbuf_tensor("wxB_s", [128, DIC, 16], BF16))
        wxC = E(nc.sbuf_tensor("wxC_s", [128, DIC, 16], BF16))
        convw = E(nc.sbuf_tensor("convw_s", [128, DIC, 4], F32))
        convb = E(nc.sbuf_tensor("convb_s", [128, DIC], F32))
        dtb = E(nc.sbuf_tensor("dtb_s", [128, DIC], F32))
        negA = E(nc.sbuf_tensor("negA", [128, DIC, 16], F32))
        dvec = E(nc.sbuf_tensor("dvec_s", [128, DIC], F32))
        carry = E(nc.sbuf_tensor("carry", [128, DIC, 16], F32))
        e16 = E(nc.sbuf_tensor("e16_s", [16, 16 * 128], BF16))
        h128p = E(nc.sbuf_tensor("h128p", [128, 128], BF16))
        bc_s = E(nc.sbuf_tensor("bc_s", [16, TB], BF16))
        dtr = E(nc.sbuf_tensor("dtr", [128, TB], BF16))
        delta = E(nc.sbuf_tensor("delta", [128, TB], F32))
        dub = E(nc.sbuf_tensor("dub", [128, TB], BF16))
        delb = E(nc.sbuf_tensor("delb", [128, TB], BF16))
        r2c = [E(nc.sbuf_tensor(f"r2c{i}", [128, TB], F32)) for i in range(2)]
        rowm = E(nc.sbuf_tensor("rowm", [1, TB], F32))
        rowe = E(nc.sbuf_tensor("rowe", [1, TB], F32))
        nmurep = E(nc.sbuf_tensor("nmurep", [128, TB], F32))
        rstdrep = E(nc.sbuf_tensor("rstdrep", [128, TB], F32))
        dAb = [nmurep, rstdrep]
        tmpf = E(nc.sbuf_tensor("tmpf", [128, TB], F32))
        tmpf2 = E(nc.sbuf_tensor("tmpf2", [128, TB], F32))
        btmp = E(nc.sbuf_tensor("btmp", [128, TB], BF16))
        sqs = btmp
        r3t = [E(nc.sbuf_tensor(f"r3t{i}", [128, TB], F32)) for i in range(2)]
        onec_b = E(nc.sbuf_tensor("onec_b", [128, 1], BF16))
        onec_f = E(nc.sbuf_tensor("onec_f", [128, 1], F32))
        oner_f = E(nc.sbuf_tensor("oner_f", [1, 128], F32))
        cstf = E(nc.sbuf_tensor("cstf", [128, 8], F32))
        pA = E(nc.psum_tensor("pA", [128, TB], F32))
        pB = E(nc.psum_tensor("pB", [128, TB], F32))
        PS = [pA, pB]
        sems = {n: E(nc.semaphore(n)) for n in P.names}

        def dma(q, out_ap, in_ap, sem, waits=()):
            P.add(q, lambda e, S, o=out_ap, i=in_ap: e.dma_start(out=o, in_=i),
                  waits=waits, incs=[(sem, 16)])
            return (sem, P.now(sem))

        for t_s, t_d in ((wxB, wxB_d), (wxC, wxC_d),
                         (convw, convw_d), (convb, convb_d), (dtb, dtb_d),
                         (negA, alog_d), (dvec, dvec_d), (e16, e16_d),
                         (h128p, h128_d)):
            cst_ok = dma("g", t_s[:], t_d[:], "dcst")
        P.add("g", lambda e, S: e.memset(onec_b[:], 1.0))
        P.add("g", lambda e, S: e.memset(onec_f[:], 1.0))
        P.add("g", lambda e, S: e.memset(oner_f[:], 1.0))
        P.add("g", lambda e, S: e.memset(carry[:], 0.0))
        for dc in range(DIC):
            P.add("g", lambda e, S, dc=dc: e.memset(xconv[:, dc, 0:3], 0.0))
        P.add("g", lambda e, S: e.memset(cstf[:, 0:1], LN_EPS))
        P.add("g", lambda e, S: e.memset(cstf[:, 1:2], -0.5))
        P.add("g", lambda e, S: e.memset(cstf[:, 2:3], -1.0))
        P.add("g", lambda e, S: e.memset(cstf[:, 3:4], 1.0))
        P.add("g", lambda e, S: e.memset(cstf[:, 4:5], float(2.0 ** -6)),
              incs=[("sg", 1)])
        consts_ok = P.tok("sg")
        P.add("s", lambda e, S: e.activation(negA[:], negA[:], AF.Exp),
              waits=[cst_ok, consts_ok], incs=[("ss", 1)])
        P.add("v", lambda e, S: e.tensor_scalar_mul(negA[:], negA[:], -1.0),
              waits=[P.tok("ss")], incs=[("sv", 1)])
        P.barrier(("t", "v", "s"))

        bankfree = [P.tok("sv"), P.tok("sv")]
        win_last = [("st", 0), ("st", 0)]
        wxw_last = ("st", 0)
        wout_last = ("st", 0)
        r2c_last = [("sv", 0), ("sv", 0)]
        r3_last = [("ds0", 0), ("ds1", 0)]

        for ts in range(NTB):
            sl = slice(ts * TB, (ts + 1) * TB)

            if ts > 0:
                for dc in range(DIC):
                    P.add("v", lambda e, S, dc=dc: e.tensor_copy(
                        xconv[:, dc, 0:3], xconv[:, dc, TB:TB + 3]),
                        waits=[P.tok("st")] if dc == 0 else [])
                P.add("v", None, incs=[("sv", 1)])
            halo_ok = P.tok("sv")

            for dc in range(NC):
                par = dc % 2
                r2_ok = dma("y", r2c[par][:], r2i[:, dc, sl], f"drc{par}",
                            waits=[r2c_last[par]])
                P.add("v", lambda e, S, par=par: e.tensor_tensor(
                    sqs[:], r2c[par][:], r2c[par][:], ALU.mult),
                    waits=[r2_ok, P.tok("st")], incs=[("sv", 1)])
                P.add("t", lambda e, S, par=par, dc=dc: e.matmul(
                    pA[0:1, :], onec_f[:], r2c[par][:],
                    start=(dc == 0), stop=(dc == NC - 1)),
                    waits=[P.tok("sv")] + ([bankfree[0], bankfree[1]] if dc == 0 else []))
                P.add("t", lambda e, S, dc=dc: e.matmul(
                    pB[0:1, :], onec_b[:], sqs[:],
                    start=(dc == 0), stop=(dc == NC - 1)), incs=[("st", 1)])
                r2c_last[par] = P.tok("st")
            P.add("s", lambda e, S: e.mul(rowm[0:1, :], pA[0:1, :], 1.0 / D),
                  waits=[P.tok("st")])
            P.add("s", lambda e, S: e.mul(rowe[0:1, :], pB[0:1, :], 1.0 / D),
                  incs=[("ss", 1)])
            P.add("v", lambda e, S: e.tensor_tensor(
                tmpf[0:1, 0:TB], rowm[0:1, :], rowm[0:1, :], ALU.mult),
                waits=[P.tok("ss")])
            P.add("v", lambda e, S: e.tensor_tensor(
                rowe[0:1, :], rowe[0:1, :], tmpf[0:1, 0:TB], ALU.subtract))
            P.add("v", lambda e, S: e.tensor_scalar_mul(rowm[0:1, :], rowm[0:1, :], -1.0),
                  incs=[("sv", 1)])
            P.add("s", lambda e, S: e.activation(rowe[0:1, :], rowe[0:1, :], AF.Ln,
                                                 bias=cstf[0:1, 0:1]), waits=[P.tok("sv")])
            P.add("s", lambda e, S: e.activation(rowe[0:1, :], rowe[0:1, :], AF.Exp,
                                                 scale=cstf[0:1, 1:2]), incs=[("ss", 1)])
            P.add("t", lambda e, S: e.matmul(pA[:], oner_f[:], rowm[0:1, :],
                                             start=True, stop=True),
                  waits=[P.tok("ss")], incs=[("st", 1)])
            P.add("v", lambda e, S: e.tensor_copy(nmurep[:], pA[:]),
                  waits=[P.tok("st")], incs=[("sv", 1)])
            P.add("t", lambda e, S: e.matmul(pB[:], oner_f[:], rowe[0:1, :],
                                             start=True, stop=True), incs=[("st", 1)])
            P.add("v", lambda e, S: e.tensor_copy(rstdrep[:], pB[:]),
                  waits=[P.tok("st")], incs=[("sv", 1)])
            rep_ok = P.tok("sv")
            bankfree = [rep_ok, rep_ok]

            for dc in range(NC):
                par = dc % 2
                r2_ok = dma("y", r2c[par][:], r2i[:, dc, sl], f"drc{par}",
                            waits=[r2c_last[par]])
                P.add("v", lambda e, S, dc=dc, par=par: e.tensor_tensor(
                    tmpf[:], r2c[par][:], nmurep[:], ALU.add),
                    waits=[r2_ok] + ([rep_ok] if dc == 0 else []))
                P.add("v", lambda e, S, dc=dc: e.tensor_tensor(
                    xln[:, dc, :], tmpf[:], rstdrep[:], ALU.mult), incs=[("sv", 1)])
                r2c_last[par] = P.tok("sv")
            xln_ok = P.tok("sv")

            for oc in range(64):
                par = oc % 2
                wi_ok = dma("g", winb[par][:], win_d[oc], f"dwi{par}",
                            waits=[win_last[par]])
                pX = PS[par]
                P.add("t", None, waits=[xln_ok, wi_ok, bankfree[par]] +
                      ([halo_ok] if oc < DIC else []))
                for dc in range(NC):
                    P.add("t", lambda e, S, dc=dc, par=par, pX=pX: e.matmul(
                        pX[:], winb[par][:, dc, :], xln[:, dc, :],
                        start=(dc == 0), stop=(dc == NC - 1)))
                P.add("t", None, incs=[("st", 1)])
                mm_ok = P.tok("st")
                win_last[par] = mm_ok
                if oc < DIC:
                    P.add("v", lambda e, S, oc=oc, pX=pX: e.tensor_copy(
                        xconv[:, oc, 3:3 + TB], pX[:]),
                        waits=[mm_ok], incs=[("sv", 1)])
                    bankfree[par] = P.tok("sv")
                else:
                    P.add("s", lambda e, S, oc=oc, pX=pX: e.activation(
                        zsil[:, oc - DIC, :], pX[:], AF.Silu),
                        waits=[mm_ok], incs=[("ss", 1)])
                    bankfree[par] = P.tok("ss")
            inproj_ok = [P.tok("sv"), P.tok("ss")]

            for dc in range(DIC):
                P.add("v", lambda e, S, dc=dc: e.tensor_scalar(
                    tmpf[:], xconv[:, dc, 0:TB], convw[:, dc, 0:1], None, ALU.mult),
                    waits=(inproj_ok if dc == 0 else []))
                for k2 in range(1, 4):
                    P.add("v", lambda e, S, dc=dc, k2=k2: e.scalar_tensor_tensor(
                        tmpf[:], xconv[:, dc, k2:k2 + TB], convw[:, dc, k2:k2 + 1],
                        tmpf[:], ALU.mult, ALU.add))
                P.add("v", None, incs=[("sv", 1)])
                P.add("s", lambda e, S, dc=dc: e.activation(
                    ubuf[:, dc, :], tmpf[:], AF.Silu, bias=convb[:, dc:dc + 1]),
                    waits=[P.tok("sv")], incs=[("ss", 1)])
                P.add("v", None, waits=[P.tok("ss")])
            u_ok = P.tok("ss")

            wx_ok = dma("g", wxw[:], wxdt_d[:], "dwx", waits=[wxw_last])
            P.add("t", None, waits=[u_ok, bankfree[0], wx_ok])
            for dc in range(DIC):
                P.add("t", lambda e, S, dc=dc: e.matmul(
                    pA[:], wxw[:, dc, :], ubuf[:, dc, :],
                    start=(dc == 0), stop=(dc == DIC - 1)))
            P.add("t", None, incs=[("st", 1)])
            wxw_last = P.tok("st")
            P.add("v", lambda e, S: e.tensor_copy(dtr[:], pA[:]),
                  waits=[P.tok("st")], incs=[("sv", 1)])
            bankfree[0] = P.tok("sv")
            for nm_w, rep in ((wxB, brep), (wxC, crep)):
                P.add("t", None, waits=[bankfree[1]])
                for dc in range(DIC):
                    P.add("t", lambda e, S, dc=dc, nm_w=nm_w: e.matmul(
                        pB[0:16, :], nm_w[:, dc, :], ubuf[:, dc, :],
                        start=(dc == 0), stop=(dc == DIC - 1)))
                P.add("t", None, incs=[("st", 1)])
                P.add("s", lambda e, S: e.copy(bc_s[:], pB[0:16, :]),
                      waits=[P.tok("st")], incs=[("ss", 1)])
                bankfree[1] = P.tok("ss")
                for n in range(16):
                    bk = n % 2
                    pX = PS[bk]
                    P.add("t", lambda e, S, n=n, pX=pX: e.matmul(
                        pX[:], e16[:, n * 128:(n + 1) * 128], bc_s[:],
                        start=True, stop=True),
                        waits=[P.tok("ss"), bankfree[bk]], incs=[("st", 1)])
                    P.add("v", lambda e, S, n=n, pX=pX, rep=rep: e.tensor_copy(
                        rep[:, n, :], pX[:]),
                        waits=[P.tok("st")], incs=[("sv", 1)])
                    bankfree[bk] = P.tok("sv")
            bc_ok = P.tok("sv")

            wd_ok = dma("g", wxw[:], wdt_d[:], "dwx", waits=[P.tok("st")])
            for dc in range(DIC):
                P.add("t", lambda e, S, dc=dc: e.matmul(
                    pA[:], wxw[:, dc, :], dtr[:], start=True, stop=True),
                    waits=([bankfree[0], bc_ok, wd_ok] if dc == 0 else [bankfree[0]]),
                    incs=[("st", 1)])
                dmm = P.tok("st")
                P.add("v", lambda e, S, dc=dc: e.tensor_scalar(
                    tmpf[:], pA[:], dtb[:, dc:dc + 1], None, ALU.add),
                    waits=[dmm], incs=[("sv", 1)])
                bankfree[0] = P.tok("sv")
                P.add("s", lambda e, S: e.activation(tmpf2[:], tmpf[:], AF.Abs),
                      waits=[P.tok("sv")])
                P.add("s", lambda e, S: e.activation(tmpf2[:], tmpf2[:], AF.Exp,
                                                     scale=cstf[:, 2:3]))
                P.add("s", lambda e, S: e.activation(tmpf2[:], tmpf2[:], AF.Ln,
                                                     bias=cstf[:, 3:4]), incs=[("ss", 1)])
                P.add("v", lambda e, S: e.tensor_scalar_max(delta[:], tmpf[:], 0.0),
                      waits=[P.tok("ss")])
                P.add("v", lambda e, S: e.tensor_tensor(
                    delta[:], delta[:], tmpf2[:], ALU.add))
                P.add("v", lambda e, S: e.tensor_copy(delb[:], delta[:]))
                P.add("v", lambda e, S, dc=dc: e.tensor_tensor(
                    dub[:], delb[:], ubuf[:, dc, :], ALU.mult))
                P.add("v", lambda e, S: e.tensor_tensor(
                    scn[:], dub[:].unsqueeze(1).broadcast_to([128, 16, TB]),
                    brep[:], ALU.mult), incs=[("sv", 1)])
                dbu_ok = P.tok("sv")
                for n in range(16):
                    da = dAb[n % 2]
                    P.add("s", lambda e, S, dc=dc, n=n, da=da: e.activation(
                        da[:], delta[:], AF.Exp,
                        scale=negA[:, dc, n:n + 1]),
                        waits=[P.tok("sv")], incs=[("ss", 1)])
                    P.add("v", lambda e, S, dc=dc, n=n, da=da: e.tensor_tensor_scan(
                        scn[:, n, :], da[:], scn[:, n, :],
                        carry[:, dc, n:n + 1], ALU.mult, ALU.add),
                        waits=[P.tok("ss")], incs=[("sv", 1)])
                scan_ok = P.tok("sv")
                P.add("v", lambda e, S, dc=dc: e.tensor_copy(
                    carry[:, dc, :], scn[:, :, TB - 1]))
                P.add("v", lambda e, S: e.tensor_tensor(scn[:], scn[:], crep[:], ALU.mult))
                P.add("v", lambda e, S: e.tensor_tensor(
                    scn[:, 0:8, :], scn[:, 0:8, :], scn[:, 8:16, :], ALU.add))
                P.add("v", lambda e, S: e.tensor_tensor(
                    scn[:, 0:4, :], scn[:, 0:4, :], scn[:, 4:8, :], ALU.add))
                P.add("v", lambda e, S: e.tensor_tensor(
                    scn[:, 0:2, :], scn[:, 0:2, :], scn[:, 2:4, :], ALU.add))
                P.add("v", lambda e, S: e.tensor_tensor(
                    tmpf[:], scn[:, 0, :], scn[:, 1, :], ALU.add))
                P.add("v", lambda e, S, dc=dc: e.scalar_tensor_tensor(
                    tmpf[:], ubuf[:, dc, :], dvec[:, dc:dc + 1], tmpf[:],
                    ALU.mult, ALU.add))
                P.add("v", lambda e, S, dc=dc: e.scalar_tensor_tensor(
                    xconv[:, dc, 0:TB], tmpf[:], cstf[:, 4:5], zsil[:, dc, :],
                    ALU.mult, ALU.mult), incs=[("sv", 1)])
            y_ok = P.tok("sv")

            for stg in range(5):
                stride = 16 >> stg
                for base in range(0, DIC, 2 * stride):
                    for i in range(base, base + stride):
                        j = i + stride
                        P.add("v", lambda e, S, i=i, j=j: e.tensor_tensor(
                            btmp[:], xconv[:, i, 0:TB], xconv[:, j, 0:TB], ALU.subtract),
                            waits=[y_ok] if (stg == 0 and i == 0) else [])
                        P.add("v", lambda e, S, i=i, j=j: e.tensor_tensor(
                            xconv[:, i, 0:TB], xconv[:, i, 0:TB], xconv[:, j, 0:TB], ALU.add))
                        P.add("v", lambda e, S, j=j: e.tensor_copy(
                            xconv[:, j, 0:TB], btmp[:]))
            P.add("v", None, incs=[("sv", 1)])
            bfl_ok = P.tok("sv")
            for ec in range(DIC):
                bk = ec % 2
                pX = PS[bk]
                P.add("t", lambda e, S, ec=ec, pX=pX: e.matmul(
                    pX[:], h128p[:], xconv[:, ec, 0:TB], start=True, stop=True),
                    waits=[bfl_ok, bankfree[bk]], incs=[("st", 1)])
                P.add("v", lambda e, S, ec=ec, pX=pX: e.tensor_copy(
                    xconv[:, ec, 0:TB], pX[:]),
                    waits=[P.tok("st")], incs=[("sv", 1)])
                bankfree[bk] = P.tok("sv")
            t2_ok = P.tok("sv")

            for dco in range(NC):
                par = dco % 2
                wo_ok = dma("g", woutb[:], wout_d[dco], "dwo0", waits=[wout_last])
                r2_ok = dma("y", r2c[par][:], r2i[:, dco, sl], f"drc{par}",
                            waits=[r2c_last[par]])
                pX = PS[par]
                P.add("t", None, waits=[t2_ok, wo_ok, bankfree[par]])
                for ec in range(DIC):
                    P.add("t", lambda e, S, ec=ec, pX=pX: e.matmul(
                        pX[:], woutb[:, ec, :], xconv[:, ec, 0:TB],
                        start=(ec == 0), stop=(ec == DIC - 1)))
                P.add("t", None, incs=[("st", 1)])
                wout_last = P.tok("st")
                rb3 = r3t[par]
                P.add("v", lambda e, S, pX=pX, par=par, rb3=rb3: e.tensor_tensor(
                    rb3[:], pX[:], r2c[par][:], ALU.add),
                    waits=[P.tok("st"), r2_ok, r3_last[par]], incs=[("sv", 1)])
                bankfree[par] = P.tok("sv")
                r2c_last[par] = P.tok("sv")
                r3_last[par] = dma("y", r3_d[:, dco, sl], rb3[:], f"ds{par}",
                                   waits=[P.tok("sv")])

        P.add("y", None, waits=[P.tok("ds0"), P.tok("ds1")])
        P.emit(nc, sems)
    return nc



GOC = 43


def prep_l1ffn_host(lut, gate_idx, up_idx, down_idx):
    gidx = np.ascontiguousarray(np.asarray(gate_idx, np.int32)
                                .reshape(GOC, 128, NC, 128).transpose(0, 3, 2, 1))
    uidx = np.ascontiguousarray(np.asarray(up_idx, np.int32)
                                .reshape(GOC, 128, NC, 128).transpose(0, 3, 2, 1))
    didx = np.ascontiguousarray(np.asarray(down_idx, np.int32).T
                                .reshape(GOC, 128, NC, 128))
    return dict(gidx=gidx, uidx=uidx, didx=didx,
                lut=np.ascontiguousarray(np.asarray(lut, np.float32).reshape(1, 16)))


def build_l1ffn():
    nc = bass.Bass("TRN2", target_bir_lowering=False, debug=False)
    r3i = nc.dram_tensor("r3i", [128, NC, T], F32, kind="ExternalInput").ap()
    gidx_d = nc.dram_tensor("gidx", [GOC, 128, NC, 128], I32, kind="ExternalInput").ap()
    uidx_d = nc.dram_tensor("uidx", [GOC, 128, NC, 128], I32, kind="ExternalInput").ap()
    didx_d = nc.dram_tensor("didx", [GOC, 128, NC, 128], I32, kind="ExternalInput").ap()
    lut_d = nc.dram_tensor("lut", [1, 16], F32, kind="ExternalInput").ap()
    y_d = nc.dram_tensor("yout", [128, NC, T], F32, kind="ExternalOutput").ap()
    wgu_scr = nc.dram_tensor("wgu_scr", [2, GOC, 128, NC, 128], BF16, kind="Internal").ap()
    wdn_scr = nc.dram_tensor("wdn_scr", [NC, 128, GOC, 128], BF16, kind="Internal").ap()

    P = Plan()
    P.sem("st", "sv", "ss", "sg", "dcst", "didx0", "didx1", "dwb0", "dwb1",
          "dwg0", "dwg1", "dwu0", "dwu1", "dwd0", "dwd1", "drc0", "drc1",
          "ds0", "ds1")

    with contextlib.ExitStack() as ctx:
        E = ctx.enter_context
        xln = E(nc.sbuf_tensor("xln", [128, NC, TB], BF16))
        hbuf = E(nc.sbuf_tensor("hbuf", [128, GOC, TB], BF16))
        idxb = [E(nc.sbuf_tensor(f"idxb{i}", [128, NC, 128], I32)) for i in range(2)]
        idxf = E(nc.sbuf_tensor("idxf", [128, 2048], BF16))
        contrib = E(nc.sbuf_tensor("contrib", [128, 2048], BF16))
        wacc = [E(nc.sbuf_tensor(f"wacc{i}", [128, 2048], BF16)) for i in range(2)]
        wgt = [E(nc.sbuf_tensor(f"wgt{i}", [128, NC, 128], BF16)) for i in range(2)]
        wut = [E(nc.sbuf_tensor(f"wut{i}", [128, NC, 128], BF16)) for i in range(2)]
        wdt = [E(nc.sbuf_tensor(f"wdt{i}", [128, GOC, 128], BF16)) for i in range(2)]
        lut_s = E(nc.sbuf_tensor("lut_s", [1, 16], F32))
        lutrep = E(nc.sbuf_tensor("lutrep", [128, 16], F32))
        r3c = [E(nc.sbuf_tensor(f"r3c{i}", [128, TB], F32)) for i in range(2)]
        rowm = E(nc.sbuf_tensor("rowm", [1, TB], F32))
        rowe = E(nc.sbuf_tensor("rowe", [1, TB], F32))
        nmurep = E(nc.sbuf_tensor("nmurep", [128, TB], F32))
        rstdrep = E(nc.sbuf_tensor("rstdrep", [128, TB], F32))
        tmpf = E(nc.sbuf_tensor("tmpf", [128, TB], F32))
        silt = E(nc.sbuf_tensor("silt", [128, TB], F32))
        sqs = E(nc.sbuf_tensor("sqs", [128, TB], BF16))
        yt = [E(nc.sbuf_tensor(f"yt{i}", [128, TB], F32)) for i in range(2)]
        onec_b = E(nc.sbuf_tensor("onec_b", [128, 1], BF16))
        onec_f = E(nc.sbuf_tensor("onec_f", [128, 1], F32))
        oner_f = E(nc.sbuf_tensor("oner_f", [1, 128], F32))
        cstf = E(nc.sbuf_tensor("cstf", [128, 8], F32))
        pA = E(nc.psum_tensor("pA", [128, TB], F32))
        pB = E(nc.psum_tensor("pB", [128, TB], F32))
        PS = [pA, pB]
        sems = {n: E(nc.semaphore(n)) for n in P.names}

        def dma(q, out_ap, in_ap, sem, waits=()):
            P.add(q, lambda e, S, o=out_ap, i=in_ap: e.dma_start(out=o, in_=i),
                  waits=waits, incs=[(sem, 16)])
            return (sem, P.now(sem))

        lut_ok = dma("g", lut_s[:], lut_d[:], "dcst")
        P.add("g", lambda e, S: e.memset(onec_b[:], 1.0))
        P.add("g", lambda e, S: e.memset(onec_f[:], 1.0))
        P.add("g", lambda e, S: e.memset(oner_f[:], 1.0))
        P.add("g", lambda e, S: e.memset(cstf[:, 0:1], LN_EPS))
        P.add("g", lambda e, S: e.memset(cstf[:, 1:2], -0.5), incs=[("sg", 1)])
        consts_ok = P.tok("sg")
        P.add("t", lambda e, S: e.matmul(pA[:, 0:16], oner_f[:], lut_s[:],
                                         start=True, stop=True),
              waits=[lut_ok, consts_ok], incs=[("st", 1)])
        P.add("v", lambda e, S: e.tensor_copy(lutrep[:], pA[:, 0:16]),
              waits=[P.tok("st")], incs=[("sv", 1)])
        lutrep_ok = P.tok("sv")

        idx_last = [("st", 0), ("st", 0)]
        wb_store = [("dwb0", 0), ("dwb1", 0)]
        builds = []
        for kind in range(3):
            src_d = (gidx_d, uidx_d, didx_d)[kind]
            for ic in range(GOC):
                builds.append((kind, ic, src_d))
        for bi, (kind, ic, src_d) in enumerate(builds):
            par = bi % 2
            i_ok = dma("g", idxb[par][:], src_d[ic], f"didx{par}",
                       waits=[idx_last[par]])
            acc = wacc[par]
            P.add("v", lambda e, S, par=par: e.tensor_copy(
                idxf[:], idxb[par][:]),
                waits=[i_ok, lutrep_ok, wb_store[par]])
            for v in range(16):
                if v == 0:
                    P.add("v", lambda e, S, v=v, acc=acc: e.tensor_scalar(
                        acc[:], idxf[:], float(v), lutrep[:, v:v + 1],
                        ALU.is_equal, ALU.mult))
                else:
                    P.add("v", lambda e, S, v=v: e.tensor_scalar(
                        contrib[:], idxf[:], float(v), lutrep[:, v:v + 1],
                        ALU.is_equal, ALU.mult))
                    P.add("v", lambda e, S, acc=acc: e.tensor_tensor(
                        acc[:], acc[:], contrib[:], ALU.add))
            P.add("v", None, incs=[("sv", 1)])
            bdone = P.tok("sv")
            idx_last[par] = bdone
            if kind < 2:
                out_ap = wgu_scr[kind, ic]
            else:
                out_ap = wdn_scr[:, :, ic, :].transpose([1, 0, 2])
            wb_store[par] = dma("y", out_ap, acc[:], f"dwb{par}", waits=[bdone])
        build_done = [P.tok("dwb0"), P.tok("dwb1")]

        bankfree = [P.tok("sv"), P.tok("sv")]
        r3c_last = [("sv", 0), ("sv", 0)]
        wg_last = [("st", 0), ("st", 0)]
        wu_last = [("st", 0), ("st", 0)]
        wd_last = [("st", 0), ("st", 0)]
        y_last = [("ds0", 0), ("ds1", 0)]
        for ts in range(NTB):
            sl = slice(ts * TB, (ts + 1) * TB)
            for dc in range(NC):
                par = dc % 2
                r3_ok = dma("y", r3c[par][:], r3i[:, dc, sl], f"drc{par}",
                            waits=[r3c_last[par]])
                P.add("v", lambda e, S, par=par: e.tensor_tensor(
                    sqs[:], r3c[par][:], r3c[par][:], ALU.mult),
                    waits=[r3_ok, P.tok("st")], incs=[("sv", 1)])
                P.add("t", lambda e, S, par=par, dc=dc: e.matmul(
                    pA[0:1, :], onec_f[:], r3c[par][:],
                    start=(dc == 0), stop=(dc == NC - 1)),
                    waits=[P.tok("sv")] + ([bankfree[0], bankfree[1]] if dc == 0 else []))
                P.add("t", lambda e, S, dc=dc: e.matmul(
                    pB[0:1, :], onec_b[:], sqs[:],
                    start=(dc == 0), stop=(dc == NC - 1)), incs=[("st", 1)])
                r3c_last[par] = P.tok("st")
            P.add("s", lambda e, S: e.mul(rowm[0:1, :], pA[0:1, :], 1.0 / D),
                  waits=[P.tok("st")])
            P.add("s", lambda e, S: e.mul(rowe[0:1, :], pB[0:1, :], 1.0 / D),
                  incs=[("ss", 1)])
            P.add("v", lambda e, S: e.tensor_tensor(
                tmpf[0:1, 0:TB], rowm[0:1, :], rowm[0:1, :], ALU.mult),
                waits=[P.tok("ss")])
            P.add("v", lambda e, S: e.tensor_tensor(
                rowe[0:1, :], rowe[0:1, :], tmpf[0:1, 0:TB], ALU.subtract))
            P.add("v", lambda e, S: e.tensor_scalar_mul(rowm[0:1, :], rowm[0:1, :], -1.0),
                  incs=[("sv", 1)])
            P.add("s", lambda e, S: e.activation(rowe[0:1, :], rowe[0:1, :], AF.Ln,
                                                 bias=cstf[0:1, 0:1]), waits=[P.tok("sv")])
            P.add("s", lambda e, S: e.activation(rowe[0:1, :], rowe[0:1, :], AF.Exp,
                                                 scale=cstf[0:1, 1:2]), incs=[("ss", 1)])
            P.add("t", lambda e, S: e.matmul(pA[:], oner_f[:], rowm[0:1, :],
                                             start=True, stop=True),
                  waits=[P.tok("ss")], incs=[("st", 1)])
            P.add("v", lambda e, S: e.tensor_copy(nmurep[:], pA[:]),
                  waits=[P.tok("st")], incs=[("sv", 1)])
            P.add("t", lambda e, S: e.matmul(pB[:], oner_f[:], rowe[0:1, :],
                                             start=True, stop=True), incs=[("st", 1)])
            P.add("v", lambda e, S: e.tensor_copy(rstdrep[:], pB[:]),
                  waits=[P.tok("st")], incs=[("sv", 1)])
            rep_ok = P.tok("sv")
            bankfree = [rep_ok, rep_ok]
            for dc in range(NC):
                par = dc % 2
                r3_ok = dma("y", r3c[par][:], r3i[:, dc, sl], f"drc{par}",
                            waits=[r3c_last[par]])
                P.add("v", lambda e, S, dc=dc, par=par: e.tensor_tensor(
                    tmpf[:], r3c[par][:], nmurep[:], ALU.add),
                    waits=[r3_ok] + ([rep_ok] if dc == 0 else []))
                P.add("v", lambda e, S, dc=dc: e.tensor_tensor(
                    xln[:, dc, :], tmpf[:], rstdrep[:], ALU.mult), incs=[("sv", 1)])
                r3c_last[par] = P.tok("sv")
            xln_ok = P.tok("sv")

            for oc in range(GOC):
                par = oc % 2
                wg_ok = dma("y", wgt[par][:], wgu_scr[0, oc], f"dwg{par}",
                            waits=[wg_last[par], build_done[0], build_done[1]])
                wu_ok = dma("y", wut[par][:], wgu_scr[1, oc], f"dwu{par}",
                            waits=[wu_last[par]])
                P.add("t", None, waits=[xln_ok, wg_ok, wu_ok, bankfree[0], bankfree[1]])
                for hc in range(NC):
                    P.add("t", lambda e, S, hc=hc, par=par: e.matmul(
                        pA[:], wgt[par][:, hc, :], xln[:, hc, :],
                        start=(hc == 0), stop=(hc == NC - 1)))
                for hc in range(NC):
                    P.add("t", lambda e, S, hc=hc, par=par: e.matmul(
                        pB[:], wut[par][:, hc, :], xln[:, hc, :],
                        start=(hc == 0), stop=(hc == NC - 1)))
                P.add("t", None, incs=[("st", 1)])
                mm_ok = P.tok("st")
                wg_last[par] = mm_ok
                wu_last[par] = mm_ok
                P.add("s", lambda e, S: e.activation(silt[:], pA[:], AF.Silu),
                      waits=[mm_ok], incs=[("ss", 1)])
                bankfree[0] = P.tok("ss")
                P.add("v", lambda e, S, oc=oc: e.tensor_tensor(
                    hbuf[:, oc, :], pB[:], silt[:], ALU.mult),
                    waits=[P.tok("ss")], incs=[("sv", 1)])
                bankfree[1] = P.tok("sv")
            h_ok = P.tok("sv")

            for dco in range(NC):
                par = dco % 2
                wd_ok = dma("y", wdt[par][:], wdn_scr[dco], f"dwd{par}",
                            waits=[wd_last[par]])
                r3_ok = dma("y", r3c[par][:], r3i[:, dco, sl], f"drc{par}",
                            waits=[r3c_last[par]])
                pX = PS[par]
                P.add("t", None, waits=[h_ok, wd_ok, bankfree[par]])
                for ic in range(GOC):
                    P.add("t", lambda e, S, ic=ic, par=par, pX=pX: e.matmul(
                        pX[:], wdt[par][:, ic, :], hbuf[:, ic, :],
                        start=(ic == 0), stop=(ic == GOC - 1)))
                P.add("t", None, incs=[("st", 1)])
                wd_last[par] = P.tok("st")
                yb = yt[par]
                P.add("v", lambda e, S, pX=pX, par=par, yb=yb: e.tensor_tensor(
                    yb[:], pX[:], r3c[par][:], ALU.add),
                    waits=[P.tok("st"), r3_ok, y_last[par]], incs=[("sv", 1)])
                bankfree[par] = P.tok("sv")
                r3c_last[par] = P.tok("sv")
                y_last[par] = dma("y", y_d[:, dco, sl], yb[:], f"ds{par}",
                                  waits=[P.tok("sv")])

        P.add("y", None, waits=[P.tok("ds0"), P.tok("ds1")])
        P.emit(nc, sems)
    return nc



def kernel(**inputs):
    return kernel_fused(**inputs)


def kernel_unfused(**inputs):
    I = {k: np.asarray(v) for k, v in inputs.items()}
    x = I["L0_attn_x"]

    wshared = prep_l0attn_host(x[0], I["L0_attn_W_K"], I["L0_attn_W_Q"],
                               I["L0_attn_W_V"], I["L0_attn_W_O"])
    in_maps = []
    for b in range(NCORES):
        hp = dict(wshared)
        hp["xin"] = pack_fm(x[b])
        in_maps.append(hp)
    nc = build_l0attn()
    res = run_bass_kernel_spmd(nc, in_maps, core_ids=list(range(NCORES)))
    r1 = [res.results[b]["r1"] for b in range(NCORES)]
    xin_packed = [m["xin"] for m in in_maps]

    wshared = prep_l0ffn_host(I["L0_ffn_SU_ug"], I["L0_ffn_SU_dp"],
                              I["L0_ffn_Wscale_ug"], I["L0_ffn_Wscale_dp"],
                              I["L0_ffn_W_up"], I["L0_ffn_W_gate"], I["L0_ffn_W_down"])
    in_maps = []
    for b in range(NCORES):
        hp = dict(wshared)
        hp["r1i"] = np.asarray(r1[b])
        in_maps.append(hp)
    nc = build_l0ffn()
    res = run_bass_kernel_spmd(nc, in_maps, core_ids=list(range(NCORES)))
    r2 = [res.results[b]["r2"] for b in range(NCORES)]

    wshared = prep_l1attn_host(I["L1_attn_W_in"], I["L1_attn_conv_w"],
                               I["L1_attn_conv_b"], I["L1_attn_W_x"],
                               I["L1_attn_W_dt"], I["L1_attn_dt_bias"],
                               I["L1_attn_A_log"], I["L1_attn_D"], I["L1_attn_W_out"])
    in_maps = []
    for b in range(NCORES):
        hp = dict(wshared)
        hp["r2i"] = np.asarray(r2[b])
        in_maps.append(hp)
    nc = build_l1attn()
    res = run_bass_kernel_spmd(nc, in_maps, core_ids=list(range(NCORES)))
    r3 = [res.results[b]["r3"] for b in range(NCORES)]

    wshared = prep_l1ffn_host(I["L1_ffn_lut"], I["L1_ffn_gate_idx"],
                              I["L1_ffn_up_idx"], I["L1_ffn_down_idx"])
    in_maps = []
    for b in range(NCORES):
        hp = dict(wshared)
        hp["r3i"] = np.asarray(r3[b])
        in_maps.append(hp)
    nc = build_l1ffn()
    res = run_bass_kernel_spmd(nc, in_maps, core_ids=list(range(NCORES)))

    out = np.empty((NCORES, T, D), np.float32)
    for b in range(NCORES):
        out[b] = unpack_fm(res.results[b]["yout"])
    return out



CSLOT = {"eps": 0, "mhalf": 1, "mone": 2, "one": 3, "i64": 4, "i128": 5, "half": 6}


class Carver:

    def __init__(self, arena):
        self.arena = arena
        self.off = 0
        self.floor = 0

    def reset(self):
        self.off = self.floor

    def alloc(self, shape, dt):
        esz = {F32: 4, BF16: 2, I32: 4}[dt]
        n = 1
        for sdim in shape[1:]:
            n *= sdim
        nbytes = n * esz
        off = (self.off + 63) // 64 * 64
        self.off = off + nbytes
        assert self.off <= ARENA_BYTES, f"arena overflow {self.off}"
        ap = self.arena[:, off:off + nbytes].bitcast(dt)
        if shape[0] != 128:
            ap = ap[0:shape[0]]
        if len(shape) > 2:
            dims = " ".join(f"d{i}" for i in range(1, len(shape)))
            kw = {f"d{i}": shape[i] for i in range(1, len(shape))}
            ap = ap.rearrange(f"p ({dims}) -> p {dims}", **kw)
        return ap


class GCtx:
    pass


ARENA_BYTES = 211200


def emit_l0attn(g, dram, after=()):
    nc, P, C = g.nc, g.P, g.C
    s = g.s
    pA, pB = g.pA, g.pB
    PS = [g.pA, g.pB]
    onec_b, onec_f, oner_b, oner_f = g.onec_b, g.onec_f, g.oner_b, g.oner_f
    cstf, h128p, h128n = g.cstf, g.h128p, g.h128n
    consts_ok = g.consts_ok
    for _eng in ENGS:
        P.pending[_eng] = P.pending[_eng] + list(after)
    xin = dram["xin"]
    wq_d = dram["wq"]
    wk_d = dram["wk"]
    wv_d = dram["wv"]
    wo_d = dram["wo"]
    cm_d = dram["cm"]
    r1_d = dram["r1"]
    zscr_d = dram["zscr"]


    if True:
        rb = C.alloc([128, NC, T], BF16)
        q_s = C.alloc([128, T], BF16)
        k_s = C.alloc([128, T], BF16)
        vt4 = C.alloc([128, NC, 4, 128], BF16)
        attnT = C.alloc([128, NC, TB], BF16)
        wqb = [C.alloc([128, NC, DH], BF16) for i in range(2)]
        wkb = [C.alloc([128, NC, DH], BF16) for i in range(2)]
        wvb = [C.alloc([128, NC, 4, DH], BF16)] * 2
        wob = [C.alloc([128, NH, DH], BF16) for i in range(2)]
        zqb = [attnT]
        cm_s = C.alloc([128, 4, TB], BF16)
        rrep = C.alloc([128, T], BF16)
        nmu_bf = C.alloc([1, T], BF16)
        mhat_bf = C.alloc([1, T], BF16)
        rowa = C.alloc([1, T], F32)
        lrec_bf = rowa[0:1, 512:768].bitcast(BF16)
        rowb = C.alloc([1, T], F32)
        kmax_s = C.alloc([1, 8], F32)
        rstd_cols = C.alloc([128, NC], F32)
        wbar = C.alloc([1, 384], BF16)
        wbar4 = C.alloc([1, 512], BF16)
        scf = [C.alloc([128, TB], F32) for i in range(2)]
        rep_sb = C.alloc([128, TB], F32)
        mrep = C.alloc([128, TB], F32)
        zbs = [C.alloc([128, TB], BF16) for i in range(2)]
        r0t = [C.alloc([128, TB], F32)] * 2
        r1t = [C.alloc([128, TB], F32)] * 2

        def dma(q, out_ap, in_ap, waits=(), sem="dg"):
            P.add(q, lambda e, S, o=out_ap, i=in_ap: e.dma_start(out=o, in_=i),
                  waits=waits, incs=[(sem, 16)])
            return (sem, P.now(sem))

        for i in range(4):
            rb_ok = dma("g", rb[:, 4 * i:4 * (i + 1), :], xin[:, 4 * i:4 * (i + 1), :], sem=s("dr"))
        cm_ok = dma("g", cm_s[:], cm_d[:], sem=s("dcm"))
        wld = {0: [dma("g", wqb[0][:], wq_d[0], sem=s("dw0")),
                   dma("g", wkb[0][:], wk_d[0], sem=s("dw0")),
                   dma("g", wvb[0][:], wv_d[0:4].rearrange("h p c d -> p c h d"),
                       sem=s("dw0"))]}

        for tb in range(NTB):
            sl = slice(tb * TB, (tb + 1) * TB)
            for dc in range(NC):
                P.add("v", lambda e, S, dc=dc, sl=sl: e.tensor_tensor(
                    attnT[:, dc, :], rb[:, dc, sl], rb[:, dc, sl], ALU.mult),
                    waits=[rb_ok, P.tok("st")] if dc == 0 else [])
            P.add("v", None, incs=[("sv", 1)])
            sq_ok = P.tok("sv")
            for dc in range(NC):
                P.add("t", lambda e, S, dc=dc, sl=sl: e.matmul(
                    pA[0:1, :], onec_b[:], rb[:, dc, sl],
                    start=(dc == 0), stop=(dc == NC - 1)),
                    waits=[rb_ok, consts_ok, P.tok("ss")] if dc == 0 else [])
            for dc in range(NC):
                P.add("t", lambda e, S, dc=dc: e.matmul(
                    pB[0:1, :], onec_b[:], attnT[:, dc, :],
                    start=(dc == 0), stop=(dc == NC - 1)),
                    waits=[sq_ok] if dc == 0 else [])
            P.add("t", None, incs=[("st", 1)])
            mm_ok = P.tok("st")
            P.add("s", lambda e, S, sl=sl: e.mul(rowa[0:1, sl], pA[0:1, :], 1.0 / D),
                  waits=[mm_ok])
            P.add("s", lambda e, S, sl=sl: e.mul(rowb[0:1, sl], pB[0:1, :], 1.0 / D),
                  incs=[("ss", 1)])
        P.barrier(("t", "v", "s"))
        P.add("v", lambda e, S: e.tensor_scalar_mul(nmu_bf[0:1, :], rowa[0:1, :], -1.0))
        P.add("v", lambda e, S: e.tensor_tensor(
            rowa[0:1, :], rowa[0:1, :], rowa[0:1, :], ALU.mult))
        P.add("v", lambda e, S: e.tensor_tensor(
            rowb[0:1, :], rowb[0:1, :], rowa[0:1, :], ALU.subtract), incs=[("sv", 1)])
        P.add("s", lambda e, S: e.activation(rowa[0:1, :], rowb[0:1, :], AF.Ln,
                                             bias=cstf[0:1, 0:1]),
              waits=[P.tok("sv"), consts_ok])
        P.add("s", lambda e, S: e.activation(rowb[0:1, :], rowa[0:1, :], AF.Exp,
                                             scale=cstf[0:1, 1:2]),
              incs=[("ss", 1)])
        rstd_ok = P.tok("ss")
        for tb in range(NTB):
            sl = slice(tb * TB, (tb + 1) * TB)
            pX = PS[tb % 2]
            P.add("t", lambda e, S, sl=sl, pX=pX: e.matmul(
                pX[:], oner_f[:], rowb[0:1, sl], start=True, stop=True),
                waits=[rstd_ok], incs=[("st", 1)])
            P.add("v", lambda e, S, sl=sl, pX=pX: e.tensor_copy(rrep[:, sl], pX[:]),
                  waits=[P.tok("st")], incs=[("sv", 1)])
        P.add("t", None, waits=[P.tok("sv")])
        for tc in range(NC):
            P.add("t", lambda e, S, tc=tc: e.matmul(
                pA[:, tc:tc + 1], rowb[0:1, tc * 128:(tc + 1) * 128], oner_f[0:1, 0:1],
                start=True, stop=True))
        P.add("t", None, incs=[("st", 1)])
        P.add("v", lambda e, S: e.tensor_copy(rstd_cols[:], pA[:, 0:NC]),
              waits=[P.tok("st")], incs=[("sv", 1)])
        P.barrier(("t", "v", "s"))

        bankfree = [P.tok("sv"), P.tok("sv")]
        attnT_free = P.tok("st")
        zb_stored = [(s("dzs0"), 0), (s("dzs1"), 0)]

        w_consumed = {}
        vt4_used = ("st", 0)
        vt4_used_next = ("st", 0)
        for h in range(NH):
            buf = h % 2
            wq, wk, wv = wqb[buf], wkb[buf], wvb[0]
            if h > 0:
                wwait = [w_consumed[h - 2]] if h >= 2 else []
                wld[h] = [dma("g", wq[:], wq_d[h], waits=wwait, sem=s(f"dw{buf}")),
                          dma("g", wk[:], wk_d[h], sem=s(f"dw{buf}"))] + (
                    [dma("g", wvb[0][:],
                         wv_d[h:h + 4].rearrange("h p c d -> p c h d"),
                         waits=[vt4_used_next] if h >= 4 else [],
                         sem=s(f"dw{buf}"))] if h % 4 == 0 else [])
            w_ok = wld[h][-1]

            P.add("t", None, waits=[w_ok, bankfree[0]])
            for j, wt in enumerate((wq, wk)):
                for dc in range(NC):
                    P.add("t", lambda e, S, j=j, wt=wt, dc=dc: e.matmul(
                        pA[0:1, j * 128:(j + 1) * 128], onec_b[:], wt[:, dc, :],
                        start=(dc == 0), stop=(dc == NC - 1)))
            P.add("t", None, incs=[("st", 1)])
            P.add("s", lambda e, S: e.copy(wbar[0:1, 0:256], pA[0:1, 0:256]),
                  waits=[P.tok("st")], incs=[("ss", 1)])
            bankfree[0] = P.tok("ss")
            wbar_ok = P.tok("ss")

            for wb_i, wt, dst in ((0, wq, q_s), (1, wk, k_s)):
                for tb in range(NTB):
                    sl = slice(tb * TB, (tb + 1) * TB)
                    bk = tb % 2
                    pX = PS[bk]
                    P.add("t", None, waits=[bankfree[bk], wbar_ok])
                    for dc in range(NC):
                        P.add("t", lambda e, S, dc=dc, sl=sl, wt=wt, pX=pX: e.matmul(
                            pX[:], wt[:, dc, :], rb[:, dc, sl],
                            start=(dc == 0), stop=False))
                    P.add("t", lambda e, S, sl=sl, pX=pX, wb_i=wb_i: e.matmul(
                        pX[:], wbar[0:1, wb_i * 128:(wb_i + 1) * 128], nmu_bf[0:1, sl],
                        start=False, stop=True), incs=[("st", 1)])
                    P.add("v", lambda e, S, sl=sl, pX=pX, dst=dst: e.tensor_tensor(
                        dst[:, sl], pX[:], rrep[:, sl], ALU.mult),
                        waits=[P.tok("st")], incs=[("sv", 1)])
                    bankfree[bk] = P.tok("sv")
                    if getattr(g, "fill", None):
                        g.fill(3)

            vt_s = vt4[:, :, h % 4, :]
            if h % 4 == 0:
                pass
            if h % 4 == 0:
                P.add("t", None, waits=[bankfree[1]])
                for hh in range(4):
                    for dc in range(NC):
                        P.add("t", lambda e, S, hh=hh, dc=dc, wv=wv: e.matmul(
                            pB[0:1, hh * 128:(hh + 1) * 128], onec_b[:],
                            wv[:, dc, hh, :],
                            start=(dc == 0), stop=(dc == NC - 1)))
                P.add("t", None, incs=[("st", 1)])
                P.add("s", lambda e, S: e.copy(wbar4[0:1, :], pB[0:1, 0:512]),
                      waits=[P.tok("st")], incs=[("ss", 1)])
                bankfree[1] = P.tok("ss")
                wbar4_ok = P.tok("ss")
                for tc in range(NC):
                    csl = slice(tc * 128, (tc + 1) * 128)
                    bk = tc % 2
                    pX = PS[bk]
                    P.add("t", None, waits=[bankfree[bk], wbar4_ok] +
                          ([vt4_used] if tc == 0 else []))
                    for dc in range(NC):
                        P.add("t", lambda e, S, dc=dc, csl=csl, pX=pX, wv=wv: e.matmul(
                            pX[:], rb[:, dc, csl],
                            wv[:, dc, :, :].rearrange("p h d -> p (h d)"),
                            start=(dc == 0), stop=False))
                    P.add("t", lambda e, S, csl=csl, pX=pX: e.matmul(
                        pX[:], nmu_bf[0:1, csl], wbar4[0:1, :],
                        start=False, stop=True), incs=[("st", 1)])
                    P.add("v", lambda e, S, tc=tc, pX=pX: e.tensor_scalar(
                        vt4[:, tc, :, :].rearrange("p h d -> p (h d)"), pX[:],
                        rstd_cols[:, tc:tc + 1], None, ALU.mult),
                        waits=[P.tok("st")], incs=[("sv", 1)])
                    bankfree[bk] = P.tok("sv")
                    if getattr(g, "fill", None):
                        g.fill(2)
                vt4_used_next = P.tok("st")
            w_consumed[h] = P.tok("st")

            for src, row in ((q_s, rowa), (k_s, rowb)):
                for tb in range(NTB):
                    sl = slice(tb * TB, (tb + 1) * TB)
                    bk = tb % 2
                    pX = PS[bk]
                    P.add("v", lambda e, S, sl=sl, src=src: e.tensor_tensor(
                        scf[0][:], src[:, sl], src[:, sl], ALU.mult), incs=[("sv", 1)])
                    P.add("t", lambda e, S, pX=pX: e.matmul(
                        pX[0:1, :], onec_f[:], scf[0][:], start=True, stop=True),
                        waits=[P.tok("sv"), bankfree[bk]], incs=[("st", 1)])
                    P.add("s", lambda e, S, sl=sl, pX=pX, row=row: e.copy(
                        row[0:1, sl], pX[0:1, :]),
                        waits=[P.tok("st")], incs=[("ss", 1)])
                    bankfree[bk] = P.tok("ss")
                    P.add("v", None, waits=[P.tok("st")])
            P.barrier(("t", "v", "s"))
            P.add("v", lambda e, S: e.reduce_max(
                kmax_s[0:1, 0:1], rowb[0:1, :], axis=AX.X), incs=[("sv", 1)])
            P.add("s", lambda e, S: e.activation(
                kmax_s[0:1, 1:2], kmax_s[0:1, 0:1], AF.Ln, scale=cstf[0:1, 5:6]),
                waits=[P.tok("sv")])
            P.add("s", lambda e, S: e.activation(
                kmax_s[0:1, 2:3], kmax_s[0:1, 1:2], AF.Exp, scale=cstf[0:1, 6:7]))
            P.add("s", lambda e, S: e.activation(rowa[0:1, :], rowa[0:1, :], AF.Ln))
            P.add("s", lambda e, S: e.activation(
                rowa[0:1, :], rowa[0:1, :], AF.Exp, scale=cstf[0:1, 6:7]),
                incs=[("ss", 1)])
            P.add("v", lambda e, S: e.tensor_scalar(
                mhat_bf[0:1, :], rowa[0:1, :], kmax_s[0:1, 2:3], 0.05, ALU.mult, ALU.add),
                waits=[P.tok("ss")], incs=[("sv", 1)])
            mhat_ok = P.tok("sv")

            for qb in range(NTB):
                sl = slice(qb * TB, (qb + 1) * TB)
                nkc = 4 * qb + 4
                P.add("t", lambda e, S, sl=sl: e.matmul(
                    pB[:], oner_b[:], mhat_bf[0:1, sl], start=True, stop=True),
                    waits=[mhat_ok, bankfree[1]], incs=[("st", 1)])
                P.add("s", lambda e, S: e.copy(mrep[:], pB[:]),
                      waits=[P.tok("st")], incs=[("ss", 1)])
                bankfree[1] = P.tok("ss")
                mrep_ok = P.tok("ss")
                scf_free = [P.tok("ss"), P.tok("ss")]
                for kc in range(nkc):
                    bk = kc % 2
                    pX = PS[bk]
                    P.add("t", lambda e, S, kc=kc, sl=sl, pX=pX: e.matmul(
                        pX[:], k_s[:, kc * 128:(kc + 1) * 128], q_s[:, sl],
                        start=True, stop=True),
                        waits=[bankfree[bk]], incs=[("st", 1)])
                    mm_tok = P.tok("st")
                    sb = scf[kc % 2]
                    P.add("v", lambda e, S, pX=pX, sb=sb: e.scalar_tensor_tensor(
                        sb[:], pX[:], ISQRT_DH, mrep[:], ALU.mult, ALU.subtract),
                        waits=[mm_tok, mrep_ok, scf_free[kc % 2]], incs=[("sv", 1)])
                    bankfree[bk] = P.tok("sv")
                    dd = kc - 4 * qb
                    if dd >= 0:
                        P.add("v", lambda e, S, sb=sb, dd=dd: e.tensor_tensor(
                            sb[:], sb[:], cm_s[:, dd, :], ALU.add),
                            waits=[cm_ok], incs=[("sv", 1)])
                    P.add("s", lambda e, S, kc=kc, sb=sb: e.activation(
                        attnT[:, kc, :], sb[:], AF.Exp),
                        waits=[P.tok("sv"), attnT_free], incs=[("ss", 1)])
                    scf_free[kc % 2] = P.tok("ss")
                exp_ok = P.tok("ss")
                P.add("t", None, waits=[exp_ok, bankfree[0], bankfree[1]])
                for kc in range(nkc):
                    P.add("t", lambda e, S, kc=kc, nkc=nkc, vt_s=vt_s: e.matmul(
                        pA[:], vt_s[:, kc, :], attnT[:, kc, :],
                        start=(kc == 0), stop=(kc == nkc - 1)))
                for kc in range(nkc):
                    P.add("t", lambda e, S, kc=kc, nkc=nkc: e.matmul(
                        pB[0:1, :], onec_b[:], attnT[:, kc, :],
                        start=(kc == 0), stop=(kc == nkc - 1)))
                P.add("t", None, incs=[("st", 1)])
                zl_ok = P.tok("st")
                attnT_free = zl_ok
                P.add("v", lambda e, S: e.reciprocal(rowb[0:1, 0:TB], pB[0:1, :]),
                      waits=[zl_ok])
                P.add("v", lambda e, S: e.tensor_copy(lrec_bf[0:1, :], rowb[0:1, 0:TB]),
                      incs=[("sv", 1)])
                P.add("t", lambda e, S: e.matmul(
                    pB[:], oner_b[:], lrec_bf[0:1, :], start=True, stop=True),
                    waits=[P.tok("sv")], incs=[("st", 1)])
                P.add("s", lambda e, S: e.copy(rep_sb[:], pB[:]),
                      waits=[P.tok("st")], incs=[("ss", 1)])
                bankfree[1] = P.tok("ss")
                zb = zbs[qb % 2]
                P.add("v", lambda e, S, zb=zb: e.tensor_tensor(
                    zb[:], pA[:], rep_sb[:], ALU.mult),
                    waits=[P.tok("ss"), zb_stored[qb % 2]], incs=[("sv", 1)])
                bankfree[0] = P.tok("sv")
                zb_stored[qb % 2] = dma("y", zscr_d[h][:, sl], zb[:],
                                        waits=[P.tok("sv")], sem=s(f"dzs{qb % 2}"))
            if h % 4 == 3:
                vt4_used = P.tok("st")
            P.barrier(("t", "v", "s"))

        zq_used = {}
        wo_last = [("st", 0), ("st", 0)]
        r0_last = [("sv", 0), ("sv", 0)]
        r1_last = [(s("ds0"), 0), (s("ds1"), 0)]
        for qb in range(NTB):
            sl = slice(qb * TB, (qb + 1) * TB)
            zbuf = zqb[0]
            zsrc = zscr_d[:, :, sl].rearrange("h p t -> p h t")
            zw = [dma("g", zbuf[:, 4 * i:4 * (i + 1), :], zsrc[:, 4 * i:4 * (i + 1), :],
                      waits=[P.tok(s("dzs0")), P.tok(s("dzs1")), P.tok("st")] +
                            ([zq_used[qb - 1]] if qb >= 1 else []),
                      sem=s("dzq"))
                  for i in range(4)]
            z_ok = zw[-1]
            for dc in range(NC):
                par = dc % 2
                wbuf = wob[par]
                wo_ok = dma("g", wbuf[:], wo_d[dc], waits=[wo_last[par]], sem=s(f"dwo{par}"))
                r0buf = r0t[par]
                r0_ok = dma("y", r0buf[:], xin[:, dc, sl], waits=[r0_last[par]], sem=s(f"da{par}"))
                pX = PS[par]
                P.add("t", None, waits=[z_ok, wo_ok, bankfree[par]])
                for hh in range(NH):
                    P.add("t", lambda e, S, hh=hh, wbuf=wbuf, zbuf=zbuf, pX=pX: e.matmul(
                        pX[:], wbuf[:, hh, :], zbuf[:, hh, :],
                        start=(hh == 0), stop=(hh == NH - 1)))
                P.add("t", None, incs=[("st", 1)])
                wo_last[par] = P.tok("st")
                r1b = r1t[par]
                P.add("v", lambda e, S, pX=pX, r0buf=r0buf, r1b=r1b: e.tensor_tensor(
                    r1b[:], pX[:], r0buf[:], ALU.add),
                    waits=[P.tok("st"), r0_ok, r1_last[par]], incs=[("sv", 1)])
                bankfree[par] = P.tok("sv")
                r0_last[par] = P.tok("sv")
                r1_last[par] = dma("y", r1_d[:, dc, sl], r1b[:], waits=[P.tok("sv")], sem=s(f"ds{par}"))
                if getattr(g, "fill", None):
                    g.fill(3)
            zq_used[qb] = P.tok("st")

def emit_l0ffn(g, dram, after=()):
    nc, P, C = g.nc, g.P, g.C
    s = g.s
    pA, pB = g.pA, g.pB
    PS = [g.pA, g.pB]
    onec_b, onec_f, oner_b, oner_f = g.onec_b, g.onec_f, g.oner_b, g.oner_f
    cstf, h128p, h128n = g.cstf, g.h128p, g.h128n
    consts_ok = g.consts_ok
    for _eng in ENGS:
        P.pending[_eng] = P.pending[_eng] + list(after)
    r1i = dram["r1i"]
    h128_d = dram["h128"]
    suug_d = dram["su_ug"]
    sudp_d = dram["su_dp"]
    wscu_d = dram["wsc_up"]
    wscg_d = dram["wsc_gt"]
    wscd_d = dram["wsc_dn"]
    wup_d = dram["wup"]
    wgt_d = dram["wgt"]
    wdn_d = dram["wdn"]
    r2_d = dram["r2"]


    S1 = float(1.0 / np.sqrt(2048.0))
    S2 = float(1.0 / np.sqrt(8192.0))

    if True:
        r1c = [C.alloc([128, TB], F32) for i in range(2)]
        xln = C.alloc([128, NC, TB], BF16)
        t1 = C.alloc([128, NC, TB], BF16)
        hbuf = C.alloc([128, 64, TB], BF16)
        wub = [C.alloc([128, NC, 128], BF16) for i in range(2)]
        wgb = [C.alloc([128, NC, 128], BF16) for i in range(2)]
        wdb = [C.alloc([128, 64, 128], BF16) for i in range(2)]
        su_ug = C.alloc([128, NC], F32)
        su_dp = C.alloc([128, 64], F32)
        wscu = C.alloc([128, 64], F32)
        wscg = C.alloc([128, 64], F32)
        wscd = C.alloc([128, NC], F32)
        sqs = C.alloc([128, TB], BF16)
        rowm = C.alloc([1, TB], F32)
        rowe = C.alloc([1, TB], F32)
        nmurep = C.alloc([128, TB], F32)
        rstdrep = C.alloc([128, TB], F32)
        dAb = [nmurep, rstdrep]
        tmpf = C.alloc([128, TB], F32)
        upt = [C.alloc([128, TB], F32) for i in range(2)]
        silt = [C.alloc([128, TB], F32) for i in range(2)]
        btmp = C.alloc([128, TB], BF16)
        r2t = [C.alloc([128, TB], F32) for i in range(2)]

        def dma(q, out_ap, in_ap, sem, waits=()):
            P.add(q, lambda e, S, o=out_ap, i=in_ap: e.dma_start(out=o, in_=i),
                  waits=waits, incs=[(sem, 16)])
            return (sem, P.now(sem))

        for t_s, t_d in ((h128p, h128_d), (su_ug, suug_d), (su_dp, sudp_d),
                         (wscu, wscu_d), (wscg, wscg_d), (wscd, wscd_d)):
            cst_ok = dma("g", t_s[:], t_d[:], s("dcst"))
        P.add("v", lambda e, S: e.tensor_scalar_mul(wscu[:], wscu[:], S1),
              waits=[cst_ok, consts_ok])
        P.add("v", lambda e, S: e.tensor_scalar_mul(wscg[:], wscg[:], S1))
        P.add("v", lambda e, S: e.tensor_scalar_mul(wscd[:], wscd[:], S2),
              incs=[("sv", 1)])
        P.barrier(("v",))

        bankfree = [P.tok("sv"), P.tok("sv")]
        wu_last = [("st", 0), ("st", 0)]
        wg_last = [("st", 0), ("st", 0)]
        wd_last = [("st", 0), ("st", 0)]
        r2_last = [(s("ds0"), 0), (s("ds1"), 0)]

        r1c_last = [("sv", 0), ("sv", 0)]
        for ts in range(NTB):
            sl = slice(ts * TB, (ts + 1) * TB)

            for dc in range(NC):
                par = dc % 2
                r1_ok = dma("g", r1c[par][:], r1i[:, dc, sl], s(f"dr1{par}"),
                            waits=[r1c_last[par]])
                P.add("v", lambda e, S, par=par: e.tensor_tensor(
                    sqs[:], r1c[par][:], r1c[par][:], ALU.mult),
                    waits=[r1_ok, P.tok("st")], incs=[("sv", 1)])
                P.add("t", lambda e, S, par=par, dc=dc: e.matmul(
                    pA[0:1, :], onec_f[:], r1c[par][:],
                    start=(dc == 0), stop=(dc == NC - 1)),
                    waits=[P.tok("sv")] + ([bankfree[0], bankfree[1]] if dc == 0 else []))
                P.add("t", lambda e, S, dc=dc: e.matmul(
                    pB[0:1, :], onec_b[:], sqs[:],
                    start=(dc == 0), stop=(dc == NC - 1)), incs=[("st", 1)])
                r1c_last[par] = P.tok("st")
            sq_mm_ok = P.tok("st")
            sx_ok = P.tok("st")
            P.add("s", lambda e, S: e.mul(rowm[0:1, :], pA[0:1, :], 1.0 / D),
                  waits=[sx_ok])
            P.add("s", lambda e, S: e.mul(rowe[0:1, :], pB[0:1, :], 1.0 / D),
                  incs=[("ss", 1)])
            st_ok = P.tok("ss")
            P.add("v", lambda e, S: e.tensor_tensor(
                tmpf[0:1, 0:TB], rowm[0:1, :], rowm[0:1, :], ALU.mult), waits=[st_ok])
            P.add("v", lambda e, S: e.tensor_tensor(
                rowe[0:1, :], rowe[0:1, :], tmpf[0:1, 0:TB], ALU.subtract))
            P.add("v", lambda e, S: e.tensor_scalar_mul(rowm[0:1, :], rowm[0:1, :], -1.0),
                  incs=[("sv", 1)])
            P.add("s", lambda e, S: e.activation(rowe[0:1, :], rowe[0:1, :], AF.Ln,
                                                 bias=cstf[0:1, 0:1]), waits=[P.tok("sv")])
            P.add("s", lambda e, S: e.activation(rowe[0:1, :], rowe[0:1, :], AF.Exp,
                                                 scale=cstf[0:1, 1:2]), incs=[("ss", 1)])
            P.add("t", lambda e, S: e.matmul(pA[:], oner_f[:], rowm[0:1, :],
                                             start=True, stop=True),
                  waits=[P.tok("ss"), bankfree[0]], incs=[("st", 1)])
            P.add("v", lambda e, S: e.tensor_copy(nmurep[:], pA[:]),
                  waits=[P.tok("st")], incs=[("sv", 1)])
            bankfree[0] = P.tok("sv")
            P.add("t", lambda e, S: e.matmul(pB[:], oner_f[:], rowe[0:1, :],
                                             start=True, stop=True),
                  waits=[P.tok("ss"), bankfree[1], sq_mm_ok], incs=[("st", 1)])
            P.add("v", lambda e, S: e.tensor_copy(rstdrep[:], pB[:]),
                  waits=[P.tok("st")], incs=[("sv", 1)])
            bankfree[1] = P.tok("sv")
            rep_ok = P.tok("sv")

            for dc in range(NC):
                par = dc % 2
                r1_ok = dma("g", r1c[par][:], r1i[:, dc, sl], s(f"dr1{par}"),
                            waits=[r1c_last[par]])
                P.add("v", lambda e, S, dc=dc, par=par: e.tensor_tensor(
                    tmpf[:], r1c[par][:], nmurep[:], ALU.add),
                    waits=[r1_ok] + ([rep_ok] if dc == 0 else []))
                P.add("v", lambda e, S, dc=dc: e.scalar_tensor_tensor(
                    xln[:, dc, :], tmpf[:], su_ug[:, dc:dc + 1], rstdrep[:],
                    ALU.mult, ALU.mult), incs=[("sv", 1)])
                r1c_last[par] = P.tok("sv")
            xln_ok = P.tok("sv")

            for ec in range(NC):
                bk = ec % 2
                pX = PS[bk]
                P.add("t", None, waits=[xln_ok, bankfree[bk]])
                jcs = sorted(range(NC), key=lambda j: _h16sign(ec, j) < 0)
                for jj, jc in enumerate(jcs):
                    hs = h128p if _h16sign(ec, jc) > 0 else h128n
                    P.add("t", lambda e, S, jc=jc, hs=hs, pX=pX, jj=jj: e.matmul(
                        pX[:], hs[:], xln[:, jc, :],
                        start=(jj == 0), stop=(jj == NC - 1)))
                P.add("t", None, incs=[("st", 1)])
                P.add("v", lambda e, S, ec=ec, pX=pX: e.tensor_copy(t1[:, ec, :], pX[:]),
                      waits=[P.tok("st")], incs=[("sv", 1)])
                bankfree[bk] = P.tok("sv")
                if getattr(g, "fill", None):
                    g.fill(3)
            t1_ok = P.tok("sv")

            hw_done = {}
            for oc in range(64):
                par = oc % 2
                wu_ok = dma("g", wub[par][:], wup_d[oc], s(f"dwu{par}"), waits=[wu_last[par]])
                wg_ok = dma("g", wgb[par][:], wgt_d[oc], s(f"dwg{par}"), waits=[wg_last[par]])
                upb = upt[oc % 2]
                slb = silt[oc % 2]
                for dc in range(NC):
                    P.add("t", lambda e, S, dc=dc, par=par: e.matmul(
                        pA[:], wub[par][:, dc, :], t1[:, dc, :],
                        start=(dc == 0), stop=(dc == NC - 1)),
                        waits=([t1_ok, wu_ok, bankfree[0]] if dc == 0 else []))
                P.add("t", None, incs=[("st", 1)])
                up_mm = P.tok("st")
                for dc in range(NC):
                    P.add("t", lambda e, S, dc=dc, par=par: e.matmul(
                        pB[:], wgb[par][:, dc, :], t1[:, dc, :],
                        start=(dc == 0), stop=(dc == NC - 1)),
                        waits=([wg_ok, bankfree[1]] if dc == 0 else []))
                P.add("t", None, incs=[("st", 1)])
                gt_mm = P.tok("st")
                wu_last[par] = gt_mm
                wg_last[par] = gt_mm
                P.add("v", lambda e, S, oc=oc, upb=upb: e.tensor_scalar(
                    upb[:], pA[:], wscu[:, oc:oc + 1], None, ALU.mult),
                    waits=[up_mm, hw_done.get(oc - 2, ("sv", 0))], incs=[("sv", 1)])
                bankfree[0] = P.tok("sv")
                P.add("s", lambda e, S, oc=oc, slb=slb: e.activation(
                    slb[:], pB[:], AF.Silu, scale=wscg[:, oc:oc + 1]),
                    waits=[gt_mm, hw_done.get(oc - 2, ("sv", 0))], incs=[("ss", 1)])
                bankfree[1] = P.tok("ss")
                P.add("v", lambda e, S, oc=oc, slb=slb, upb=upb: e.scalar_tensor_tensor(
                    hbuf[:, oc, :], slb[:], su_dp[:, oc:oc + 1], upb[:],
                    ALU.mult, ALU.mult),
                    waits=[P.tok("ss")], incs=[("sv", 1)])
                hw_done[oc] = P.tok("sv")
                if getattr(g, "fill", None):
                    g.fill(6)
            h_ok = P.tok("sv")

            for stg in range(6):
                stride = 32 >> stg
                for base in range(0, 64, 2 * stride):
                    for i in range(base, base + stride):
                        j = i + stride
                        P.add("v", lambda e, S, i=i, j=j: e.tensor_tensor(
                            btmp[:], hbuf[:, i, :], hbuf[:, j, :], ALU.subtract))
                        P.add("v", lambda e, S, i=i, j=j: e.tensor_tensor(
                            hbuf[:, i, :], hbuf[:, i, :], hbuf[:, j, :], ALU.add))
                        P.add("v", lambda e, S, j=j: e.tensor_copy(
                            hbuf[:, j, :], btmp[:]))
            P.add("v", None, incs=[("sv", 1)])
            bfl_ok = P.tok("sv")
            for oc in range(64):
                bk = oc % 2
                pX = PS[bk]
                P.add("t", lambda e, S, oc=oc, pX=pX: e.matmul(
                    pX[:], h128p[:], hbuf[:, oc, :], start=True, stop=True),
                    waits=[bfl_ok, bankfree[bk]], incs=[("st", 1)])
                P.add("v", lambda e, S, oc=oc, pX=pX: e.tensor_copy(
                    hbuf[:, oc, :], pX[:]),
                    waits=[P.tok("st")], incs=[("sv", 1)])
                bankfree[bk] = P.tok("sv")
            t2_ok = P.tok("sv")

            for dco in range(NC):
                par = dco % 2
                wd_ok = dma("g", wdb[par][:], wdn_d[dco], s(f"dwd{par}"), waits=[wd_last[par]])
                pX = PS[par]
                P.add("t", None, waits=[t2_ok, wd_ok, bankfree[par]])
                for ic in range(64):
                    P.add("t", lambda e, S, ic=ic, par=par, pX=pX: e.matmul(
                        pX[:], wdb[par][:, ic, :], hbuf[:, ic, :],
                        start=(ic == 0), stop=(ic == 63)))
                P.add("t", None, incs=[("st", 1)])
                wd_last[par] = P.tok("st")
                rb2 = r2t[par]
                r1_ok = dma("g", r1c[par][:], r1i[:, dco, sl], s(f"dr1{par}"),
                            waits=[r1c_last[par]])
                P.add("v", lambda e, S, dco=dco, pX=pX, rb2=rb2, par=par: e.scalar_tensor_tensor(
                    rb2[:], pX[:], wscd[:, dco:dco + 1], r1c[par][:],
                    ALU.mult, ALU.add),
                    waits=[P.tok("st"), r2_last[par], r1_ok], incs=[("sv", 1)])
                r1c_last[par] = P.tok("sv")
                bankfree[par] = P.tok("sv")
                r2_last[par] = dma("y", r2_d[:, dco, sl], rb2[:], s(f"ds{par}"),
                                   waits=[P.tok("sv")])
                if getattr(g, "fill", None):
                    g.fill(14)

def emit_l1attn(g, dram, after=()):
    nc, P, C = g.nc, g.P, g.C
    s = g.s
    pA, pB = g.pA, g.pB
    PS = [g.pA, g.pB]
    onec_b, onec_f, oner_b, oner_f = g.onec_b, g.onec_f, g.oner_b, g.oner_f
    cstf, h128p, h128n = g.cstf, g.h128p, g.h128n
    consts_ok = g.consts_ok
    for _eng in ENGS:
        P.pending[_eng] = P.pending[_eng] + list(after)
    r2i = dram["r2i"]
    win_d = dram["win"]
    convw_d = dram["convw"]
    convb_d = dram["convb"]
    wxdt_d = dram["wxdt"]
    wxB_d = dram["wxB"]
    wxC_d = dram["wxC"]
    wdt_d = dram["wdt"]
    dtb_d = dram["dtb"]
    alog_d = dram["alog"]
    dvec_d = dram["dvec"]
    wout_d = dram["wout"]
    h128_d = dram["h128"]
    e16_d = dram["e16"]
    r3_d = dram["r3"]


    if True:
        scn = C.alloc([128, 16, TB], BF16)
        xln = C.alloc([128, NC, TB], BF16)
        xconv = C.alloc([128, DIC, 3 + TB], BF16)
        halo3 = C.alloc([128, DIC, 3], BF16)
        dA_all = xconv.rearrange("p c t -> p (c t)").bitcast(F32)[
            :, 0:16 * TB].rearrange("p (n t) -> p n t", t=TB)
        zsil = C.alloc([128, DIC, TB], BF16)
        ubuf = C.alloc([128, DIC, TB], BF16)
        crep = C.alloc([128, 16, TB], BF16)
        brep = xln
        winb = [C.alloc([128, NC, 128], BF16) for i in range(2)]
        woutb = C.alloc([128, DIC, 128], BF16)
        wxw = C.alloc([128, DIC, 128], BF16)
        wxB = C.alloc([128, DIC, 16], BF16)
        wxC = C.alloc([128, DIC, 16], BF16)
        convw = C.alloc([128, DIC, 4], F32)
        convb = C.alloc([128, DIC], F32)
        dtb = C.alloc([128, DIC], F32)
        negA = C.alloc([128, DIC, 16], F32)
        dvec = C.alloc([128, DIC], F32)
        carry = C.alloc([128, DIC, 16], F32)
        e16 = C.alloc([16, 16 * 128], BF16)
        bc_s = C.alloc([16, TB], BF16)
        dtr = C.alloc([128, TB], BF16)
        delta = C.alloc([128, TB], F32)
        dub = C.alloc([128, TB], BF16)
        delb = C.alloc([128, TB], BF16)
        r2c = [C.alloc([128, TB], F32) for i in range(2)]
        rowm = C.alloc([1, TB], F32)
        rowe = C.alloc([1, TB], F32)
        nmurep = C.alloc([128, TB], F32)
        rstdrep = C.alloc([128, TB], F32)
        dAb = [nmurep, rstdrep]
        tmpf = C.alloc([128, TB], F32)
        tmpf2 = C.alloc([128, TB], F32)
        btmp = C.alloc([128, TB], BF16)
        sqs = btmp
        r3t = [C.alloc([128, TB], F32) for i in range(2)]

        def dma(q, out_ap, in_ap, sem, waits=()):
            P.add(q, lambda e, S, o=out_ap, i=in_ap: e.dma_start(out=o, in_=i),
                  waits=waits, incs=[(sem, 16)])
            return (sem, P.now(sem))

        for t_s, t_d in ((wxB, wxB_d), (wxC, wxC_d),
                         (convw, convw_d), (convb, convb_d), (dtb, dtb_d),
                         (negA, alog_d), (dvec, dvec_d), (e16, e16_d),
                         (h128p, h128_d)):
            cst_ok = dma("g", t_s[:], t_d[:], s("dcst"))
        P.add("g", lambda e, S: e.memset(carry[:], 0.0))
        for dc in range(DIC):
            P.add("g", lambda e, S, dc=dc: e.memset(xconv[:, dc, 0:3], 0.0))
        P.add("s", lambda e, S: e.activation(negA[:], negA[:], AF.Exp),
              waits=[cst_ok, consts_ok], incs=[("ss", 1)])
        P.add("v", lambda e, S: e.tensor_scalar_mul(negA[:], negA[:], -1.0),
              waits=[P.tok("ss")], incs=[("sv", 1)])
        P.barrier(("t", "v", "s"))

        bankfree = [P.tok("sv"), P.tok("sv")]
        win_last = [("st", 0), ("st", 0)]
        wxw_last = ("st", 0)
        wout_last = ("st", 0)
        r2c_last = [("sv", 0), ("sv", 0)]
        r3_last = [(s("ds0"), 0), (s("ds1"), 0)]

        for ts in range(NTB):
            sl = slice(ts * TB, (ts + 1) * TB)

            if ts > 0:
                for dc in range(DIC):
                    P.add("v", lambda e, S, dc=dc: e.tensor_copy(
                        xconv[:, dc, 0:3], halo3[:, dc, :]),
                        waits=[P.tok("ss")] if dc == 0 else [])
                P.add("v", None, incs=[("sv", 1)])
            halo_ok = P.tok("sv")

            for dc in range(NC):
                par = dc % 2
                r2_ok = dma("y", r2c[par][:], r2i[:, dc, sl], s(f"drc{par}"),
                            waits=[r2c_last[par]])
                P.add("v", lambda e, S, par=par: e.tensor_tensor(
                    sqs[:], r2c[par][:], r2c[par][:], ALU.mult),
                    waits=[r2_ok, P.tok("st")], incs=[("sv", 1)])
                P.add("t", lambda e, S, par=par, dc=dc: e.matmul(
                    pA[0:1, :], onec_f[:], r2c[par][:],
                    start=(dc == 0), stop=(dc == NC - 1)),
                    waits=[P.tok("sv")] + ([bankfree[0], bankfree[1]] if dc == 0 else []))
                P.add("t", lambda e, S, dc=dc: e.matmul(
                    pB[0:1, :], onec_b[:], sqs[:],
                    start=(dc == 0), stop=(dc == NC - 1)), incs=[("st", 1)])
                r2c_last[par] = P.tok("st")
            P.add("s", lambda e, S: e.mul(rowm[0:1, :], pA[0:1, :], 1.0 / D),
                  waits=[P.tok("st")])
            P.add("s", lambda e, S: e.mul(rowe[0:1, :], pB[0:1, :], 1.0 / D),
                  incs=[("ss", 1)])
            P.add("v", lambda e, S: e.tensor_tensor(
                tmpf[0:1, 0:TB], rowm[0:1, :], rowm[0:1, :], ALU.mult),
                waits=[P.tok("ss")])
            P.add("v", lambda e, S: e.tensor_tensor(
                rowe[0:1, :], rowe[0:1, :], tmpf[0:1, 0:TB], ALU.subtract))
            P.add("v", lambda e, S: e.tensor_scalar_mul(rowm[0:1, :], rowm[0:1, :], -1.0),
                  incs=[("sv", 1)])
            P.add("s", lambda e, S: e.activation(rowe[0:1, :], rowe[0:1, :], AF.Ln,
                                                 bias=cstf[0:1, 0:1]), waits=[P.tok("sv")])
            P.add("s", lambda e, S: e.activation(rowe[0:1, :], rowe[0:1, :], AF.Exp,
                                                 scale=cstf[0:1, 1:2]), incs=[("ss", 1)])
            P.add("t", lambda e, S: e.matmul(pA[:], oner_f[:], rowm[0:1, :],
                                             start=True, stop=True),
                  waits=[P.tok("ss")], incs=[("st", 1)])
            P.add("v", lambda e, S: e.tensor_copy(nmurep[:], pA[:]),
                  waits=[P.tok("st")], incs=[("sv", 1)])
            P.add("t", lambda e, S: e.matmul(pB[:], oner_f[:], rowe[0:1, :],
                                             start=True, stop=True), incs=[("st", 1)])
            P.add("v", lambda e, S: e.tensor_copy(rstdrep[:], pB[:]),
                  waits=[P.tok("st")], incs=[("sv", 1)])
            rep_ok = P.tok("sv")
            bankfree = [rep_ok, rep_ok]

            for dc in range(NC):
                par = dc % 2
                r2_ok = dma("y", r2c[par][:], r2i[:, dc, sl], s(f"drc{par}"),
                            waits=[r2c_last[par]])
                P.add("v", lambda e, S, dc=dc, par=par: e.tensor_tensor(
                    tmpf[:], r2c[par][:], nmurep[:], ALU.add),
                    waits=[r2_ok] + ([rep_ok] if dc == 0 else []))
                P.add("v", lambda e, S, dc=dc: e.tensor_tensor(
                    xln[:, dc, :], tmpf[:], rstdrep[:], ALU.mult), incs=[("sv", 1)])
                r2c_last[par] = P.tok("sv")
            xln_ok = P.tok("sv")

            for oc in range(64):
                par = oc % 2
                wi_ok = dma("g", winb[par][:], win_d[oc], s(f"dwi{par}"),
                            waits=[win_last[par]])
                pX = PS[par]
                P.add("t", None, waits=[xln_ok, wi_ok, bankfree[par]] +
                      ([halo_ok] if oc < DIC else []))
                for dc in range(NC):
                    P.add("t", lambda e, S, dc=dc, par=par, pX=pX: e.matmul(
                        pX[:], winb[par][:, dc, :], xln[:, dc, :],
                        start=(dc == 0), stop=(dc == NC - 1)))
                P.add("t", None, incs=[("st", 1)])
                mm_ok = P.tok("st")
                win_last[par] = mm_ok
                if oc < DIC:
                    P.add("v", lambda e, S, oc=oc, pX=pX: e.tensor_copy(
                        xconv[:, oc, 3:3 + TB], pX[:]),
                        waits=[mm_ok], incs=[("sv", 1)])
                    bankfree[par] = P.tok("sv")
                    if getattr(g, "fill", None):
                        g.fill(3)
                    if oc == DIC - 1:
                        xpart_ok = P.tok("sv")
                else:
                    P.add("s", lambda e, S, oc=oc, pX=pX: e.activation(
                        zsil[:, oc - DIC, :], pX[:], AF.Silu),
                        waits=[mm_ok], incs=[("ss", 1)])
                    bankfree[par] = P.tok("ss")
            inproj_ok = [P.tok("sv"), P.tok("ss")]

            conv_sil = {}
            for dc in range(DIC):
                tb_ = tmpf if dc % 2 == 0 else tmpf2
                P.add("v", lambda e, S, dc=dc, tb_=tb_: e.tensor_scalar(
                    tb_[:], xconv[:, dc, 0:TB], convw[:, dc, 0:1], None, ALU.mult),
                    waits=([xpart_ok] if dc == 0 else []) +
                          ([conv_sil[dc - 2]] if dc >= 2 else []))
                for k2 in range(1, 4):
                    P.add("v", lambda e, S, dc=dc, k2=k2, tb_=tb_: e.scalar_tensor_tensor(
                        tb_[:], xconv[:, dc, k2:k2 + TB], convw[:, dc, k2:k2 + 1],
                        tb_[:], ALU.mult, ALU.add))
                P.add("v", None, incs=[("sv", 1)])
                P.add("s", lambda e, S, dc=dc, tb_=tb_: e.activation(
                    ubuf[:, dc, :], tb_[:], AF.Silu, bias=convb[:, dc:dc + 1]),
                    waits=[P.tok("sv")], incs=[("ss", 1)])
                conv_sil[dc] = P.tok("ss")
                P.add("v", lambda e, S, dc=dc: e.tensor_copy(
                    halo3[:, dc, :], xconv[:, dc, TB:TB + 3]))
            P.add("v", None, incs=[("sv", 1)])
            halo_saved = P.tok("sv")
            u_ok = P.tok("ss")

            wx_ok = dma("g", wxw[:], wxdt_d[:], s("dwx"), waits=[wxw_last])
            P.add("t", None, waits=[u_ok, bankfree[0], wx_ok])
            for dc in range(DIC):
                P.add("t", lambda e, S, dc=dc: e.matmul(
                    pA[:], wxw[:, dc, :], ubuf[:, dc, :],
                    start=(dc == 0), stop=(dc == DIC - 1)))
            P.add("t", None, incs=[("st", 1)])
            wxw_last = P.tok("st")
            P.add("v", lambda e, S: e.tensor_copy(dtr[:], pA[:]),
                  waits=[P.tok("st")], incs=[("sv", 1)])
            bankfree[0] = P.tok("sv")
            for nm_w, rep in ((wxB, brep), (wxC, crep)):
                P.add("t", None, waits=[bankfree[1]])
                for dc in range(DIC):
                    P.add("t", lambda e, S, dc=dc, nm_w=nm_w: e.matmul(
                        pB[0:16, :], nm_w[:, dc, :], ubuf[:, dc, :],
                        start=(dc == 0), stop=(dc == DIC - 1)))
                P.add("t", None, incs=[("st", 1)])
                P.add("s", lambda e, S: e.copy(bc_s[:], pB[0:16, :]),
                      waits=[P.tok("st")], incs=[("ss", 1)])
                bankfree[1] = P.tok("ss")
                for n in range(16):
                    bk = n % 2
                    pX = PS[bk]
                    P.add("t", lambda e, S, n=n, pX=pX: e.matmul(
                        pX[:], e16[:, n * 128:(n + 1) * 128], bc_s[:],
                        start=True, stop=True),
                        waits=[P.tok("ss"), bankfree[bk]], incs=[("st", 1)])
                    P.add("v", lambda e, S, n=n, pX=pX, rep=rep: e.tensor_copy(
                        rep[:, n, :], pX[:]),
                        waits=[P.tok("st")], incs=[("sv", 1)])
                    bankfree[bk] = P.tok("sv")
            bc_ok = P.tok("sv")

            if ts == 0 and getattr(g, "drain_build", None):
                g.drain_build()
            wd_ok = dma("g", wxw[:], wdt_d[:], s("dwx"), waits=[P.tok("st")])
            for dc in range(DIC):
                P.add("t", lambda e, S, dc=dc: e.matmul(
                    pA[:], wxw[:, dc, :], dtr[:], start=True, stop=True),
                    waits=([bankfree[0], bc_ok, wd_ok] if dc == 0 else [bankfree[0]]),
                    incs=[("st", 1)])
                dmm = P.tok("st")
                P.add("v", lambda e, S, dc=dc: e.tensor_scalar(
                    tmpf[:], pA[:], dtb[:, dc:dc + 1], None, ALU.add),
                    waits=[dmm], incs=[("sv", 1)])
                bankfree[0] = P.tok("sv")
                P.add("s", lambda e, S: e.activation(tmpf2[:], tmpf[:], AF.Abs),
                      waits=[P.tok("sv")])
                P.add("s", lambda e, S: e.activation(tmpf2[:], tmpf2[:], AF.Exp,
                                                     scale=cstf[:, 2:3]))
                P.add("s", lambda e, S: e.activation(tmpf2[:], tmpf2[:], AF.Ln,
                                                     bias=cstf[:, 3:4]), incs=[("ss", 1)])
                P.add("v", lambda e, S: e.tensor_scalar_max(delta[:], tmpf[:], 0.0),
                      waits=[P.tok("ss")])
                P.add("v", lambda e, S: e.tensor_tensor(
                    delta[:], delta[:], tmpf2[:], ALU.add), incs=[("sv", 1)])
                delta_ok = P.tok("sv")
                P.add("v", lambda e, S: e.tensor_copy(delb[:], delta[:]))
                P.add("v", lambda e, S, dc=dc: e.tensor_tensor(
                    dub[:], delb[:], ubuf[:, dc, :], ALU.mult))
                for nn in range(16):
                    P.add("v", lambda e, S, nn=nn: e.tensor_tensor(
                        scn[:, nn, :], dub[:], brep[:, nn, :], ALU.mult),
                        waits=(getattr(g, "build_wait", [])
                               if (ts == 0 and dc == 0 and nn == 0) else []))
                P.add("v", None, incs=[("sv", 1)])
                dbu_ok = P.tok("sv")
                half_toks = []
                for half in range(2):
                    ew = []
                    if half == 0:
                        ew = [delta_ok, halo_saved, u_ok]
                    for n in range(8 * half, 8 * half + 8):
                        P.add("s", lambda e, S, dc=dc, n=n: e.activation(
                            dA_all[:, n, :], delta[:], AF.Exp,
                            scale=negA[:, dc, n:n + 1]),
                            waits=(ew if n == 8 * half else []))
                    P.add("s", None, incs=[("ss", 1)])
                    half_toks.append(P.tok("ss"))
                for n in range(16):
                    P.add("v", lambda e, S, dc=dc, n=n: e.tensor_tensor_scan(
                        scn[:, n, :], dA_all[:, n, :], scn[:, n, :],
                        carry[:, dc, n:n + 1], ALU.mult, ALU.add),
                        waits=([half_toks[0], dbu_ok] if n == 0 else
                               ([half_toks[1]] if n == 8 else [])))
                P.add("v", None, incs=[("sv", 1)])
                scan_half_prev = P.tok("sv")
                scan_ok = P.tok("sv")
                P.add("v", lambda e, S, dc=dc: e.tensor_copy(
                    carry[:, dc, :], scn[:, :, TB - 1]))
                P.add("v", lambda e, S: e.tensor_tensor(scn[:], scn[:], crep[:], ALU.mult))
                P.add("v", lambda e, S: e.tensor_tensor(
                    scn[:, 0:8, :], scn[:, 0:8, :], scn[:, 8:16, :], ALU.add))
                P.add("v", lambda e, S: e.tensor_tensor(
                    scn[:, 0:4, :], scn[:, 0:4, :], scn[:, 4:8, :], ALU.add))
                P.add("v", lambda e, S: e.tensor_tensor(
                    scn[:, 0:2, :], scn[:, 0:2, :], scn[:, 2:4, :], ALU.add))
                P.add("v", lambda e, S: e.tensor_tensor(
                    tmpf[:], scn[:, 0, :], scn[:, 1, :], ALU.add))
                P.add("v", lambda e, S, dc=dc: e.scalar_tensor_tensor(
                    tmpf[:], ubuf[:, dc, :], dvec[:, dc:dc + 1], tmpf[:],
                    ALU.mult, ALU.add))
                P.add("v", lambda e, S, dc=dc: e.scalar_tensor_tensor(
                    zsil[:, dc, :], tmpf[:], cstf[:, 4:5], zsil[:, dc, :],
                    ALU.mult, ALU.mult), incs=[("sv", 1)])
            y_ok = P.tok("sv")

            for stg in range(5):
                stride = 16 >> stg
                for base in range(0, DIC, 2 * stride):
                    for i in range(base, base + stride):
                        j = i + stride
                        P.add("v", lambda e, S, i=i, j=j: e.tensor_tensor(
                            btmp[:], zsil[:, i, :], zsil[:, j, :], ALU.subtract),
                            waits=[y_ok] if (stg == 0 and i == 0) else [])
                        P.add("v", lambda e, S, i=i, j=j: e.tensor_tensor(
                            zsil[:, i, :], zsil[:, i, :], zsil[:, j, :], ALU.add))
                        P.add("v", lambda e, S, j=j: e.tensor_copy(
                            zsil[:, j, :], btmp[:]))
            P.add("v", None, incs=[("sv", 1)])
            bfl_ok = P.tok("sv")
            for ec in range(DIC):
                bk = ec % 2
                pX = PS[bk]
                P.add("t", lambda e, S, ec=ec, pX=pX: e.matmul(
                    pX[:], h128p[:], zsil[:, ec, :], start=True, stop=True),
                    waits=[bfl_ok, bankfree[bk]], incs=[("st", 1)])
                P.add("v", lambda e, S, ec=ec, pX=pX: e.tensor_copy(
                    zsil[:, ec, :], pX[:]),
                    waits=[P.tok("st")], incs=[("sv", 1)])
                bankfree[bk] = P.tok("sv")
            t2_ok = P.tok("sv")

            for dco in range(NC):
                par = dco % 2
                wo_ok = dma("g", woutb[:], wout_d[dco], s("dwo0"), waits=[wout_last])
                r2_ok = dma("y", r2c[par][:], r2i[:, dco, sl], s(f"drc{par}"),
                            waits=[r2c_last[par]])
                pX = PS[par]
                P.add("t", None, waits=[t2_ok, wo_ok, bankfree[par]])
                for ec in range(DIC):
                    P.add("t", lambda e, S, ec=ec, pX=pX: e.matmul(
                        pX[:], woutb[:, ec, :], zsil[:, ec, :],
                        start=(ec == 0), stop=(ec == DIC - 1)))
                P.add("t", None, incs=[("st", 1)])
                wout_last = P.tok("st")
                rb3 = r3t[par]
                P.add("v", lambda e, S, pX=pX, par=par, rb3=rb3: e.tensor_tensor(
                    rb3[:], pX[:], r2c[par][:], ALU.add),
                    waits=[P.tok("st"), r2_ok, r3_last[par]], incs=[("sv", 1)])
                bankfree[par] = P.tok("sv")
                r2c_last[par] = P.tok("sv")
                r3_last[par] = dma("y", r3_d[:, dco, sl], rb3[:], s(f"ds{par}"),
                                   waits=[P.tok("sv")])

def emit_l1ffn(g, dram, after=()):
    nc, P, C = g.nc, g.P, g.C
    s = g.s
    pA, pB = g.pA, g.pB
    PS = [g.pA, g.pB]
    onec_b, onec_f, oner_b, oner_f = g.onec_b, g.onec_f, g.oner_b, g.oner_f
    cstf, h128p, h128n = g.cstf, g.h128p, g.h128n
    consts_ok = g.consts_ok
    for _eng in ENGS:
        P.pending[_eng] = P.pending[_eng] + list(after)
    r3i = dram["r3i"]
    gidx_d = dram["gidx"]
    uidx_d = dram["uidx"]
    didx_d = dram["didx"]
    lut_d = dram["lut"]
    y_d = dram["yout"]
    wgu_scr = dram["wgu_scr"]
    wdn_scr = dram["wdn_scr"]


    if True:
        xln = C.alloc([128, NC, TB], BF16)
        hbuf = C.alloc([128, GOC, TB], BF16)
        wgt = [C.alloc([128, NC, 128], BF16) for i in range(2)]
        wut = [C.alloc([128, NC, 128], BF16) for i in range(2)]
        wdt = [C.alloc([128, GOC, 128], BF16) for i in range(2)]
        r3c = [C.alloc([128, TB], F32) for i in range(2)]
        rowm = C.alloc([1, TB], F32)
        rowe = C.alloc([1, TB], F32)
        nmurep = C.alloc([128, TB], F32)
        rstdrep = C.alloc([128, TB], F32)
        tmpf = C.alloc([128, TB], F32)
        silt = C.alloc([128, TB], F32)
        sqs = C.alloc([128, TB], BF16)
        yt = [C.alloc([128, TB], F32) for i in range(2)]

        def dma(q, out_ap, in_ap, sem, waits=()):
            P.add(q, lambda e, S, o=out_ap, i=in_ap: e.dma_start(out=o, in_=i),
                  waits=waits, incs=[(sem, 16)])
            return (sem, P.now(sem))

        build_done = list(getattr(g, "build_wait", [])) or [("sv", 0), ("sv", 0)]

        bankfree = [P.tok("sv"), P.tok("sv")]
        r3c_last = [("sv", 0), ("sv", 0)]
        wg_last = [("st", 0), ("st", 0)]
        wu_last = [("st", 0), ("st", 0)]
        wd_last = [("st", 0), ("st", 0)]
        y_last = [(s("ds0"), 0), (s("ds1"), 0)]
        for ts in range(NTB):
            sl = slice(ts * TB, (ts + 1) * TB)
            for dc in range(NC):
                par = dc % 2
                r3_ok = dma("y", r3c[par][:], r3i[:, dc, sl], s(f"drc{par}"),
                            waits=[r3c_last[par]])
                P.add("v", lambda e, S, par=par: e.tensor_tensor(
                    sqs[:], r3c[par][:], r3c[par][:], ALU.mult),
                    waits=[r3_ok, P.tok("st")], incs=[("sv", 1)])
                P.add("t", lambda e, S, par=par, dc=dc: e.matmul(
                    pA[0:1, :], onec_f[:], r3c[par][:],
                    start=(dc == 0), stop=(dc == NC - 1)),
                    waits=[P.tok("sv")] + ([bankfree[0], bankfree[1]] if dc == 0 else []))
                P.add("t", lambda e, S, dc=dc: e.matmul(
                    pB[0:1, :], onec_b[:], sqs[:],
                    start=(dc == 0), stop=(dc == NC - 1)), incs=[("st", 1)])
                r3c_last[par] = P.tok("st")
            P.add("s", lambda e, S: e.mul(rowm[0:1, :], pA[0:1, :], 1.0 / D),
                  waits=[P.tok("st")])
            P.add("s", lambda e, S: e.mul(rowe[0:1, :], pB[0:1, :], 1.0 / D),
                  incs=[("ss", 1)])
            P.add("v", lambda e, S: e.tensor_tensor(
                tmpf[0:1, 0:TB], rowm[0:1, :], rowm[0:1, :], ALU.mult),
                waits=[P.tok("ss")])
            P.add("v", lambda e, S: e.tensor_tensor(
                rowe[0:1, :], rowe[0:1, :], tmpf[0:1, 0:TB], ALU.subtract))
            P.add("v", lambda e, S: e.tensor_scalar_mul(rowm[0:1, :], rowm[0:1, :], -1.0),
                  incs=[("sv", 1)])
            P.add("s", lambda e, S: e.activation(rowe[0:1, :], rowe[0:1, :], AF.Ln,
                                                 bias=cstf[0:1, 0:1]), waits=[P.tok("sv")])
            P.add("s", lambda e, S: e.activation(rowe[0:1, :], rowe[0:1, :], AF.Exp,
                                                 scale=cstf[0:1, 1:2]), incs=[("ss", 1)])
            P.add("t", lambda e, S: e.matmul(pA[:], oner_f[:], rowm[0:1, :],
                                             start=True, stop=True),
                  waits=[P.tok("ss")], incs=[("st", 1)])
            P.add("v", lambda e, S: e.tensor_copy(nmurep[:], pA[:]),
                  waits=[P.tok("st")], incs=[("sv", 1)])
            P.add("t", lambda e, S: e.matmul(pB[:], oner_f[:], rowe[0:1, :],
                                             start=True, stop=True), incs=[("st", 1)])
            P.add("v", lambda e, S: e.tensor_copy(rstdrep[:], pB[:]),
                  waits=[P.tok("st")], incs=[("sv", 1)])
            rep_ok = P.tok("sv")
            bankfree = [rep_ok, rep_ok]
            for dc in range(NC):
                par = dc % 2
                r3_ok = dma("y", r3c[par][:], r3i[:, dc, sl], s(f"drc{par}"),
                            waits=[r3c_last[par]])
                P.add("v", lambda e, S, dc=dc, par=par: e.tensor_tensor(
                    tmpf[:], r3c[par][:], nmurep[:], ALU.add),
                    waits=[r3_ok] + ([rep_ok] if dc == 0 else []))
                P.add("v", lambda e, S, dc=dc: e.tensor_tensor(
                    xln[:, dc, :], tmpf[:], rstdrep[:], ALU.mult), incs=[("sv", 1)])
                r3c_last[par] = P.tok("sv")
            xln_ok = P.tok("sv")

            for oc in range(GOC):
                par = oc % 2
                wg_ok = dma("y", wgt[par][:], wgu_scr[0, oc], s(f"dwg{par}"),
                            waits=[wg_last[par], build_done[0], build_done[1]])
                wu_ok = dma("y", wut[par][:], wgu_scr[1, oc], s(f"dwu{par}"),
                            waits=[wu_last[par]])
                for hc in range(NC):
                    P.add("t", lambda e, S, hc=hc, par=par: e.matmul(
                        pA[:], wgt[par][:, hc, :], xln[:, hc, :],
                        start=(hc == 0), stop=(hc == NC - 1)),
                        waits=([xln_ok, wg_ok, bankfree[0]] if hc == 0 else []))
                P.add("t", None, incs=[("st", 1)])
                gmm_ok = P.tok("st")
                for hc in range(NC):
                    P.add("t", lambda e, S, hc=hc, par=par: e.matmul(
                        pB[:], wut[par][:, hc, :], xln[:, hc, :],
                        start=(hc == 0), stop=(hc == NC - 1)),
                        waits=([wu_ok, bankfree[1]] if hc == 0 else []))
                P.add("t", None, incs=[("st", 1)])
                mm_ok = P.tok("st")
                wg_last[par] = mm_ok
                wu_last[par] = mm_ok
                P.add("s", lambda e, S: e.activation(silt[:], pA[:], AF.Silu),
                      waits=[gmm_ok], incs=[("ss", 1)])
                bankfree[0] = P.tok("ss")
                P.add("v", lambda e, S, oc=oc: e.tensor_tensor(
                    hbuf[:, oc, :], pB[:], silt[:], ALU.mult),
                    waits=[P.tok("ss"), mm_ok], incs=[("sv", 1)])
                bankfree[1] = P.tok("sv")
            h_ok = P.tok("sv")

            for dco in range(NC):
                par = dco % 2
                wd_ok = dma("y", wdt[par][:], wdn_scr[dco], s(f"dwd{par}"),
                            waits=[wd_last[par]])
                r3_ok = dma("y", r3c[par][:], r3i[:, dco, sl], s(f"drc{par}"),
                            waits=[r3c_last[par]])
                pX = PS[par]
                P.add("t", None, waits=[h_ok, wd_ok, bankfree[par]])
                for ic in range(GOC):
                    P.add("t", lambda e, S, ic=ic, par=par, pX=pX: e.matmul(
                        pX[:], wdt[par][:, ic, :], hbuf[:, ic, :],
                        start=(ic == 0), stop=(ic == GOC - 1)))
                P.add("t", None, incs=[("st", 1)])
                wd_last[par] = P.tok("st")
                yb = yt[par]
                P.add("v", lambda e, S, pX=pX, par=par, yb=yb: e.tensor_tensor(
                    yb[:], pX[:], r3c[par][:], ALU.add),
                    waits=[P.tok("st"), r3_ok, y_last[par]], incs=[("sv", 1)])
                bankfree[par] = P.tok("sv")
                r3c_last[par] = P.tok("sv")
                y_last[par] = dma("y", y_d[:, dco, sl], yb[:], s(f"ds{par}"),
                                  waits=[P.tok("sv")])




def mk_lut_build(g, dram):
    P, C = g.P, g.C
    idxb = [C.alloc([128, 8, 128], I32)] * 2
    idxf = C.alloc([128, 1024], BF16)
    contrib = C.alloc([128, 1024], BF16)
    wacc = [C.alloc([128, 1024], BF16) for _ in range(2)]
    lut_s = C.alloc([1, 16], F32)
    lutrep = C.alloc([128, 16], F32)
    P.sem("b_di0", "b_di1", "b_ws0", "b_ws1", "b_dl")

    class BB:
        pass
    bb = BB()
    jobs = [(kind, ic, half) for kind in range(3) for ic in range(GOC)
            for half in range(2)]
    bb.jobs = jobs
    bb.i = 0
    bb.cast_tok = [("sv", 0), ("sv", 0)]
    bb.ws_last = [("b_ws0", 0), ("b_ws1", 0)]

    def prelude():
        P.add("g", lambda e, S: e.dma_start(out=lut_s, in_=dram["lut"]),
              incs=[("b_dl", 16)])
        P.add("t", lambda e, S: e.matmul(g.pA[:, 0:16], g.oner_f, lut_s,
                                         start=True, stop=True),
              waits=[("b_dl", 16), g.consts_ok], incs=[("st", 1)])
        P.add("v", lambda e, S: e.tensor_copy(lutrep, g.pA[:, 0:16]),
              waits=[P.tok("st")], incs=[("sv", 1)])
        bb.lutrep_ok = P.tok("sv")
    bb.prelude = prelude

    bb.op = 0

    def _emit_one():
        if bb.i >= len(jobs):
            return False
        kind, ic, half = jobs[bb.i]
        par = bb.i % 2
        wa = wacc[par]
        op = bb.op
        if op == 0:
            src_d = (dram["gidx"], dram["uidx"], dram["didx"])[kind]
            in_ap = src_d[ic][:, 8 * half:8 * half + 8, :]
            ib = idxb[0]
            P.add("g", lambda e, S, ib=ib, in_ap=in_ap: e.dma_start(
                out=ib[:], in_=in_ap),
                waits=[bb.cast_tok[0]], incs=[("b_di0", 16)])
            i_ok = ("b_di0", P.now("b_di0"))
            P.add("v", lambda e, S, ib=ib: e.tensor_copy(idxf[:], ib[:]),
                  waits=[i_ok, bb.lutrep_ok], incs=[("sv", 1)])
            bb.cast_tok[0] = P.tok("sv")
        elif op == 1:
            P.add("v", lambda e, S, wa=wa: e.tensor_scalar(
                wa[:], idxf[:], 0.0, lutrep[:, 0:1],
                ALU.is_equal, ALU.mult), waits=[bb.ws_last[par]])
        else:
            v, phase = divmod(op - 2, 2)
            v += 1
            if phase == 0:
                P.add("v", lambda e, S, v=v: e.tensor_scalar(
                    contrib[:], idxf[:], float(v), lutrep[:, v:v + 1],
                    ALU.is_equal, ALU.mult))
            else:
                P.add("v", lambda e, S, wa=wa: e.tensor_tensor(
                    wa[:], wa[:], contrib[:], ALU.add))
        bb.op += 1
        if bb.op == 32:
            P.add("v", None, incs=[("sv", 1)])
            bdone = P.tok("sv")
            if kind < 2:
                out_ap = dram["wgu_scr"][kind, ic][:, 8 * half:8 * half + 8, :]
            else:
                out_ap = dram["wdn_scr"][:, :, ic, :].transpose([1, 0, 2])[
                    :, 8 * half:8 * half + 8, :]
            P.add("y", lambda e, S, out_ap=out_ap, wa=wa: e.dma_start(
                out=out_ap, in_=wa[:].rearrange("p (c j) -> p c j", j=128)),
                waits=[bdone], incs=[(f"b_ws{par}", 16)])
            bb.ws_last[par] = (f"b_ws{par}", P.now(f"b_ws{par}"))
            bb.op = 0
            bb.i += 1
        return True

    def fill(k):
        for _ in range(k):
            if not _emit_one():
                return False
        return True
    bb.fill = fill

    def emit_job():
        return fill(32)
    bb.emit_job = emit_job
    return bb



def build_fused():
    nc = bass.Bass("TRN2", target_bir_lowering=False, debug=False)
    d = {}
    d["xin"] = nc.dram_tensor("xin", [128, NC, T], F32, kind="ExternalInput").ap()
    d["wq"] = nc.dram_tensor("wq", [NH, 128, NC, DH], F32, kind="ExternalInput").ap()
    d["wk"] = nc.dram_tensor("wk", [NH, 128, NC, DH], F32, kind="ExternalInput").ap()
    d["wv"] = nc.dram_tensor("wv", [NH, 128, NC, DH], F32, kind="ExternalInput").ap()
    d["wo"] = nc.dram_tensor("wo", [NC, 128, NH, DH], F32, kind="ExternalInput").ap()
    d["cm"] = nc.dram_tensor("cm", [128, 4, TB], F32, kind="ExternalInput").ap()
    d["h128"] = nc.dram_tensor("h128", [128, 128], F32, kind="ExternalInput").ap()
    d["su_ug"] = nc.dram_tensor("su_ug", [128, NC], F32, kind="ExternalInput").ap()
    d["su_dp"] = nc.dram_tensor("su_dp", [128, 64], F32, kind="ExternalInput").ap()
    d["wsc_up"] = nc.dram_tensor("wsc_up", [128, 64], F32, kind="ExternalInput").ap()
    d["wsc_gt"] = nc.dram_tensor("wsc_gt", [128, 64], F32, kind="ExternalInput").ap()
    d["wsc_dn"] = nc.dram_tensor("wsc_dn", [128, NC], F32, kind="ExternalInput").ap()
    d["wup"] = nc.dram_tensor("wup", [64, 128, NC, 128], F32, kind="ExternalInput").ap()
    d["wgt"] = nc.dram_tensor("wgt", [64, 128, NC, 128], F32, kind="ExternalInput").ap()
    d["wdn"] = nc.dram_tensor("wdn", [NC, 128, 64, 128], F32, kind="ExternalInput").ap()
    d["win"] = nc.dram_tensor("win", [64, 128, NC, 128], F32, kind="ExternalInput").ap()
    d["convw"] = nc.dram_tensor("convw", [128, DIC, 4], F32, kind="ExternalInput").ap()
    d["convb"] = nc.dram_tensor("convb", [128, DIC], F32, kind="ExternalInput").ap()
    d["wxdt"] = nc.dram_tensor("wxdt", [128, DIC, 128], F32, kind="ExternalInput").ap()
    d["wxB"] = nc.dram_tensor("wxB", [128, DIC, 16], F32, kind="ExternalInput").ap()
    d["wxC"] = nc.dram_tensor("wxC", [128, DIC, 16], F32, kind="ExternalInput").ap()
    d["wdt"] = nc.dram_tensor("wdt", [128, DIC, 128], F32, kind="ExternalInput").ap()
    d["dtb"] = nc.dram_tensor("dtb", [128, DIC], F32, kind="ExternalInput").ap()
    d["alog"] = nc.dram_tensor("alog", [128, DIC, 16], F32, kind="ExternalInput").ap()
    d["dvec"] = nc.dram_tensor("dvec", [128, DIC], F32, kind="ExternalInput").ap()
    d["wout"] = nc.dram_tensor("wout", [NC, 128, DIC, 128], F32, kind="ExternalInput").ap()
    d["e16"] = nc.dram_tensor("e16", [16, 16 * 128], F32, kind="ExternalInput").ap()
    d["gidx"] = nc.dram_tensor("gidx", [GOC, 128, NC, 128], I32, kind="ExternalInput").ap()
    d["uidx"] = nc.dram_tensor("uidx", [GOC, 128, NC, 128], I32, kind="ExternalInput").ap()
    d["didx"] = nc.dram_tensor("didx", [GOC, 128, NC, 128], I32, kind="ExternalInput").ap()
    d["lut"] = nc.dram_tensor("lut", [1, 16], F32, kind="ExternalInput").ap()
    d["yout"] = nc.dram_tensor("yout", [128, NC, T], F32, kind="ExternalOutput").ap()
    d["zscr"] = nc.dram_tensor("zscr", [NH, 128, T], BF16, kind="Internal").ap()
    d["r1"] = nc.dram_tensor("r1", [128, NC, T], F32, kind="Internal").ap()
    d["r2"] = nc.dram_tensor("r2", [128, NC, T], F32, kind="Internal").ap()
    d["r3"] = nc.dram_tensor("r3", [128, NC, T], F32, kind="Internal").ap()
    d["r1i"] = d["r1"]
    d["r2i"] = d["r2"]
    d["r3i"] = d["r3"]
    d["wgu_scr"] = nc.dram_tensor("wgu_scr", [2, GOC, 128, NC, 128], BF16, kind="Internal").ap()
    d["wdn_scr"] = nc.dram_tensor("wdn_scr", [NC, 128, GOC, 128], BF16, kind="Internal").ap()

    g = GCtx()
    g.nc = nc
    P = g.P = Plan()
    P.sem("st", "sv", "ss", "sg")
    prefix_holder = {"px": ""}

    def s(nm):
        if nm in ("st", "sv", "ss", "sg"):
            return nm
        full = prefix_holder["px"] + nm
        P.sem(full)
        return full
    g.s = s

    with contextlib.ExitStack() as ctx:
        E = ctx.enter_context
        arena_t = E(nc.sbuf_tensor("arena", [128, ARENA_BYTES], mybir.dt.int8))
        g.C = Carver(arena_t[:])
        g.onec_b = E(nc.sbuf_tensor("g_onec_b", [128, 1], BF16))[:]
        g.onec_f = E(nc.sbuf_tensor("g_onec_f", [128, 1], F32))[:]
        g.oner_b = E(nc.sbuf_tensor("g_oner_b", [1, 128], BF16))[:]
        g.oner_f = E(nc.sbuf_tensor("g_oner_f", [1, 128], F32))[:]
        g.cstf = E(nc.sbuf_tensor("g_cstf", [128, 8], F32))[:]
        g.h128p = E(nc.sbuf_tensor("g_h128p", [128, 128], BF16))[:]
        g.h128n = E(nc.sbuf_tensor("g_h128n", [128, 128], BF16))[:]
        g.pA = E(nc.psum_tensor("pA", [128, TB], F32))
        g.pB = E(nc.psum_tensor("pB", [128, TB], F32))

        P.add("g", lambda e, S: e.memset(g.onec_b, 1.0))
        P.add("g", lambda e, S: e.memset(g.onec_f, 1.0))
        P.add("g", lambda e, S: e.memset(g.oner_b, 1.0))
        P.add("g", lambda e, S: e.memset(g.oner_f, 1.0))
        for slot, val in ((0, LN_EPS), (1, -0.5), (2, -1.0), (3, 1.0),
                          (4, float(2.0 ** -6)), (5, 1.0 / 128.0), (6, 0.5)):
            P.add("g", lambda e, S, slot=slot, val=val: e.memset(
                g.cstf[:, slot:slot + 1], val))
        P.add("g", lambda e, S: e.dma_start(out=g.h128p, in_=d["h128"]),
              incs=[("sg_h", 16)] if P.sem("sg_h") or True else [])
        P.add("v", lambda e, S: e.tensor_scalar_mul(g.h128n, g.h128p, -1.0),
              waits=[("sg_h", 16)], incs=[("sv", 1)])
        P.add("g", lambda e, S: e.nop())
        P.add("g", None, incs=[("sg", 1)])
        g.consts_ok = P.tok("sg")

        bb = mk_lut_build(g, d)
        g.C.floor = g.C.off
        bb.prelude()
        g.fill = bb.fill
        prefix_holder["px"] = "a_"
        emit_l0attn(g, d, after=[P.tok("sv")])
        P.barrier(("t", "v", "s"))
        after = [P.tok("a_ds0"), P.tok("a_ds1")]
        g.C.reset()
        prefix_holder["px"] = "f_"
        emit_l0ffn(g, d, after=after)
        def drain_build():
            while bb.emit_job():
                pass
            g.build_wait = [bb.ws_last[0], bb.ws_last[1]]
            g.fill = None
        g.drain_build = drain_build
        P.barrier(("t", "v", "s"))
        after = [P.tok("f_ds0"), P.tok("f_ds1")]
        g.C.floor = 0
        g.C.reset()
        prefix_holder["px"] = "m_"
        emit_l1attn(g, d, after=after)
        P.barrier(("t", "v", "s"))
        after = [P.tok("m_ds0"), P.tok("m_ds1")]
        g.C.reset()
        prefix_holder["px"] = "g_"
        emit_l1ffn(g, d, after=after)
        P.add("y", None, waits=[P.tok("g_ds0"), P.tok("g_ds1")])

        sems = {n: E(nc.semaphore(n)) for n in P.names}
        P.emit(nc, sems)
    return nc


def kernel_fused(**inputs):
    I = {k: np.asarray(v) for k, v in inputs.items()}
    x = I["L0_attn_x"]
    w = {}
    w.update(prep_l0attn_host(x[0], I["L0_attn_W_K"], I["L0_attn_W_Q"],
                              I["L0_attn_W_V"], I["L0_attn_W_O"]))
    del w["xin"]
    w.update(prep_l0ffn_host(I["L0_ffn_SU_ug"], I["L0_ffn_SU_dp"],
                             I["L0_ffn_Wscale_ug"], I["L0_ffn_Wscale_dp"],
                             I["L0_ffn_W_up"], I["L0_ffn_W_gate"], I["L0_ffn_W_down"]))
    w.update(prep_l1attn_host(I["L1_attn_W_in"], I["L1_attn_conv_w"],
                              I["L1_attn_conv_b"], I["L1_attn_W_x"],
                              I["L1_attn_W_dt"], I["L1_attn_dt_bias"],
                              I["L1_attn_A_log"], I["L1_attn_D"], I["L1_attn_W_out"]))
    w.update(prep_l1ffn_host(I["L1_ffn_lut"], I["L1_ffn_gate_idx"],
                             I["L1_ffn_up_idx"], I["L1_ffn_down_idx"]))
    in_maps = []
    for b in range(NCORES):
        hp = dict(w)
        hp["xin"] = pack_fm(x[b])
        in_maps.append(hp)
    nc = build_fused()
    res = run_bass_kernel_spmd(nc, in_maps, core_ids=list(range(NCORES)))
    out = np.empty((NCORES, T, D), np.float32)
    for b in range(NCORES):
        out[b] = unpack_fm(res.results[b]["yout"])
    return out


# version 13 (final)
# speedup vs baseline: 1.2694x; 1.0114x over previous
import contextlib
import numpy as np
import ml_dtypes
import concourse.bass as bass
import concourse.mybir as mybir
from concourse.bass_utils import run_bass_kernel_spmd

F32 = mybir.dt.float32
BF16 = mybir.dt.bfloat16
I32 = mybir.dt.int32
AF = mybir.ActivationFunctionType
ALU = mybir.AluOpType
AX = mybir.AxisListType

NCORES = 8
D = 2048
T = 2048
NH = 16
DH = 128
NC = 16
TB = 512
NTB = T // TB

ISQRT_DH = float(1.0 / np.sqrt(128.0))
MASKNEG = -30000.0
LN_EPS = 1e-5

ENGS = ("y", "g", "t", "v", "s")
ENGMAP = {"y": "sync", "g": "gpsimd", "t": "tensor", "v": "vector", "s": "scalar"}


class Plan:

    def __init__(self):
        self.ops = {e: [] for e in ENGS}
        self.cnt = {}
        self.names = []
        self.pending = {e: [] for e in ENGS}

    def sem(self, *names):
        for n in names:
            if n not in self.cnt:
                self.cnt[n] = 0
                self.names.append(n)

    def add(self, eng, fn, waits=(), incs=()):
        w = self.pending[eng] + [(nm, v) for nm, v in waits if v > 0]
        self.pending[eng] = []
        wd = {}
        for nm, v in w:
            wd[nm] = max(wd.get(nm, 0), v)
        for nm, amt in incs:
            self.cnt[nm] += amt
        if fn is None and incs and not wd:
            for j in range(len(self.ops[eng]) - 1, -1, -1):
                if self.ops[eng][j][0] is not None:
                    if not self.ops[eng][j][2]:
                        self.ops[eng][j][2].extend(incs)
                    else:
                        self.ops[eng].append(
                            (lambda e, S: e.drain(), [], list(incs)))
                    return
        self.ops[eng].append((fn, sorted(wd.items()), list(incs)))

    def now(self, nm):
        return self.cnt[nm]

    def tok(self, nm):
        return (nm, self.cnt[nm])

    def barrier(self, engs=("t", "v", "s")):
        vals = {}
        for e in engs:
            nm = "s" + e
            self.add(e, None, incs=[(nm, 1)])
            vals[nm] = self.cnt[nm]
        wl = list(vals.items())
        for e in ENGS:
            self.pending[e] = self.pending[e] + wl
        return wl

    def emit(self, nc, sems):
        with nc.Block() as block:
            for ename in ENGS:
                ops = self.ops[ename]
                if not ops:
                    continue

                def body(e, ops=ops):
                    for fn, waits, incs in ops:
                        for nm, val in waits:
                            e.wait_ge(sems[nm], val)
                        inst = fn(e, sems) if fn is not None else None
                        if incs:
                            if inst is None:
                                inst = e.nop()
                            for nm, amt in incs:
                                inst = inst.then_inc(sems[nm], amt)

                getattr(block, ENGMAP[ename])(body)


def _bf(x):
    return np.ascontiguousarray(x).astype(ml_dtypes.bfloat16)



def prep_l0attn_host(x_b, W_K, W_Q, W_V, W_O):
    xp = np.ascontiguousarray(np.asarray(x_b, np.float32).T
                              .reshape(NC, 128, T).transpose(1, 0, 2))

    def wqkv(W):
        return np.ascontiguousarray(
            np.asarray(W, np.float32).reshape(NH, DH, NC, 128).transpose(0, 3, 2, 1))

    wo = np.ascontiguousarray(
        np.asarray(W_O, np.float32).reshape(NC, 128, NH, DH).transpose(0, 3, 2, 1))
    p = np.arange(128)[:, None, None]
    d_ = np.arange(4)[None, :, None]
    j2 = np.arange(TB)[None, None, :]
    cm = np.where(d_ * 128 + p <= j2, 0.0, MASKNEG).astype(np.float32)
    return dict(xin=xp, wq=wqkv(W_Q), wk=wqkv(W_K), wv=wqkv(W_V), wo=wo, cm=cm)


def pack_fm(x_b):
    return np.ascontiguousarray(np.asarray(x_b, np.float32).T
                                .reshape(NC, 128, T).transpose(1, 0, 2))


def unpack_fm(r):
    return np.ascontiguousarray(
        np.asarray(r).transpose(1, 0, 2).reshape(D, T).T)



def build_l0attn():
    nc = bass.Bass("TRN2", target_bir_lowering=False, debug=False)
    xin = nc.dram_tensor("xin", [128, NC, T], F32, kind="ExternalInput").ap()
    wq_d = nc.dram_tensor("wq", [NH, 128, NC, DH], F32, kind="ExternalInput").ap()
    wk_d = nc.dram_tensor("wk", [NH, 128, NC, DH], F32, kind="ExternalInput").ap()
    wv_d = nc.dram_tensor("wv", [NH, 128, NC, DH], F32, kind="ExternalInput").ap()
    wo_d = nc.dram_tensor("wo", [NC, 128, NH, DH], F32, kind="ExternalInput").ap()
    cm_d = nc.dram_tensor("cm", [128, 4, TB], F32, kind="ExternalInput").ap()
    r1_d = nc.dram_tensor("r1", [128, NC, T], F32, kind="ExternalOutput").ap()
    zscr_d = nc.dram_tensor("zscr", [NH, 128, T], BF16, kind="Internal").ap()

    P = Plan()
    P.sem("dg", "dy", "st", "sv", "ss", "sg",
          "dr", "dw0", "dw1", "dzs0", "dzs1", "dzq", "dwo0", "dwo1", "da0", "da1", "ds0", "ds1", "dcm")

    with contextlib.ExitStack() as ctx:
        E = ctx.enter_context
        rb = E(nc.sbuf_tensor("rb", [128, NC, T], BF16))
        q_s = E(nc.sbuf_tensor("q_s", [128, T], BF16))
        k_s = E(nc.sbuf_tensor("k_s", [128, T], BF16))
        vt_s = E(nc.sbuf_tensor("vt_s", [128, NC, 128], BF16))
        attnT = E(nc.sbuf_tensor("attnT", [128, NC, TB], BF16))
        wqb = [E(nc.sbuf_tensor(f"wqb{i}", [128, NC, DH], BF16)) for i in range(2)]
        wkb = [E(nc.sbuf_tensor(f"wkb{i}", [128, NC, DH], BF16)) for i in range(2)]
        wvb = [E(nc.sbuf_tensor(f"wvb{i}", [128, NC, DH], BF16)) for i in range(2)]
        wob = [E(nc.sbuf_tensor(f"wob{i}", [128, NH, DH], BF16)) for i in range(2)]
        zqb = [E(nc.sbuf_tensor("zqb0", [128, NH, TB], BF16))]
        cm_s = E(nc.sbuf_tensor("cm_s", [128, 4, TB], F32))
        rrep = E(nc.sbuf_tensor("rrep", [128, T], F32))
        nmu_bf = E(nc.sbuf_tensor("nmu_bf", [1, T], BF16))
        mhat_bf = E(nc.sbuf_tensor("mhat_bf", [1, T], BF16))
        lrec_bf = E(nc.sbuf_tensor("lrec_bf", [1, TB], BF16))
        rowa = E(nc.sbuf_tensor("rowa", [1, T], F32))
        rowb = E(nc.sbuf_tensor("rowb", [1, T], F32))
        kmax_s = E(nc.sbuf_tensor("kmax_s", [1, 8], F32))
        rstd_cols = E(nc.sbuf_tensor("rstd_cols", [128, NC], F32))
        wbar = E(nc.sbuf_tensor("wbar", [1, 384], BF16))
        scf = [E(nc.sbuf_tensor(f"scf{i}", [128, TB], F32)) for i in range(2)]
        rep_sb = E(nc.sbuf_tensor("rep_sb", [128, TB], F32))
        mrep = E(nc.sbuf_tensor("mrep", [128, TB], F32))
        zbs = [E(nc.sbuf_tensor(f"zbs{i}", [128, TB], BF16)) for i in range(2)]
        r0t = [E(nc.sbuf_tensor(f"r0t{i}", [128, TB], F32)) for i in range(2)]
        r1t = [E(nc.sbuf_tensor(f"r1t{i}", [128, TB], F32)) for i in range(2)]
        cstf = E(nc.sbuf_tensor("cstf", [128, 8], F32))
        onec_b = E(nc.sbuf_tensor("onec_b", [128, 1], BF16))
        onec_f = E(nc.sbuf_tensor("onec_f", [128, 1], F32))
        oner_b = E(nc.sbuf_tensor("oner_b", [1, 128], BF16))
        oner_f = E(nc.sbuf_tensor("oner_f", [1, 128], F32))
        pA = E(nc.psum_tensor("pA", [128, TB], F32))
        pB = E(nc.psum_tensor("pB", [128, TB], F32))
        PS = [pA, pB]
        sems = {n: E(nc.semaphore(n)) for n in P.names}

        def dma(q, out_ap, in_ap, waits=(), sem="dg"):
            P.add(q, lambda e, S, o=out_ap, i=in_ap: e.dma_start(out=o, in_=i),
                  waits=waits, incs=[(sem, 16)])
            return (sem, P.now(sem))

        for i in range(4):
            rb_ok = dma("g", rb[:, 4 * i:4 * (i + 1), :], xin[:, 4 * i:4 * (i + 1), :], sem="dr")
        cm_ok = dma("y", cm_s[:], cm_d[:], sem="dcm")
        P.add("g", lambda e, S: e.memset(onec_b[:], 1.0))
        P.add("g", lambda e, S: e.memset(onec_f[:], 1.0))
        P.add("g", lambda e, S: e.memset(oner_b[:], 1.0))
        P.add("g", lambda e, S: e.memset(oner_f[:], 1.0))
        for j, val in enumerate([LN_EPS, -0.5, 1.0 / 128.0, 0.5]):
            P.add("g", lambda e, S, j=j, val=val: e.memset(cstf[:, j:j + 1], val),
                  incs=([("sg", 1)] if j == 3 else []))
        consts_ok = P.tok("sg")
        wld = {0: [dma("g", wqb[0][:], wq_d[0], sem="dw0"),
                   dma("g", wkb[0][:], wk_d[0], sem="dw0"),
                   dma("g", wvb[0][:], wv_d[0], sem="dw0")]}

        for tb in range(NTB):
            sl = slice(tb * TB, (tb + 1) * TB)
            for dc in range(NC):
                P.add("v", lambda e, S, dc=dc, sl=sl: e.tensor_tensor(
                    attnT[:, dc, :], rb[:, dc, sl], rb[:, dc, sl], ALU.mult),
                    waits=[rb_ok, P.tok("st")] if dc == 0 else [])
            P.add("v", None, incs=[("sv", 1)])
            sq_ok = P.tok("sv")
            for dc in range(NC):
                P.add("t", lambda e, S, dc=dc, sl=sl: e.matmul(
                    pA[0:1, :], onec_b[:], rb[:, dc, sl],
                    start=(dc == 0), stop=(dc == NC - 1)),
                    waits=[rb_ok, consts_ok, P.tok("ss")] if dc == 0 else [])
            for dc in range(NC):
                P.add("t", lambda e, S, dc=dc: e.matmul(
                    pB[0:1, :], onec_b[:], attnT[:, dc, :],
                    start=(dc == 0), stop=(dc == NC - 1)),
                    waits=[sq_ok] if dc == 0 else [])
            P.add("t", None, incs=[("st", 1)])
            mm_ok = P.tok("st")
            P.add("s", lambda e, S, sl=sl: e.mul(rowa[0:1, sl], pA[0:1, :], 1.0 / D),
                  waits=[mm_ok])
            P.add("s", lambda e, S, sl=sl: e.mul(rowb[0:1, sl], pB[0:1, :], 1.0 / D),
                  incs=[("ss", 1)])
        P.barrier(("t", "v", "s"))
        P.add("v", lambda e, S: e.tensor_scalar_mul(nmu_bf[0:1, :], rowa[0:1, :], -1.0))
        P.add("v", lambda e, S: e.tensor_tensor(
            rowa[0:1, :], rowa[0:1, :], rowa[0:1, :], ALU.mult))
        P.add("v", lambda e, S: e.tensor_tensor(
            rowb[0:1, :], rowb[0:1, :], rowa[0:1, :], ALU.subtract), incs=[("sv", 1)])
        P.add("s", lambda e, S: e.activation(rowa[0:1, :], rowb[0:1, :], AF.Ln,
                                             bias=cstf[0:1, 0:1]),
              waits=[P.tok("sv"), consts_ok])
        P.add("s", lambda e, S: e.activation(rowb[0:1, :], rowa[0:1, :], AF.Exp,
                                             scale=cstf[0:1, 1:2]),
              incs=[("ss", 1)])
        rstd_ok = P.tok("ss")
        for tb in range(NTB):
            sl = slice(tb * TB, (tb + 1) * TB)
            pX = PS[tb % 2]
            P.add("t", lambda e, S, sl=sl, pX=pX: e.matmul(
                pX[:], oner_f[:], rowb[0:1, sl], start=True, stop=True),
                waits=[rstd_ok], incs=[("st", 1)])
            P.add("v", lambda e, S, sl=sl, pX=pX: e.tensor_copy(rrep[:, sl], pX[:]),
                  waits=[P.tok("st")], incs=[("sv", 1)])
        P.add("t", None, waits=[P.tok("sv")])
        for tc in range(NC):
            P.add("t", lambda e, S, tc=tc: e.matmul(
                pA[:, tc:tc + 1], rowb[0:1, tc * 128:(tc + 1) * 128], oner_f[0:1, 0:1],
                start=True, stop=True))
        P.add("t", None, incs=[("st", 1)])
        P.add("v", lambda e, S: e.tensor_copy(rstd_cols[:], pA[:, 0:NC]),
              waits=[P.tok("st")], incs=[("sv", 1)])
        P.barrier(("t", "v", "s"))

        bankfree = [P.tok("sv"), P.tok("sv")]
        attnT_free = P.tok("st")
        zb_stored = [("dzs0", 0), ("dzs1", 0)]

        w_consumed = {}
        for h in range(NH):
            buf = h % 2
            wq, wk, wv = wqb[buf], wkb[buf], wvb[buf]
            if h > 0:
                wwait = [w_consumed[h - 2]] if h >= 2 else []
                wld[h] = [dma("g", wq[:], wq_d[h], waits=wwait, sem=f"dw{buf}"),
                          dma("g", wk[:], wk_d[h], sem=f"dw{buf}"),
                          dma("g", wv[:], wv_d[h], sem=f"dw{buf}")]
            w_ok = wld[h][-1]

            P.add("t", None, waits=[w_ok, bankfree[0]])
            for j, wt in enumerate((wq, wk, wv)):
                for dc in range(NC):
                    P.add("t", lambda e, S, j=j, wt=wt, dc=dc: e.matmul(
                        pA[0:1, j * 128:(j + 1) * 128], onec_b[:], wt[:, dc, :],
                        start=(dc == 0), stop=(dc == NC - 1)))
            P.add("t", None, incs=[("st", 1)])
            P.add("s", lambda e, S: e.copy(wbar[0:1, :], pA[0:1, 0:384]),
                  waits=[P.tok("st")], incs=[("ss", 1)])
            bankfree[0] = P.tok("ss")
            wbar_ok = P.tok("ss")

            for wb_i, wt, dst in ((0, wq, q_s), (1, wk, k_s)):
                for tb in range(NTB):
                    sl = slice(tb * TB, (tb + 1) * TB)
                    bk = tb % 2
                    pX = PS[bk]
                    P.add("t", None, waits=[bankfree[bk], wbar_ok])
                    for dc in range(NC):
                        P.add("t", lambda e, S, dc=dc, sl=sl, wt=wt, pX=pX: e.matmul(
                            pX[:], wt[:, dc, :], rb[:, dc, sl],
                            start=(dc == 0), stop=False))
                    P.add("t", lambda e, S, sl=sl, pX=pX, wb_i=wb_i: e.matmul(
                        pX[:], wbar[0:1, wb_i * 128:(wb_i + 1) * 128], nmu_bf[0:1, sl],
                        start=False, stop=True), incs=[("st", 1)])
                    P.add("v", lambda e, S, sl=sl, pX=pX, dst=dst: e.tensor_tensor(
                        dst[:, sl], pX[:], rrep[:, sl], ALU.mult),
                        waits=[P.tok("st")], incs=[("sv", 1)])
                    bankfree[bk] = P.tok("sv")

            for tc in range(NC):
                csl = slice(tc * 128, (tc + 1) * 128)
                bk = tc % 2
                pX = PS[bk]
                P.add("t", None, waits=[bankfree[bk]])
                for dc in range(NC):
                    P.add("t", lambda e, S, dc=dc, csl=csl, pX=pX, wv=wv: e.matmul(
                        pX[:, 0:128], rb[:, dc, csl], wv[:, dc, :],
                        start=(dc == 0), stop=False))
                P.add("t", lambda e, S, csl=csl, pX=pX: e.matmul(
                    pX[:, 0:128], nmu_bf[0:1, csl], wbar[0:1, 256:384],
                    start=False, stop=True), incs=[("st", 1)])
                P.add("v", lambda e, S, tc=tc, pX=pX: e.tensor_scalar(
                    vt_s[:, tc, :], pX[:, 0:128], rstd_cols[:, tc:tc + 1], None, ALU.mult),
                    waits=[P.tok("st")], incs=[("sv", 1)])
                bankfree[bk] = P.tok("sv")
            w_consumed[h] = P.tok("st")

            for src, row in ((q_s, rowa), (k_s, rowb)):
                for tb in range(NTB):
                    sl = slice(tb * TB, (tb + 1) * TB)
                    bk = tb % 2
                    pX = PS[bk]
                    P.add("v", lambda e, S, sl=sl, src=src: e.tensor_tensor(
                        scf[0][:], src[:, sl], src[:, sl], ALU.mult), incs=[("sv", 1)])
                    P.add("t", lambda e, S, pX=pX: e.matmul(
                        pX[0:1, :], onec_f[:], scf[0][:], start=True, stop=True),
                        waits=[P.tok("sv"), bankfree[bk]], incs=[("st", 1)])
                    P.add("s", lambda e, S, sl=sl, pX=pX, row=row: e.copy(
                        row[0:1, sl], pX[0:1, :]),
                        waits=[P.tok("st")], incs=[("ss", 1)])
                    bankfree[bk] = P.tok("ss")
                    P.add("v", None, waits=[P.tok("st")])
            P.barrier(("t", "v", "s"))
            P.add("v", lambda e, S: e.reduce_max(
                kmax_s[0:1, 0:1], rowb[0:1, :], axis=AX.X), incs=[("sv", 1)])
            P.add("s", lambda e, S: e.activation(
                kmax_s[0:1, 1:2], kmax_s[0:1, 0:1], AF.Ln, scale=cstf[0:1, 2:3]),
                waits=[P.tok("sv")])
            P.add("s", lambda e, S: e.activation(
                kmax_s[0:1, 2:3], kmax_s[0:1, 1:2], AF.Exp, scale=cstf[0:1, 3:4]))
            P.add("s", lambda e, S: e.activation(rowa[0:1, :], rowa[0:1, :], AF.Ln))
            P.add("s", lambda e, S: e.activation(
                rowa[0:1, :], rowa[0:1, :], AF.Exp, scale=cstf[0:1, 3:4]),
                incs=[("ss", 1)])
            P.add("v", lambda e, S: e.tensor_scalar(
                mhat_bf[0:1, :], rowa[0:1, :], kmax_s[0:1, 2:3], 0.05, ALU.mult, ALU.add),
                waits=[P.tok("ss")], incs=[("sv", 1)])
            mhat_ok = P.tok("sv")

            for qb in range(NTB):
                sl = slice(qb * TB, (qb + 1) * TB)
                nkc = 4 * qb + 4
                P.add("t", lambda e, S, sl=sl: e.matmul(
                    pB[:], oner_b[:], mhat_bf[0:1, sl], start=True, stop=True),
                    waits=[mhat_ok, bankfree[1]], incs=[("st", 1)])
                P.add("s", lambda e, S: e.copy(mrep[:], pB[:]),
                      waits=[P.tok("st")], incs=[("ss", 1)])
                bankfree[1] = P.tok("ss")
                mrep_ok = P.tok("ss")
                scf_free = [P.tok("ss"), P.tok("ss")]
                for kc in range(nkc):
                    bk = kc % 2
                    pX = PS[bk]
                    P.add("t", lambda e, S, kc=kc, sl=sl, pX=pX: e.matmul(
                        pX[:], k_s[:, kc * 128:(kc + 1) * 128], q_s[:, sl],
                        start=True, stop=True),
                        waits=[bankfree[bk]], incs=[("st", 1)])
                    mm_tok = P.tok("st")
                    sb = scf[kc % 2]
                    P.add("v", lambda e, S, pX=pX, sb=sb: e.scalar_tensor_tensor(
                        sb[:], pX[:], ISQRT_DH, mrep[:], ALU.mult, ALU.subtract),
                        waits=[mm_tok, mrep_ok, scf_free[kc % 2]], incs=[("sv", 1)])
                    bankfree[bk] = P.tok("sv")
                    dd = kc - 4 * qb
                    if dd >= 0:
                        P.add("v", lambda e, S, sb=sb, dd=dd: e.tensor_tensor(
                            sb[:], sb[:], cm_s[:, dd, :], ALU.add),
                            waits=[cm_ok], incs=[("sv", 1)])
                    P.add("s", lambda e, S, kc=kc, sb=sb: e.activation(
                        attnT[:, kc, :], sb[:], AF.Exp),
                        waits=[P.tok("sv"), attnT_free], incs=[("ss", 1)])
                    scf_free[kc % 2] = P.tok("ss")
                exp_ok = P.tok("ss")
                P.add("t", None, waits=[exp_ok, bankfree[0], bankfree[1]])
                for kc in range(nkc):
                    P.add("t", lambda e, S, kc=kc, nkc=nkc: e.matmul(
                        pA[:], vt_s[:, kc, :], attnT[:, kc, :],
                        start=(kc == 0), stop=(kc == nkc - 1)))
                for kc in range(nkc):
                    P.add("t", lambda e, S, kc=kc, nkc=nkc: e.matmul(
                        pB[0:1, :], onec_b[:], attnT[:, kc, :],
                        start=(kc == 0), stop=(kc == nkc - 1)))
                P.add("t", None, incs=[("st", 1)])
                zl_ok = P.tok("st")
                attnT_free = zl_ok
                P.add("v", lambda e, S: e.reciprocal(rowb[0:1, 0:TB], pB[0:1, :]),
                      waits=[zl_ok])
                P.add("v", lambda e, S: e.tensor_copy(lrec_bf[0:1, :], rowb[0:1, 0:TB]),
                      incs=[("sv", 1)])
                P.add("t", lambda e, S: e.matmul(
                    pB[:], oner_b[:], lrec_bf[0:1, :], start=True, stop=True),
                    waits=[P.tok("sv")], incs=[("st", 1)])
                P.add("s", lambda e, S: e.copy(rep_sb[:], pB[:]),
                      waits=[P.tok("st")], incs=[("ss", 1)])
                bankfree[1] = P.tok("ss")
                zb = zbs[qb % 2]
                P.add("v", lambda e, S, zb=zb: e.tensor_tensor(
                    zb[:], pA[:], rep_sb[:], ALU.mult),
                    waits=[P.tok("ss"), zb_stored[qb % 2]], incs=[("sv", 1)])
                bankfree[0] = P.tok("sv")
                zb_stored[qb % 2] = dma("y", zscr_d[h][:, sl], zb[:],
                                        waits=[P.tok("sv")], sem=f"dzs{qb % 2}")
            P.barrier(("t", "v", "s"))

        zq_used = {}
        wo_last = [("st", 0), ("st", 0)]
        r0_last = [("sv", 0), ("sv", 0)]
        r1_last = [("ds0", 0), ("ds1", 0)]
        for qb in range(NTB):
            sl = slice(qb * TB, (qb + 1) * TB)
            zbuf = zqb[0]
            zsrc = zscr_d[:, :, sl].rearrange("h p t -> p h t")
            zw = [dma("g", zbuf[:, 4 * i:4 * (i + 1), :], zsrc[:, 4 * i:4 * (i + 1), :],
                      waits=[P.tok("dzs0"), P.tok("dzs1")] + ([zq_used[qb - 1]] if qb >= 1 else []),
                      sem="dzq")
                  for i in range(4)]
            z_ok = zw[-1]
            for dc in range(NC):
                par = dc % 2
                wbuf = wob[par]
                wo_ok = dma("g", wbuf[:], wo_d[dc], waits=[wo_last[par]], sem=f"dwo{par}")
                r0buf = r0t[par]
                r0_ok = dma("y", r0buf[:], xin[:, dc, sl], waits=[r0_last[par]], sem=f"da{par}")
                pX = PS[par]
                P.add("t", None, waits=[z_ok, wo_ok, bankfree[par]])
                for hh in range(NH):
                    P.add("t", lambda e, S, hh=hh, wbuf=wbuf, zbuf=zbuf, pX=pX: e.matmul(
                        pX[:], wbuf[:, hh, :], zbuf[:, hh, :],
                        start=(hh == 0), stop=(hh == NH - 1)))
                P.add("t", None, incs=[("st", 1)])
                wo_last[par] = P.tok("st")
                r1b = r1t[par]
                P.add("v", lambda e, S, pX=pX, r0buf=r0buf, r1b=r1b: e.tensor_tensor(
                    r1b[:], pX[:], r0buf[:], ALU.add),
                    waits=[P.tok("st"), r0_ok, r1_last[par]], incs=[("sv", 1)])
                bankfree[par] = P.tok("sv")
                r0_last[par] = P.tok("sv")
                r1_last[par] = dma("y", r1_d[:, dc, sl], r1b[:], waits=[P.tok("sv")], sem=f"ds{par}")
            zq_used[qb] = P.tok("st")

        P.add("y", None, waits=[P.tok(s) for s in
              ("dr", "dw0", "dw1", "dzs0", "dzs1", "dzq", "dwo0", "dwo1", "da0", "da1", "ds0", "ds1", "dcm")])
        P.emit(nc, sems)
    return nc




def _had128():
    i = np.arange(128)
    pc = np.zeros((128, 128), np.int64)
    H = np.ones((128, 128), np.float32)
    for b in range(7):
        bi = ((i >> b) & 1)
        H *= np.where(np.outer(bi, bi) == 1, -1.0, 1.0)
    return H


def _h16sign(a, b):
    return (-1.0) ** bin(a & b).count("1")


def prep_l0ffn_host(SU_ug, SU_dp, Wscale_ug, Wscale_dp, W_up, W_gate, W_down):
    h128 = _had128()
    su_ug = np.ascontiguousarray(np.asarray(SU_ug, np.float32).reshape(NC, 128).T)
    su_dp = np.ascontiguousarray(np.asarray(SU_dp, np.float32).reshape(64, 128).T)
    wsc_up = np.ascontiguousarray(np.asarray(Wscale_ug[:8192], np.float32).reshape(64, 128).T)
    wsc_gt = np.ascontiguousarray(np.asarray(Wscale_ug[8192:], np.float32).reshape(64, 128).T)
    wsc_dn = np.ascontiguousarray(np.asarray(Wscale_dp, np.float32).reshape(NC, 128).T)
    wup = np.ascontiguousarray(np.asarray(W_up, np.float32)
                               .reshape(64, 128, NC, 128).transpose(0, 3, 2, 1))
    wgt = np.ascontiguousarray(np.asarray(W_gate, np.float32)
                               .reshape(64, 128, NC, 128).transpose(0, 3, 2, 1))
    wdn = np.ascontiguousarray(np.asarray(W_down, np.float32)
                               .reshape(NC, 128, 64, 128).transpose(0, 3, 2, 1))
    return dict(h128=h128, su_ug=su_ug, su_dp=su_dp, wsc_up=wsc_up, wsc_gt=wsc_gt,
                wsc_dn=wsc_dn, wup=wup, wgt=wgt, wdn=wdn)


def build_l0ffn():
    nc = bass.Bass("TRN2", target_bir_lowering=False, debug=False)
    r1i = nc.dram_tensor("r1i", [128, NC, T], F32, kind="ExternalInput").ap()
    h128_d = nc.dram_tensor("h128", [128, 128], F32, kind="ExternalInput").ap()
    suug_d = nc.dram_tensor("su_ug", [128, NC], F32, kind="ExternalInput").ap()
    sudp_d = nc.dram_tensor("su_dp", [128, 64], F32, kind="ExternalInput").ap()
    wscu_d = nc.dram_tensor("wsc_up", [128, 64], F32, kind="ExternalInput").ap()
    wscg_d = nc.dram_tensor("wsc_gt", [128, 64], F32, kind="ExternalInput").ap()
    wscd_d = nc.dram_tensor("wsc_dn", [128, NC], F32, kind="ExternalInput").ap()
    wup_d = nc.dram_tensor("wup", [64, 128, NC, 128], F32, kind="ExternalInput").ap()
    wgt_d = nc.dram_tensor("wgt", [64, 128, NC, 128], F32, kind="ExternalInput").ap()
    wdn_d = nc.dram_tensor("wdn", [NC, 128, 64, 128], F32, kind="ExternalInput").ap()
    r2_d = nc.dram_tensor("r2", [128, NC, T], F32, kind="ExternalOutput").ap()

    P = Plan()
    P.sem("st", "sv", "ss", "sg", "dcst", "dr1", "dwu0", "dwu1", "dwg0", "dwg1",
          "dwd0", "dwd1", "ds0", "ds1")

    S1 = float(1.0 / np.sqrt(2048.0))
    S2 = float(1.0 / np.sqrt(8192.0))

    with contextlib.ExitStack() as ctx:
        E = ctx.enter_context
        r1sup = E(nc.sbuf_tensor("r1sup", [128, NC, TB], F32))
        xln = E(nc.sbuf_tensor("xln", [128, NC, TB], BF16))
        t1 = E(nc.sbuf_tensor("t1", [128, NC, TB], BF16))
        hbuf = E(nc.sbuf_tensor("hbuf", [128, 64, TB], BF16))
        wub = [E(nc.sbuf_tensor(f"wub{i}", [128, NC, 128], BF16)) for i in range(2)]
        wgb = [E(nc.sbuf_tensor(f"wgb{i}", [128, NC, 128], BF16)) for i in range(2)]
        wdb = [E(nc.sbuf_tensor(f"wdb{i}", [128, 64, 128], BF16)) for i in range(2)]
        h128p = E(nc.sbuf_tensor("h128p", [128, 128], BF16))
        h128n = E(nc.sbuf_tensor("h128n", [128, 128], BF16))
        su_ug = E(nc.sbuf_tensor("su_ug_s", [128, NC], F32))
        su_dp = E(nc.sbuf_tensor("su_dp_s", [128, 64], F32))
        wscu = E(nc.sbuf_tensor("wscu_s", [128, 64], F32))
        wscg = E(nc.sbuf_tensor("wscg_s", [128, 64], F32))
        wscd = E(nc.sbuf_tensor("wscd_s", [128, NC], F32))
        sqs = E(nc.sbuf_tensor("sqs", [128, TB], BF16))
        rowm = E(nc.sbuf_tensor("rowm", [1, TB], F32))
        rowe = E(nc.sbuf_tensor("rowe", [1, TB], F32))
        nmurep = E(nc.sbuf_tensor("nmurep", [128, TB], F32))
        rstdrep = E(nc.sbuf_tensor("rstdrep", [128, TB], F32))
        dAb = [nmurep, rstdrep]
        tmpf = E(nc.sbuf_tensor("tmpf", [128, TB], F32))
        upt = E(nc.sbuf_tensor("upt", [128, TB], F32))
        silt = E(nc.sbuf_tensor("silt", [128, TB], F32))
        btmp = E(nc.sbuf_tensor("btmp", [128, TB], BF16))
        r2t = [E(nc.sbuf_tensor(f"r2t{i}", [128, TB], F32)) for i in range(2)]
        onec_b = E(nc.sbuf_tensor("onec_b", [128, 1], BF16))
        onec_f = E(nc.sbuf_tensor("onec_f", [128, 1], F32))
        oner_f = E(nc.sbuf_tensor("oner_f", [1, 128], F32))
        cstf = E(nc.sbuf_tensor("cstf", [128, 8], F32))
        pA = E(nc.psum_tensor("pA", [128, TB], F32))
        pB = E(nc.psum_tensor("pB", [128, TB], F32))
        PS = [pA, pB]
        sems = {n: E(nc.semaphore(n)) for n in P.names}

        def dma(q, out_ap, in_ap, sem, waits=()):
            P.add(q, lambda e, S, o=out_ap, i=in_ap: e.dma_start(out=o, in_=i),
                  waits=waits, incs=[(sem, 16)])
            return (sem, P.now(sem))

        for t_s, t_d in ((h128p, h128_d), (su_ug, suug_d), (su_dp, sudp_d),
                         (wscu, wscu_d), (wscg, wscg_d), (wscd, wscd_d)):
            cst_ok = dma("g", t_s[:], t_d[:], "dcst")
        P.add("g", lambda e, S: e.memset(onec_b[:], 1.0))
        P.add("g", lambda e, S: e.memset(onec_f[:], 1.0))
        P.add("g", lambda e, S: e.memset(oner_f[:], 1.0))
        P.add("g", lambda e, S: e.memset(cstf[:, 0:1], LN_EPS))
        P.add("g", lambda e, S: e.memset(cstf[:, 1:2], -0.5), incs=[("sg", 1)])
        consts_ok = P.tok("sg")
        P.add("v", lambda e, S: e.tensor_scalar_mul(h128n[:], h128p[:], -1.0),
              waits=[cst_ok, consts_ok])
        P.add("v", lambda e, S: e.tensor_scalar_mul(wscu[:], wscu[:], S1))
        P.add("v", lambda e, S: e.tensor_scalar_mul(wscg[:], wscg[:], S1))
        P.add("v", lambda e, S: e.tensor_scalar_mul(wscd[:], wscd[:], S2),
              incs=[("sv", 1)])
        P.barrier(("v",))

        bankfree = [P.tok("sv"), P.tok("sv")]
        wu_last = [("st", 0), ("st", 0)]
        wg_last = [("st", 0), ("st", 0)]
        wd_last = [("st", 0), ("st", 0)]
        r2_last = [("ds0", 0), ("ds1", 0)]
        r1_used = ("sv", 0)

        for ts in range(NTB):
            sl = slice(ts * TB, (ts + 1) * TB)
            r1_ok = dma("g", r1sup[:], r1i[:, :, sl], "dr1", waits=[r1_used])

            for dc in range(NC):
                P.add("v", lambda e, S, dc=dc: e.tensor_tensor(
                    sqs[:], r1sup[:, dc, :], r1sup[:, dc, :], ALU.mult),
                    waits=[r1_ok, P.tok("st")], incs=[("sv", 1)])
                P.add("t", lambda e, S, dc=dc: e.matmul(
                    pB[0:1, :], onec_b[:], sqs[:],
                    start=(dc == 0), stop=(dc == NC - 1)),
                    waits=[P.tok("sv")] + ([bankfree[1]] if dc == 0 else []),
                    incs=[("st", 1)])
            sq_mm_ok = P.tok("st")
            P.add("t", None, waits=[bankfree[0], r1_ok])
            for dc in range(NC):
                P.add("t", lambda e, S, dc=dc: e.matmul(
                    pA[0:1, :], onec_f[:], r1sup[:, dc, :],
                    start=(dc == 0), stop=(dc == NC - 1)))
            P.add("t", None, incs=[("st", 1)])
            sx_ok = P.tok("st")
            P.add("s", lambda e, S: e.mul(rowm[0:1, :], pA[0:1, :], 1.0 / D),
                  waits=[sx_ok])
            P.add("s", lambda e, S: e.mul(rowe[0:1, :], pB[0:1, :], 1.0 / D),
                  incs=[("ss", 1)])
            st_ok = P.tok("ss")
            P.add("v", lambda e, S: e.tensor_tensor(
                tmpf[0:1, 0:TB], rowm[0:1, :], rowm[0:1, :], ALU.mult), waits=[st_ok])
            P.add("v", lambda e, S: e.tensor_tensor(
                rowe[0:1, :], rowe[0:1, :], tmpf[0:1, 0:TB], ALU.subtract))
            P.add("v", lambda e, S: e.tensor_scalar_mul(rowm[0:1, :], rowm[0:1, :], -1.0),
                  incs=[("sv", 1)])
            P.add("s", lambda e, S: e.activation(rowe[0:1, :], rowe[0:1, :], AF.Ln,
                                                 bias=cstf[0:1, 0:1]), waits=[P.tok("sv")])
            P.add("s", lambda e, S: e.activation(rowe[0:1, :], rowe[0:1, :], AF.Exp,
                                                 scale=cstf[0:1, 1:2]), incs=[("ss", 1)])
            P.add("t", lambda e, S: e.matmul(pA[:], oner_f[:], rowm[0:1, :],
                                             start=True, stop=True),
                  waits=[P.tok("ss"), bankfree[0]], incs=[("st", 1)])
            P.add("v", lambda e, S: e.tensor_copy(nmurep[:], pA[:]),
                  waits=[P.tok("st")], incs=[("sv", 1)])
            bankfree[0] = P.tok("sv")
            P.add("t", lambda e, S: e.matmul(pB[:], oner_f[:], rowe[0:1, :],
                                             start=True, stop=True),
                  waits=[P.tok("ss"), bankfree[1], sq_mm_ok], incs=[("st", 1)])
            P.add("v", lambda e, S: e.tensor_copy(rstdrep[:], pB[:]),
                  waits=[P.tok("st")], incs=[("sv", 1)])
            bankfree[1] = P.tok("sv")
            rep_ok = P.tok("sv")

            for dc in range(NC):
                P.add("v", lambda e, S, dc=dc: e.tensor_tensor(
                    tmpf[:], r1sup[:, dc, :], nmurep[:], ALU.add), waits=[rep_ok] if dc == 0 else [])
                P.add("v", lambda e, S, dc=dc: e.scalar_tensor_tensor(
                    xln[:, dc, :], tmpf[:], su_ug[:, dc:dc + 1], rstdrep[:],
                    ALU.mult, ALU.mult))
            P.add("v", None, incs=[("sv", 1)])
            xln_ok = P.tok("sv")

            for ec in range(NC):
                bk = ec % 2
                pX = PS[bk]
                P.add("t", None, waits=[xln_ok, bankfree[bk]])
                for jc in range(NC):
                    hs = h128p if _h16sign(ec, jc) > 0 else h128n
                    P.add("t", lambda e, S, jc=jc, hs=hs, pX=pX: e.matmul(
                        pX[:], hs[:], xln[:, jc, :],
                        start=(jc == 0), stop=(jc == NC - 1)))
                P.add("t", None, incs=[("st", 1)])
                P.add("v", lambda e, S, ec=ec, pX=pX: e.tensor_copy(t1[:, ec, :], pX[:]),
                      waits=[P.tok("st")], incs=[("sv", 1)])
                bankfree[bk] = P.tok("sv")
            t1_ok = P.tok("sv")

            for oc in range(64):
                par = oc % 2
                wu_ok = dma("g", wub[par][:], wup_d[oc], f"dwu{par}", waits=[wu_last[par]])
                wg_ok = dma("g", wgb[par][:], wgt_d[oc], f"dwg{par}", waits=[wg_last[par]])
                P.add("t", None, waits=[t1_ok, wu_ok, wg_ok, bankfree[0], bankfree[1]])
                for dc in range(NC):
                    P.add("t", lambda e, S, dc=dc, par=par: e.matmul(
                        pA[:], wub[par][:, dc, :], t1[:, dc, :],
                        start=(dc == 0), stop=(dc == NC - 1)))
                for dc in range(NC):
                    P.add("t", lambda e, S, dc=dc, par=par: e.matmul(
                        pB[:], wgb[par][:, dc, :], t1[:, dc, :],
                        start=(dc == 0), stop=(dc == NC - 1)))
                P.add("t", None, incs=[("st", 1)])
                mm_ok = P.tok("st")
                wu_last[par] = mm_ok
                wg_last[par] = mm_ok
                P.add("v", lambda e, S, oc=oc: e.tensor_scalar(
                    upt[:], pA[:], wscu[:, oc:oc + 1], None, ALU.mult),
                    waits=[mm_ok], incs=[("sv", 1)])
                bankfree[0] = P.tok("sv")
                P.add("s", lambda e, S, oc=oc: e.activation(
                    silt[:], pB[:], AF.Silu, scale=wscg[:, oc:oc + 1]),
                    waits=[mm_ok, P.tok("sv")], incs=[("ss", 1)])
                bankfree[1] = P.tok("ss")
                P.add("v", lambda e, S, oc=oc: e.scalar_tensor_tensor(
                    hbuf[:, oc, :], silt[:], su_dp[:, oc:oc + 1], upt[:],
                    ALU.mult, ALU.mult),
                    waits=[P.tok("ss")], incs=[("sv", 1)])
            h_ok = P.tok("sv")

            for stg in range(6):
                stride = 32 >> stg
                for base in range(0, 64, 2 * stride):
                    for i in range(base, base + stride):
                        j = i + stride
                        P.add("v", lambda e, S, i=i, j=j: e.tensor_tensor(
                            btmp[:], hbuf[:, i, :], hbuf[:, j, :], ALU.subtract))
                        P.add("v", lambda e, S, i=i, j=j: e.tensor_tensor(
                            hbuf[:, i, :], hbuf[:, i, :], hbuf[:, j, :], ALU.add))
                        P.add("v", lambda e, S, j=j: e.tensor_copy(
                            hbuf[:, j, :], btmp[:]))
            P.add("v", None, incs=[("sv", 1)])
            bfl_ok = P.tok("sv")
            for oc in range(64):
                bk = oc % 2
                pX = PS[bk]
                P.add("t", lambda e, S, oc=oc, pX=pX: e.matmul(
                    pX[:], h128p[:], hbuf[:, oc, :], start=True, stop=True),
                    waits=[bfl_ok, bankfree[bk]], incs=[("st", 1)])
                P.add("v", lambda e, S, oc=oc, pX=pX: e.tensor_copy(
                    hbuf[:, oc, :], pX[:]),
                    waits=[P.tok("st")], incs=[("sv", 1)])
                bankfree[bk] = P.tok("sv")
            t2_ok = P.tok("sv")

            for dco in range(NC):
                par = dco % 2
                wd_ok = dma("g", wdb[par][:], wdn_d[dco], f"dwd{par}", waits=[wd_last[par]])
                pX = PS[par]
                P.add("t", None, waits=[t2_ok, wd_ok, bankfree[par]])
                for ic in range(64):
                    P.add("t", lambda e, S, ic=ic, par=par, pX=pX: e.matmul(
                        pX[:], wdb[par][:, ic, :], hbuf[:, ic, :],
                        start=(ic == 0), stop=(ic == 63)))
                P.add("t", None, incs=[("st", 1)])
                wd_last[par] = P.tok("st")
                rb2 = r2t[par]
                P.add("v", lambda e, S, dco=dco, pX=pX, rb2=rb2: e.scalar_tensor_tensor(
                    rb2[:], pX[:], wscd[:, dco:dco + 1], r1sup[:, dco, :],
                    ALU.mult, ALU.add),
                    waits=[P.tok("st"), r2_last[par]], incs=[("sv", 1)])
                bankfree[par] = P.tok("sv")
                r2_last[par] = dma("y", r2_d[:, dco, sl], rb2[:], f"ds{par}",
                                   waits=[P.tok("sv")])
            r1_used = P.tok("sv")

        P.add("y", None, waits=[P.tok("ds0"), P.tok("ds1")])
        P.emit(nc, sems)
    return nc



DI = 4096
DIC = 32
DSTATE = 16


def prep_l1attn_host(W_in, conv_w, conv_b, W_x, W_dt, dt_bias, A_log, Dv, W_out):
    win = np.ascontiguousarray(np.asarray(W_in, np.float32)
                               .reshape(64, 128, NC, 128).transpose(0, 3, 2, 1))
    convw = np.ascontiguousarray(np.asarray(conv_w, np.float32)
                                 .reshape(DIC, 128, 4).transpose(1, 0, 2))
    convb = np.ascontiguousarray(np.asarray(conv_b, np.float32).reshape(DIC, 128).T)
    wx = np.asarray(W_x, np.float32)
    wxdt = np.ascontiguousarray(wx[:128].reshape(128, DIC, 128).transpose(2, 1, 0))
    wxB = np.ascontiguousarray(wx[128:144].reshape(16, DIC, 128).transpose(2, 1, 0))
    wxC = np.ascontiguousarray(wx[144:160].reshape(16, DIC, 128).transpose(2, 1, 0))
    wdt = np.ascontiguousarray(np.asarray(W_dt, np.float32).T.reshape(128, DIC, 128))
    dtb = np.ascontiguousarray(np.asarray(dt_bias, np.float32).reshape(DIC, 128).T)
    alog = np.ascontiguousarray(np.asarray(A_log, np.float32)
                                .reshape(DIC, 128, 16).transpose(1, 0, 2))
    dvec = np.ascontiguousarray(np.asarray(Dv, np.float32).reshape(DIC, 128).T)
    wout = np.ascontiguousarray(np.asarray(W_out, np.float32)
                                .reshape(NC, 128, DIC, 128).transpose(0, 3, 2, 1))
    e16 = np.zeros((16, 16, 128), np.float32)
    for n in range(16):
        e16[n, n, :] = 1.0
    e16 = np.ascontiguousarray(e16.transpose(1, 0, 2).reshape(16, 16 * 128))
    return dict(win=win, convw=convw, convb=convb, wxdt=wxdt, wxB=wxB, wxC=wxC,
                wdt=wdt, dtb=dtb, alog=alog, dvec=dvec, wout=wout,
                h128=_had128(), e16=e16)


def build_l1attn():
    nc = bass.Bass("TRN2", target_bir_lowering=False, debug=False)
    r2i = nc.dram_tensor("r2i", [128, NC, T], F32, kind="ExternalInput").ap()
    win_d = nc.dram_tensor("win", [64, 128, NC, 128], F32, kind="ExternalInput").ap()
    convw_d = nc.dram_tensor("convw", [128, DIC, 4], F32, kind="ExternalInput").ap()
    convb_d = nc.dram_tensor("convb", [128, DIC], F32, kind="ExternalInput").ap()
    wxdt_d = nc.dram_tensor("wxdt", [128, DIC, 128], F32, kind="ExternalInput").ap()
    wxB_d = nc.dram_tensor("wxB", [128, DIC, 16], F32, kind="ExternalInput").ap()
    wxC_d = nc.dram_tensor("wxC", [128, DIC, 16], F32, kind="ExternalInput").ap()
    wdt_d = nc.dram_tensor("wdt", [128, DIC, 128], F32, kind="ExternalInput").ap()
    dtb_d = nc.dram_tensor("dtb", [128, DIC], F32, kind="ExternalInput").ap()
    alog_d = nc.dram_tensor("alog", [128, DIC, 16], F32, kind="ExternalInput").ap()
    dvec_d = nc.dram_tensor("dvec", [128, DIC], F32, kind="ExternalInput").ap()
    wout_d = nc.dram_tensor("wout", [NC, 128, DIC, 128], F32, kind="ExternalInput").ap()
    h128_d = nc.dram_tensor("h128", [128, 128], F32, kind="ExternalInput").ap()
    e16_d = nc.dram_tensor("e16", [16, 16 * 128], F32, kind="ExternalInput").ap()
    r3_d = nc.dram_tensor("r3", [128, NC, T], F32, kind="ExternalOutput").ap()

    P = Plan()
    P.sem("st", "sv", "ss", "sg", "dcst", "dwx", "dwi0", "dwi1",
          "dwo0", "dwo1", "ds0", "ds1", "drc0", "drc1")

    with contextlib.ExitStack() as ctx:
        E = ctx.enter_context
        xln = E(nc.sbuf_tensor("xln", [128, NC, TB], BF16))
        xconv = E(nc.sbuf_tensor("xconv", [128, DIC, 3 + TB], BF16))
        zsil = E(nc.sbuf_tensor("zsil", [128, DIC, TB], BF16))
        ubuf = E(nc.sbuf_tensor("ubuf", [128, DIC, TB], BF16))
        crep = E(nc.sbuf_tensor("crep", [128, 16, TB], BF16))
        scn = E(nc.sbuf_tensor("scn", [128, 16, TB], BF16))
        brep = xln
        winb = [E(nc.sbuf_tensor(f"winb{i}", [128, NC, 128], BF16)) for i in range(2)]
        woutb = E(nc.sbuf_tensor("woutb", [128, DIC, 128], BF16))
        wxw = E(nc.sbuf_tensor("wxw_s", [128, DIC, 128], BF16))
        wxB = E(nc.sbuf_tensor("wxB_s", [128, DIC, 16], BF16))
        wxC = E(nc.sbuf_tensor("wxC_s", [128, DIC, 16], BF16))
        convw = E(nc.sbuf_tensor("convw_s", [128, DIC, 4], F32))
        convb = E(nc.sbuf_tensor("convb_s", [128, DIC], F32))
        dtb = E(nc.sbuf_tensor("dtb_s", [128, DIC], F32))
        negA = E(nc.sbuf_tensor("negA", [128, DIC, 16], F32))
        dvec = E(nc.sbuf_tensor("dvec_s", [128, DIC], F32))
        carry = E(nc.sbuf_tensor("carry", [128, DIC, 16], F32))
        e16 = E(nc.sbuf_tensor("e16_s", [16, 16 * 128], BF16))
        h128p = E(nc.sbuf_tensor("h128p", [128, 128], BF16))
        bc_s = E(nc.sbuf_tensor("bc_s", [16, TB], BF16))
        dtr = E(nc.sbuf_tensor("dtr", [128, TB], BF16))
        delta = E(nc.sbuf_tensor("delta", [128, TB], F32))
        dub = E(nc.sbuf_tensor("dub", [128, TB], BF16))
        delb = E(nc.sbuf_tensor("delb", [128, TB], BF16))
        r2c = [E(nc.sbuf_tensor(f"r2c{i}", [128, TB], F32)) for i in range(2)]
        rowm = E(nc.sbuf_tensor("rowm", [1, TB], F32))
        rowe = E(nc.sbuf_tensor("rowe", [1, TB], F32))
        nmurep = E(nc.sbuf_tensor("nmurep", [128, TB], F32))
        rstdrep = E(nc.sbuf_tensor("rstdrep", [128, TB], F32))
        dAb = [nmurep, rstdrep]
        tmpf = E(nc.sbuf_tensor("tmpf", [128, TB], F32))
        tmpf2 = E(nc.sbuf_tensor("tmpf2", [128, TB], F32))
        btmp = E(nc.sbuf_tensor("btmp", [128, TB], BF16))
        sqs = btmp
        r3t = [E(nc.sbuf_tensor(f"r3t{i}", [128, TB], F32)) for i in range(2)]
        onec_b = E(nc.sbuf_tensor("onec_b", [128, 1], BF16))
        onec_f = E(nc.sbuf_tensor("onec_f", [128, 1], F32))
        oner_f = E(nc.sbuf_tensor("oner_f", [1, 128], F32))
        cstf = E(nc.sbuf_tensor("cstf", [128, 8], F32))
        pA = E(nc.psum_tensor("pA", [128, TB], F32))
        pB = E(nc.psum_tensor("pB", [128, TB], F32))
        PS = [pA, pB]
        sems = {n: E(nc.semaphore(n)) for n in P.names}

        def dma(q, out_ap, in_ap, sem, waits=()):
            P.add(q, lambda e, S, o=out_ap, i=in_ap: e.dma_start(out=o, in_=i),
                  waits=waits, incs=[(sem, 16)])
            return (sem, P.now(sem))

        for t_s, t_d in ((wxB, wxB_d), (wxC, wxC_d),
                         (convw, convw_d), (convb, convb_d), (dtb, dtb_d),
                         (negA, alog_d), (dvec, dvec_d), (e16, e16_d),
                         (h128p, h128_d)):
            cst_ok = dma("g", t_s[:], t_d[:], "dcst")
        P.add("g", lambda e, S: e.memset(onec_b[:], 1.0))
        P.add("g", lambda e, S: e.memset(onec_f[:], 1.0))
        P.add("g", lambda e, S: e.memset(oner_f[:], 1.0))
        P.add("g", lambda e, S: e.memset(carry[:], 0.0))
        for dc in range(DIC):
            P.add("g", lambda e, S, dc=dc: e.memset(xconv[:, dc, 0:3], 0.0))
        P.add("g", lambda e, S: e.memset(cstf[:, 0:1], LN_EPS))
        P.add("g", lambda e, S: e.memset(cstf[:, 1:2], -0.5))
        P.add("g", lambda e, S: e.memset(cstf[:, 2:3], -1.0))
        P.add("g", lambda e, S: e.memset(cstf[:, 3:4], 1.0))
        P.add("g", lambda e, S: e.memset(cstf[:, 4:5], float(2.0 ** -6)),
              incs=[("sg", 1)])
        consts_ok = P.tok("sg")
        P.add("s", lambda e, S: e.activation(negA[:], negA[:], AF.Exp),
              waits=[cst_ok, consts_ok], incs=[("ss", 1)])
        P.add("v", lambda e, S: e.tensor_scalar_mul(negA[:], negA[:], -1.0),
              waits=[P.tok("ss")], incs=[("sv", 1)])
        P.barrier(("t", "v", "s"))

        bankfree = [P.tok("sv"), P.tok("sv")]
        win_last = [("st", 0), ("st", 0)]
        wxw_last = ("st", 0)
        wout_last = ("st", 0)
        r2c_last = [("sv", 0), ("sv", 0)]
        r3_last = [("ds0", 0), ("ds1", 0)]

        for ts in range(NTB):
            sl = slice(ts * TB, (ts + 1) * TB)

            if ts > 0:
                for dc in range(DIC):
                    P.add("v", lambda e, S, dc=dc: e.tensor_copy(
                        xconv[:, dc, 0:3], xconv[:, dc, TB:TB + 3]),
                        waits=[P.tok("st")] if dc == 0 else [])
                P.add("v", None, incs=[("sv", 1)])
            halo_ok = P.tok("sv")

            for dc in range(NC):
                par = dc % 2
                r2_ok = dma("y", r2c[par][:], r2i[:, dc, sl], f"drc{par}",
                            waits=[r2c_last[par]])
                P.add("v", lambda e, S, par=par: e.tensor_tensor(
                    sqs[:], r2c[par][:], r2c[par][:], ALU.mult),
                    waits=[r2_ok, P.tok("st")], incs=[("sv", 1)])
                P.add("t", lambda e, S, par=par, dc=dc: e.matmul(
                    pA[0:1, :], onec_f[:], r2c[par][:],
                    start=(dc == 0), stop=(dc == NC - 1)),
                    waits=[P.tok("sv")] + ([bankfree[0], bankfree[1]] if dc == 0 else []))
                P.add("t", lambda e, S, dc=dc: e.matmul(
                    pB[0:1, :], onec_b[:], sqs[:],
                    start=(dc == 0), stop=(dc == NC - 1)), incs=[("st", 1)])
                r2c_last[par] = P.tok("st")
            P.add("s", lambda e, S: e.mul(rowm[0:1, :], pA[0:1, :], 1.0 / D),
                  waits=[P.tok("st")])
            P.add("s", lambda e, S: e.mul(rowe[0:1, :], pB[0:1, :], 1.0 / D),
                  incs=[("ss", 1)])
            P.add("v", lambda e, S: e.tensor_tensor(
                tmpf[0:1, 0:TB], rowm[0:1, :], rowm[0:1, :], ALU.mult),
                waits=[P.tok("ss")])
            P.add("v", lambda e, S: e.tensor_tensor(
                rowe[0:1, :], rowe[0:1, :], tmpf[0:1, 0:TB], ALU.subtract))
            P.add("v", lambda e, S: e.tensor_scalar_mul(rowm[0:1, :], rowm[0:1, :], -1.0),
                  incs=[("sv", 1)])
            P.add("s", lambda e, S: e.activation(rowe[0:1, :], rowe[0:1, :], AF.Ln,
                                                 bias=cstf[0:1, 0:1]), waits=[P.tok("sv")])
            P.add("s", lambda e, S: e.activation(rowe[0:1, :], rowe[0:1, :], AF.Exp,
                                                 scale=cstf[0:1, 1:2]), incs=[("ss", 1)])
            P.add("t", lambda e, S: e.matmul(pA[:], oner_f[:], rowm[0:1, :],
                                             start=True, stop=True),
                  waits=[P.tok("ss")], incs=[("st", 1)])
            P.add("v", lambda e, S: e.tensor_copy(nmurep[:], pA[:]),
                  waits=[P.tok("st")], incs=[("sv", 1)])
            P.add("t", lambda e, S: e.matmul(pB[:], oner_f[:], rowe[0:1, :],
                                             start=True, stop=True), incs=[("st", 1)])
            P.add("v", lambda e, S: e.tensor_copy(rstdrep[:], pB[:]),
                  waits=[P.tok("st")], incs=[("sv", 1)])
            rep_ok = P.tok("sv")
            bankfree = [rep_ok, rep_ok]

            for dc in range(NC):
                par = dc % 2
                r2_ok = dma("y", r2c[par][:], r2i[:, dc, sl], f"drc{par}",
                            waits=[r2c_last[par]])
                P.add("v", lambda e, S, dc=dc, par=par: e.tensor_tensor(
                    tmpf[:], r2c[par][:], nmurep[:], ALU.add),
                    waits=[r2_ok] + ([rep_ok] if dc == 0 else []))
                P.add("v", lambda e, S, dc=dc: e.tensor_tensor(
                    xln[:, dc, :], tmpf[:], rstdrep[:], ALU.mult), incs=[("sv", 1)])
                r2c_last[par] = P.tok("sv")
            xln_ok = P.tok("sv")

            for oc in range(64):
                par = oc % 2
                wi_ok = dma("g", winb[par][:], win_d[oc], f"dwi{par}",
                            waits=[win_last[par]])
                pX = PS[par]
                P.add("t", None, waits=[xln_ok, wi_ok, bankfree[par]] +
                      ([halo_ok] if oc < DIC else []))
                for dc in range(NC):
                    P.add("t", lambda e, S, dc=dc, par=par, pX=pX: e.matmul(
                        pX[:], winb[par][:, dc, :], xln[:, dc, :],
                        start=(dc == 0), stop=(dc == NC - 1)))
                P.add("t", None, incs=[("st", 1)])
                mm_ok = P.tok("st")
                win_last[par] = mm_ok
                if oc < DIC:
                    P.add("v", lambda e, S, oc=oc, pX=pX: e.tensor_copy(
                        xconv[:, oc, 3:3 + TB], pX[:]),
                        waits=[mm_ok], incs=[("sv", 1)])
                    bankfree[par] = P.tok("sv")
                else:
                    P.add("s", lambda e, S, oc=oc, pX=pX: e.activation(
                        zsil[:, oc - DIC, :], pX[:], AF.Silu),
                        waits=[mm_ok], incs=[("ss", 1)])
                    bankfree[par] = P.tok("ss")
            inproj_ok = [P.tok("sv"), P.tok("ss")]

            for dc in range(DIC):
                P.add("v", lambda e, S, dc=dc: e.tensor_scalar(
                    tmpf[:], xconv[:, dc, 0:TB], convw[:, dc, 0:1], None, ALU.mult),
                    waits=(inproj_ok if dc == 0 else []))
                for k2 in range(1, 4):
                    P.add("v", lambda e, S, dc=dc, k2=k2: e.scalar_tensor_tensor(
                        tmpf[:], xconv[:, dc, k2:k2 + TB], convw[:, dc, k2:k2 + 1],
                        tmpf[:], ALU.mult, ALU.add))
                P.add("v", None, incs=[("sv", 1)])
                P.add("s", lambda e, S, dc=dc: e.activation(
                    ubuf[:, dc, :], tmpf[:], AF.Silu, bias=convb[:, dc:dc + 1]),
                    waits=[P.tok("sv")], incs=[("ss", 1)])
                P.add("v", None, waits=[P.tok("ss")])
            u_ok = P.tok("ss")

            wx_ok = dma("g", wxw[:], wxdt_d[:], "dwx", waits=[wxw_last])
            P.add("t", None, waits=[u_ok, bankfree[0], wx_ok])
            for dc in range(DIC):
                P.add("t", lambda e, S, dc=dc: e.matmul(
                    pA[:], wxw[:, dc, :], ubuf[:, dc, :],
                    start=(dc == 0), stop=(dc == DIC - 1)))
            P.add("t", None, incs=[("st", 1)])
            wxw_last = P.tok("st")
            P.add("v", lambda e, S: e.tensor_copy(dtr[:], pA[:]),
                  waits=[P.tok("st")], incs=[("sv", 1)])
            bankfree[0] = P.tok("sv")
            for nm_w, rep in ((wxB, brep), (wxC, crep)):
                P.add("t", None, waits=[bankfree[1]])
                for dc in range(DIC):
                    P.add("t", lambda e, S, dc=dc, nm_w=nm_w: e.matmul(
                        pB[0:16, :], nm_w[:, dc, :], ubuf[:, dc, :],
                        start=(dc == 0), stop=(dc == DIC - 1)))
                P.add("t", None, incs=[("st", 1)])
                P.add("s", lambda e, S: e.copy(bc_s[:], pB[0:16, :]),
                      waits=[P.tok("st")], incs=[("ss", 1)])
                bankfree[1] = P.tok("ss")
                for n in range(16):
                    bk = n % 2
                    pX = PS[bk]
                    P.add("t", lambda e, S, n=n, pX=pX: e.matmul(
                        pX[:], e16[:, n * 128:(n + 1) * 128], bc_s[:],
                        start=True, stop=True),
                        waits=[P.tok("ss"), bankfree[bk]], incs=[("st", 1)])
                    P.add("v", lambda e, S, n=n, pX=pX, rep=rep: e.tensor_copy(
                        rep[:, n, :], pX[:]),
                        waits=[P.tok("st")], incs=[("sv", 1)])
                    bankfree[bk] = P.tok("sv")
            bc_ok = P.tok("sv")

            wd_ok = dma("g", wxw[:], wdt_d[:], "dwx", waits=[P.tok("st")])
            for dc in range(DIC):
                P.add("t", lambda e, S, dc=dc: e.matmul(
                    pA[:], wxw[:, dc, :], dtr[:], start=True, stop=True),
                    waits=([bankfree[0], bc_ok, wd_ok] if dc == 0 else [bankfree[0]]),
                    incs=[("st", 1)])
                dmm = P.tok("st")
                P.add("v", lambda e, S, dc=dc: e.tensor_scalar(
                    tmpf[:], pA[:], dtb[:, dc:dc + 1], None, ALU.add),
                    waits=[dmm], incs=[("sv", 1)])
                bankfree[0] = P.tok("sv")
                P.add("s", lambda e, S: e.activation(tmpf2[:], tmpf[:], AF.Abs),
                      waits=[P.tok("sv")])
                P.add("s", lambda e, S: e.activation(tmpf2[:], tmpf2[:], AF.Exp,
                                                     scale=cstf[:, 2:3]))
                P.add("s", lambda e, S: e.activation(tmpf2[:], tmpf2[:], AF.Ln,
                                                     bias=cstf[:, 3:4]), incs=[("ss", 1)])
                P.add("v", lambda e, S: e.tensor_scalar_max(delta[:], tmpf[:], 0.0),
                      waits=[P.tok("ss")])
                P.add("v", lambda e, S: e.tensor_tensor(
                    delta[:], delta[:], tmpf2[:], ALU.add))
                P.add("v", lambda e, S: e.tensor_copy(delb[:], delta[:]))
                P.add("v", lambda e, S, dc=dc: e.tensor_tensor(
                    dub[:], delb[:], ubuf[:, dc, :], ALU.mult))
                P.add("v", lambda e, S: e.tensor_tensor(
                    scn[:], dub[:].unsqueeze(1).broadcast_to([128, 16, TB]),
                    brep[:], ALU.mult), incs=[("sv", 1)])
                dbu_ok = P.tok("sv")
                for n in range(16):
                    da = dAb[n % 2]
                    P.add("s", lambda e, S, dc=dc, n=n, da=da: e.activation(
                        da[:], delta[:], AF.Exp,
                        scale=negA[:, dc, n:n + 1]),
                        waits=[P.tok("sv")], incs=[("ss", 1)])
                    P.add("v", lambda e, S, dc=dc, n=n, da=da: e.tensor_tensor_scan(
                        scn[:, n, :], da[:], scn[:, n, :],
                        carry[:, dc, n:n + 1], ALU.mult, ALU.add),
                        waits=[P.tok("ss")], incs=[("sv", 1)])
                scan_ok = P.tok("sv")
                P.add("v", lambda e, S, dc=dc: e.tensor_copy(
                    carry[:, dc, :], scn[:, :, TB - 1]))
                P.add("v", lambda e, S: e.tensor_tensor(scn[:], scn[:], crep[:], ALU.mult))
                P.add("v", lambda e, S: e.tensor_tensor(
                    scn[:, 0:8, :], scn[:, 0:8, :], scn[:, 8:16, :], ALU.add))
                P.add("v", lambda e, S: e.tensor_tensor(
                    scn[:, 0:4, :], scn[:, 0:4, :], scn[:, 4:8, :], ALU.add))
                P.add("v", lambda e, S: e.tensor_tensor(
                    scn[:, 0:2, :], scn[:, 0:2, :], scn[:, 2:4, :], ALU.add))
                P.add("v", lambda e, S: e.tensor_tensor(
                    tmpf[:], scn[:, 0, :], scn[:, 1, :], ALU.add))
                P.add("v", lambda e, S, dc=dc: e.scalar_tensor_tensor(
                    tmpf[:], ubuf[:, dc, :], dvec[:, dc:dc + 1], tmpf[:],
                    ALU.mult, ALU.add))
                P.add("v", lambda e, S, dc=dc: e.scalar_tensor_tensor(
                    xconv[:, dc, 0:TB], tmpf[:], cstf[:, 4:5], zsil[:, dc, :],
                    ALU.mult, ALU.mult), incs=[("sv", 1)])
            y_ok = P.tok("sv")

            for stg in range(5):
                stride = 16 >> stg
                for base in range(0, DIC, 2 * stride):
                    for i in range(base, base + stride):
                        j = i + stride
                        P.add("v", lambda e, S, i=i, j=j: e.tensor_tensor(
                            btmp[:], xconv[:, i, 0:TB], xconv[:, j, 0:TB], ALU.subtract),
                            waits=[y_ok] if (stg == 0 and i == 0) else [])
                        P.add("v", lambda e, S, i=i, j=j: e.tensor_tensor(
                            xconv[:, i, 0:TB], xconv[:, i, 0:TB], xconv[:, j, 0:TB], ALU.add))
                        P.add("v", lambda e, S, j=j: e.tensor_copy(
                            xconv[:, j, 0:TB], btmp[:]))
            P.add("v", None, incs=[("sv", 1)])
            bfl_ok = P.tok("sv")
            for ec in range(DIC):
                bk = ec % 2
                pX = PS[bk]
                P.add("t", lambda e, S, ec=ec, pX=pX: e.matmul(
                    pX[:], h128p[:], xconv[:, ec, 0:TB], start=True, stop=True),
                    waits=[bfl_ok, bankfree[bk]], incs=[("st", 1)])
                P.add("v", lambda e, S, ec=ec, pX=pX: e.tensor_copy(
                    xconv[:, ec, 0:TB], pX[:]),
                    waits=[P.tok("st")], incs=[("sv", 1)])
                bankfree[bk] = P.tok("sv")
            t2_ok = P.tok("sv")

            for dco in range(NC):
                par = dco % 2
                wo_ok = dma("g", woutb[:], wout_d[dco], "dwo0", waits=[wout_last])
                r2_ok = dma("y", r2c[par][:], r2i[:, dco, sl], f"drc{par}",
                            waits=[r2c_last[par]])
                pX = PS[par]
                P.add("t", None, waits=[t2_ok, wo_ok, bankfree[par]])
                for ec in range(DIC):
                    P.add("t", lambda e, S, ec=ec, pX=pX: e.matmul(
                        pX[:], woutb[:, ec, :], xconv[:, ec, 0:TB],
                        start=(ec == 0), stop=(ec == DIC - 1)))
                P.add("t", None, incs=[("st", 1)])
                wout_last = P.tok("st")
                rb3 = r3t[par]
                P.add("v", lambda e, S, pX=pX, par=par, rb3=rb3: e.tensor_tensor(
                    rb3[:], pX[:], r2c[par][:], ALU.add),
                    waits=[P.tok("st"), r2_ok, r3_last[par]], incs=[("sv", 1)])
                bankfree[par] = P.tok("sv")
                r2c_last[par] = P.tok("sv")
                r3_last[par] = dma("y", r3_d[:, dco, sl], rb3[:], f"ds{par}",
                                   waits=[P.tok("sv")])

        P.add("y", None, waits=[P.tok("ds0"), P.tok("ds1")])
        P.emit(nc, sems)
    return nc



GOC = 43


def prep_l1ffn_host(lut, gate_idx, up_idx, down_idx):
    gidx = np.ascontiguousarray(np.asarray(gate_idx, np.int32)
                                .reshape(GOC, 128, NC, 128).transpose(0, 3, 2, 1))
    uidx = np.ascontiguousarray(np.asarray(up_idx, np.int32)
                                .reshape(GOC, 128, NC, 128).transpose(0, 3, 2, 1))
    didx = np.ascontiguousarray(np.asarray(down_idx, np.int32).T
                                .reshape(GOC, 128, NC, 128))
    return dict(gidx=gidx, uidx=uidx, didx=didx,
                lut=np.ascontiguousarray(np.asarray(lut, np.float32).reshape(1, 16)))


def build_l1ffn():
    nc = bass.Bass("TRN2", target_bir_lowering=False, debug=False)
    r3i = nc.dram_tensor("r3i", [128, NC, T], F32, kind="ExternalInput").ap()
    gidx_d = nc.dram_tensor("gidx", [GOC, 128, NC, 128], I32, kind="ExternalInput").ap()
    uidx_d = nc.dram_tensor("uidx", [GOC, 128, NC, 128], I32, kind="ExternalInput").ap()
    didx_d = nc.dram_tensor("didx", [GOC, 128, NC, 128], I32, kind="ExternalInput").ap()
    lut_d = nc.dram_tensor("lut", [1, 16], F32, kind="ExternalInput").ap()
    y_d = nc.dram_tensor("yout", [128, NC, T], F32, kind="ExternalOutput").ap()
    wgu_scr = nc.dram_tensor("wgu_scr", [2, GOC, 128, NC, 128], BF16, kind="Internal").ap()
    wdn_scr = nc.dram_tensor("wdn_scr", [NC, 128, GOC, 128], BF16, kind="Internal").ap()

    P = Plan()
    P.sem("st", "sv", "ss", "sg", "dcst", "didx0", "didx1", "dwb0", "dwb1",
          "dwg0", "dwg1", "dwu0", "dwu1", "dwd0", "dwd1", "drc0", "drc1",
          "ds0", "ds1")

    with contextlib.ExitStack() as ctx:
        E = ctx.enter_context
        xln = E(nc.sbuf_tensor("xln", [128, NC, TB], BF16))
        hbuf = E(nc.sbuf_tensor("hbuf", [128, GOC, TB], BF16))
        idxb = [E(nc.sbuf_tensor(f"idxb{i}", [128, NC, 128], I32)) for i in range(2)]
        idxf = E(nc.sbuf_tensor("idxf", [128, 2048], BF16))
        contrib = E(nc.sbuf_tensor("contrib", [128, 2048], BF16))
        wacc = [E(nc.sbuf_tensor(f"wacc{i}", [128, 2048], BF16)) for i in range(2)]
        wgt = [E(nc.sbuf_tensor(f"wgt{i}", [128, NC, 128], BF16)) for i in range(2)]
        wut = [E(nc.sbuf_tensor(f"wut{i}", [128, NC, 128], BF16)) for i in range(2)]
        wdt = [E(nc.sbuf_tensor(f"wdt{i}", [128, GOC, 128], BF16)) for i in range(2)]
        lut_s = E(nc.sbuf_tensor("lut_s", [1, 16], F32))
        lutrep = E(nc.sbuf_tensor("lutrep", [128, 16], F32))
        r3c = [E(nc.sbuf_tensor(f"r3c{i}", [128, TB], F32)) for i in range(2)]
        rowm = E(nc.sbuf_tensor("rowm", [1, TB], F32))
        rowe = E(nc.sbuf_tensor("rowe", [1, TB], F32))
        nmurep = E(nc.sbuf_tensor("nmurep", [128, TB], F32))
        rstdrep = E(nc.sbuf_tensor("rstdrep", [128, TB], F32))
        tmpf = E(nc.sbuf_tensor("tmpf", [128, TB], F32))
        silt = E(nc.sbuf_tensor("silt", [128, TB], F32))
        sqs = E(nc.sbuf_tensor("sqs", [128, TB], BF16))
        yt = [E(nc.sbuf_tensor(f"yt{i}", [128, TB], F32)) for i in range(2)]
        onec_b = E(nc.sbuf_tensor("onec_b", [128, 1], BF16))
        onec_f = E(nc.sbuf_tensor("onec_f", [128, 1], F32))
        oner_f = E(nc.sbuf_tensor("oner_f", [1, 128], F32))
        cstf = E(nc.sbuf_tensor("cstf", [128, 8], F32))
        pA = E(nc.psum_tensor("pA", [128, TB], F32))
        pB = E(nc.psum_tensor("pB", [128, TB], F32))
        PS = [pA, pB]
        sems = {n: E(nc.semaphore(n)) for n in P.names}

        def dma(q, out_ap, in_ap, sem, waits=()):
            P.add(q, lambda e, S, o=out_ap, i=in_ap: e.dma_start(out=o, in_=i),
                  waits=waits, incs=[(sem, 16)])
            return (sem, P.now(sem))

        lut_ok = dma("g", lut_s[:], lut_d[:], "dcst")
        P.add("g", lambda e, S: e.memset(onec_b[:], 1.0))
        P.add("g", lambda e, S: e.memset(onec_f[:], 1.0))
        P.add("g", lambda e, S: e.memset(oner_f[:], 1.0))
        P.add("g", lambda e, S: e.memset(cstf[:, 0:1], LN_EPS))
        P.add("g", lambda e, S: e.memset(cstf[:, 1:2], -0.5), incs=[("sg", 1)])
        consts_ok = P.tok("sg")
        P.add("t", lambda e, S: e.matmul(pA[:, 0:16], oner_f[:], lut_s[:],
                                         start=True, stop=True),
              waits=[lut_ok, consts_ok], incs=[("st", 1)])
        P.add("v", lambda e, S: e.tensor_copy(lutrep[:], pA[:, 0:16]),
              waits=[P.tok("st")], incs=[("sv", 1)])
        lutrep_ok = P.tok("sv")

        idx_last = [("st", 0), ("st", 0)]
        wb_store = [("dwb0", 0), ("dwb1", 0)]
        builds = []
        for kind in range(3):
            src_d = (gidx_d, uidx_d, didx_d)[kind]
            for ic in range(GOC):
                builds.append((kind, ic, src_d))
        for bi, (kind, ic, src_d) in enumerate(builds):
            par = bi % 2
            i_ok = dma("g", idxb[par][:], src_d[ic], f"didx{par}",
                       waits=[idx_last[par]])
            acc = wacc[par]
            P.add("v", lambda e, S, par=par: e.tensor_copy(
                idxf[:], idxb[par][:]),
                waits=[i_ok, lutrep_ok, wb_store[par]])
            for v in range(16):
                if v == 0:
                    P.add("v", lambda e, S, v=v, acc=acc: e.tensor_scalar(
                        acc[:], idxf[:], float(v), lutrep[:, v:v + 1],
                        ALU.is_equal, ALU.mult))
                else:
                    P.add("v", lambda e, S, v=v: e.tensor_scalar(
                        contrib[:], idxf[:], float(v), lutrep[:, v:v + 1],
                        ALU.is_equal, ALU.mult))
                    P.add("v", lambda e, S, acc=acc: e.tensor_tensor(
                        acc[:], acc[:], contrib[:], ALU.add))
            P.add("v", None, incs=[("sv", 1)])
            bdone = P.tok("sv")
            idx_last[par] = bdone
            if kind < 2:
                out_ap = wgu_scr[kind, ic]
            else:
                out_ap = wdn_scr[:, :, ic, :].transpose([1, 0, 2])
            wb_store[par] = dma("y", out_ap, acc[:], f"dwb{par}", waits=[bdone])
        build_done = [P.tok("dwb0"), P.tok("dwb1")]

        bankfree = [P.tok("sv"), P.tok("sv")]
        r3c_last = [("sv", 0), ("sv", 0)]
        wg_last = [("st", 0), ("st", 0)]
        wu_last = [("st", 0), ("st", 0)]
        wd_last = [("st", 0), ("st", 0)]
        y_last = [("ds0", 0), ("ds1", 0)]
        for ts in range(NTB):
            sl = slice(ts * TB, (ts + 1) * TB)
            for dc in range(NC):
                par = dc % 2
                r3_ok = dma("y", r3c[par][:], r3i[:, dc, sl], f"drc{par}",
                            waits=[r3c_last[par]])
                P.add("v", lambda e, S, par=par: e.tensor_tensor(
                    sqs[:], r3c[par][:], r3c[par][:], ALU.mult),
                    waits=[r3_ok, P.tok("st")], incs=[("sv", 1)])
                P.add("t", lambda e, S, par=par, dc=dc: e.matmul(
                    pA[0:1, :], onec_f[:], r3c[par][:],
                    start=(dc == 0), stop=(dc == NC - 1)),
                    waits=[P.tok("sv")] + ([bankfree[0], bankfree[1]] if dc == 0 else []))
                P.add("t", lambda e, S, dc=dc: e.matmul(
                    pB[0:1, :], onec_b[:], sqs[:],
                    start=(dc == 0), stop=(dc == NC - 1)), incs=[("st", 1)])
                r3c_last[par] = P.tok("st")
            P.add("s", lambda e, S: e.mul(rowm[0:1, :], pA[0:1, :], 1.0 / D),
                  waits=[P.tok("st")])
            P.add("s", lambda e, S: e.mul(rowe[0:1, :], pB[0:1, :], 1.0 / D),
                  incs=[("ss", 1)])
            P.add("v", lambda e, S: e.tensor_tensor(
                tmpf[0:1, 0:TB], rowm[0:1, :], rowm[0:1, :], ALU.mult),
                waits=[P.tok("ss")])
            P.add("v", lambda e, S: e.tensor_tensor(
                rowe[0:1, :], rowe[0:1, :], tmpf[0:1, 0:TB], ALU.subtract))
            P.add("v", lambda e, S: e.tensor_scalar_mul(rowm[0:1, :], rowm[0:1, :], -1.0),
                  incs=[("sv", 1)])
            P.add("s", lambda e, S: e.activation(rowe[0:1, :], rowe[0:1, :], AF.Ln,
                                                 bias=cstf[0:1, 0:1]), waits=[P.tok("sv")])
            P.add("s", lambda e, S: e.activation(rowe[0:1, :], rowe[0:1, :], AF.Exp,
                                                 scale=cstf[0:1, 1:2]), incs=[("ss", 1)])
            P.add("t", lambda e, S: e.matmul(pA[:], oner_f[:], rowm[0:1, :],
                                             start=True, stop=True),
                  waits=[P.tok("ss")], incs=[("st", 1)])
            P.add("v", lambda e, S: e.tensor_copy(nmurep[:], pA[:]),
                  waits=[P.tok("st")], incs=[("sv", 1)])
            P.add("t", lambda e, S: e.matmul(pB[:], oner_f[:], rowe[0:1, :],
                                             start=True, stop=True), incs=[("st", 1)])
            P.add("v", lambda e, S: e.tensor_copy(rstdrep[:], pB[:]),
                  waits=[P.tok("st")], incs=[("sv", 1)])
            rep_ok = P.tok("sv")
            bankfree = [rep_ok, rep_ok]
            for dc in range(NC):
                par = dc % 2
                r3_ok = dma("y", r3c[par][:], r3i[:, dc, sl], f"drc{par}",
                            waits=[r3c_last[par]])
                P.add("v", lambda e, S, dc=dc, par=par: e.tensor_tensor(
                    tmpf[:], r3c[par][:], nmurep[:], ALU.add),
                    waits=[r3_ok] + ([rep_ok] if dc == 0 else []))
                P.add("v", lambda e, S, dc=dc: e.tensor_tensor(
                    xln[:, dc, :], tmpf[:], rstdrep[:], ALU.mult), incs=[("sv", 1)])
                r3c_last[par] = P.tok("sv")
            xln_ok = P.tok("sv")

            for oc in range(GOC):
                par = oc % 2
                wg_ok = dma("y", wgt[par][:], wgu_scr[0, oc], f"dwg{par}",
                            waits=[wg_last[par], build_done[0], build_done[1]])
                wu_ok = dma("y", wut[par][:], wgu_scr[1, oc], f"dwu{par}",
                            waits=[wu_last[par]])
                P.add("t", None, waits=[xln_ok, wg_ok, wu_ok, bankfree[0], bankfree[1]])
                for hc in range(NC):
                    P.add("t", lambda e, S, hc=hc, par=par: e.matmul(
                        pA[:], wgt[par][:, hc, :], xln[:, hc, :],
                        start=(hc == 0), stop=(hc == NC - 1)))
                for hc in range(NC):
                    P.add("t", lambda e, S, hc=hc, par=par: e.matmul(
                        pB[:], wut[par][:, hc, :], xln[:, hc, :],
                        start=(hc == 0), stop=(hc == NC - 1)))
                P.add("t", None, incs=[("st", 1)])
                mm_ok = P.tok("st")
                wg_last[par] = mm_ok
                wu_last[par] = mm_ok
                P.add("s", lambda e, S: e.activation(silt[:], pA[:], AF.Silu),
                      waits=[mm_ok], incs=[("ss", 1)])
                bankfree[0] = P.tok("ss")
                P.add("v", lambda e, S, oc=oc: e.tensor_tensor(
                    hbuf[:, oc, :], pB[:], silt[:], ALU.mult),
                    waits=[P.tok("ss")], incs=[("sv", 1)])
                bankfree[1] = P.tok("sv")
            h_ok = P.tok("sv")

            for dco in range(NC):
                par = dco % 2
                wd_ok = dma("y", wdt[par][:], wdn_scr[dco], f"dwd{par}",
                            waits=[wd_last[par]])
                r3_ok = dma("y", r3c[par][:], r3i[:, dco, sl], f"drc{par}",
                            waits=[r3c_last[par]])
                pX = PS[par]
                P.add("t", None, waits=[h_ok, wd_ok, bankfree[par]])
                for ic in range(GOC):
                    P.add("t", lambda e, S, ic=ic, par=par, pX=pX: e.matmul(
                        pX[:], wdt[par][:, ic, :], hbuf[:, ic, :],
                        start=(ic == 0), stop=(ic == GOC - 1)))
                P.add("t", None, incs=[("st", 1)])
                wd_last[par] = P.tok("st")
                yb = yt[par]
                P.add("v", lambda e, S, pX=pX, par=par, yb=yb: e.tensor_tensor(
                    yb[:], pX[:], r3c[par][:], ALU.add),
                    waits=[P.tok("st"), r3_ok, y_last[par]], incs=[("sv", 1)])
                bankfree[par] = P.tok("sv")
                r3c_last[par] = P.tok("sv")
                y_last[par] = dma("y", y_d[:, dco, sl], yb[:], f"ds{par}",
                                  waits=[P.tok("sv")])

        P.add("y", None, waits=[P.tok("ds0"), P.tok("ds1")])
        P.emit(nc, sems)
    return nc



def kernel(**inputs):
    return kernel_fused(**inputs)


def kernel_unfused(**inputs):
    I = {k: np.asarray(v) for k, v in inputs.items()}
    x = I["L0_attn_x"]

    wshared = prep_l0attn_host(x[0], I["L0_attn_W_K"], I["L0_attn_W_Q"],
                               I["L0_attn_W_V"], I["L0_attn_W_O"])
    in_maps = []
    for b in range(NCORES):
        hp = dict(wshared)
        hp["xin"] = pack_fm(x[b])
        in_maps.append(hp)
    nc = build_l0attn()
    res = run_bass_kernel_spmd(nc, in_maps, core_ids=list(range(NCORES)))
    r1 = [res.results[b]["r1"] for b in range(NCORES)]
    xin_packed = [m["xin"] for m in in_maps]

    wshared = prep_l0ffn_host(I["L0_ffn_SU_ug"], I["L0_ffn_SU_dp"],
                              I["L0_ffn_Wscale_ug"], I["L0_ffn_Wscale_dp"],
                              I["L0_ffn_W_up"], I["L0_ffn_W_gate"], I["L0_ffn_W_down"])
    in_maps = []
    for b in range(NCORES):
        hp = dict(wshared)
        hp["r1i"] = np.asarray(r1[b])
        in_maps.append(hp)
    nc = build_l0ffn()
    res = run_bass_kernel_spmd(nc, in_maps, core_ids=list(range(NCORES)))
    r2 = [res.results[b]["r2"] for b in range(NCORES)]

    wshared = prep_l1attn_host(I["L1_attn_W_in"], I["L1_attn_conv_w"],
                               I["L1_attn_conv_b"], I["L1_attn_W_x"],
                               I["L1_attn_W_dt"], I["L1_attn_dt_bias"],
                               I["L1_attn_A_log"], I["L1_attn_D"], I["L1_attn_W_out"])
    in_maps = []
    for b in range(NCORES):
        hp = dict(wshared)
        hp["r2i"] = np.asarray(r2[b])
        in_maps.append(hp)
    nc = build_l1attn()
    res = run_bass_kernel_spmd(nc, in_maps, core_ids=list(range(NCORES)))
    r3 = [res.results[b]["r3"] for b in range(NCORES)]

    wshared = prep_l1ffn_host(I["L1_ffn_lut"], I["L1_ffn_gate_idx"],
                              I["L1_ffn_up_idx"], I["L1_ffn_down_idx"])
    in_maps = []
    for b in range(NCORES):
        hp = dict(wshared)
        hp["r3i"] = np.asarray(r3[b])
        in_maps.append(hp)
    nc = build_l1ffn()
    res = run_bass_kernel_spmd(nc, in_maps, core_ids=list(range(NCORES)))

    out = np.empty((NCORES, T, D), np.float32)
    for b in range(NCORES):
        out[b] = unpack_fm(res.results[b]["yout"])
    return out



CSLOT = {"eps": 0, "mhalf": 1, "mone": 2, "one": 3, "i64": 4, "i128": 5, "half": 6}


class Carver:

    def __init__(self, arena):
        self.arena = arena
        self.off = 0
        self.floor = 0

    def reset(self):
        self.off = self.floor

    def alloc(self, shape, dt):
        esz = {F32: 4, BF16: 2, I32: 4}[dt]
        n = 1
        for sdim in shape[1:]:
            n *= sdim
        nbytes = n * esz
        off = (self.off + 63) // 64 * 64
        self.off = off + nbytes
        assert self.off <= ARENA_BYTES, f"arena overflow {self.off}"
        ap = self.arena[:, off:off + nbytes].bitcast(dt)
        if shape[0] != 128:
            ap = ap[0:shape[0]]
        if len(shape) > 2:
            dims = " ".join(f"d{i}" for i in range(1, len(shape)))
            kw = {f"d{i}": shape[i] for i in range(1, len(shape))}
            ap = ap.rearrange(f"p ({dims}) -> p {dims}", **kw)
        return ap


class GCtx:
    pass


ARENA_BYTES = 211200


def emit_l0attn(g, dram, after=()):
    nc, P, C = g.nc, g.P, g.C
    s = g.s
    pA, pB = g.pA, g.pB
    PS = [g.pA, g.pB]
    onec_b, onec_f, oner_b, oner_f = g.onec_b, g.onec_f, g.oner_b, g.oner_f
    cstf, h128p, h128n = g.cstf, g.h128p, g.h128n
    consts_ok = g.consts_ok
    for _eng in ENGS:
        P.pending[_eng] = P.pending[_eng] + list(after)
    xin = dram["xin"]
    wq_d = dram["wq"]
    wk_d = dram["wk"]
    wv_d = dram["wv"]
    wo_d = dram["wo"]
    cm_d = dram["cm"]
    r1_d = dram["r1"]
    zscr_d = dram["zscr"]


    if True:
        rb = C.alloc([128, NC, T], BF16)
        q_s = C.alloc([128, T], BF16)
        k_s = C.alloc([128, T], BF16)
        vt4 = C.alloc([128, NC, 4, 128], BF16)
        attnT = C.alloc([128, NC, TB], BF16)
        wqb = [C.alloc([128, NC, DH], BF16) for i in range(2)]
        wkb = [C.alloc([128, NC, DH], BF16) for i in range(2)]
        wvb = [C.alloc([128, NC, 4, DH], BF16)] * 2
        wob = [C.alloc([128, NH, DH], BF16) for i in range(2)]
        zqb = [attnT]
        cm_s = C.alloc([128, 4, TB], BF16)
        rrep = C.alloc([128, T], BF16)
        nmu_bf = C.alloc([1, T], BF16)
        mhat_bf = C.alloc([1, T], BF16)
        rowa = C.alloc([1, T], F32)
        lrec_bf = rowa[0:1, 512:768].bitcast(BF16)
        rowb = C.alloc([1, T], F32)
        kmax_s = C.alloc([1, 8], F32)
        rstd_cols = C.alloc([128, NC], F32)
        wbar = C.alloc([1, 384], BF16)
        wbar4 = C.alloc([1, 512], BF16)
        scf = [C.alloc([128, TB], F32) for i in range(2)]
        rep_sb = C.alloc([128, TB], F32)
        mrep = C.alloc([128, TB], F32)
        zbs = [C.alloc([128, TB], BF16) for i in range(2)]
        r0t = [C.alloc([128, TB], F32)] * 2
        r1t = [C.alloc([128, TB], F32)] * 2

        def dma(q, out_ap, in_ap, waits=(), sem="dg"):
            P.add(q, lambda e, S, o=out_ap, i=in_ap: e.dma_start(out=o, in_=i),
                  waits=waits, incs=[(sem, 16)])
            return (sem, P.now(sem))

        for i in range(4):
            rb_ok = dma("g", rb[:, 4 * i:4 * (i + 1), :], xin[:, 4 * i:4 * (i + 1), :], sem=s("dr"))
        cm_ok = dma("g", cm_s[:], cm_d[:], sem=s("dcm"))
        wld = {0: [dma("g", wqb[0][:], wq_d[0], sem=s("dw0")),
                   dma("g", wkb[0][:], wk_d[0], sem=s("dw0")),
                   dma("g", wvb[0][:], wv_d[0:4].rearrange("h p c d -> p c h d"),
                       sem=s("dw0"))]}

        for tb in range(NTB):
            sl = slice(tb * TB, (tb + 1) * TB)
            for dc in range(NC):
                P.add("v", lambda e, S, dc=dc, sl=sl: e.tensor_tensor(
                    attnT[:, dc, :], rb[:, dc, sl], rb[:, dc, sl], ALU.mult),
                    waits=[rb_ok, P.tok("st")] if dc == 0 else [])
            P.add("v", None, incs=[("sv", 1)])
            sq_ok = P.tok("sv")
            for dc in range(NC):
                P.add("t", lambda e, S, dc=dc, sl=sl: e.matmul(
                    pA[0:1, :], onec_b[:], rb[:, dc, sl],
                    start=(dc == 0), stop=(dc == NC - 1)),
                    waits=[rb_ok, consts_ok, P.tok("ss")] if dc == 0 else [])
            for dc in range(NC):
                P.add("t", lambda e, S, dc=dc: e.matmul(
                    pB[0:1, :], onec_b[:], attnT[:, dc, :],
                    start=(dc == 0), stop=(dc == NC - 1)),
                    waits=[sq_ok] if dc == 0 else [])
            P.add("t", None, incs=[("st", 1)])
            mm_ok = P.tok("st")
            P.add("s", lambda e, S, sl=sl: e.mul(rowa[0:1, sl], pA[0:1, :], 1.0 / D),
                  waits=[mm_ok])
            P.add("s", lambda e, S, sl=sl: e.mul(rowb[0:1, sl], pB[0:1, :], 1.0 / D),
                  incs=[("ss", 1)])
        P.barrier(("t", "v", "s"))
        P.add("v", lambda e, S: e.tensor_scalar_mul(nmu_bf[0:1, :], rowa[0:1, :], -1.0))
        P.add("v", lambda e, S: e.tensor_tensor(
            rowa[0:1, :], rowa[0:1, :], rowa[0:1, :], ALU.mult))
        P.add("v", lambda e, S: e.tensor_tensor(
            rowb[0:1, :], rowb[0:1, :], rowa[0:1, :], ALU.subtract), incs=[("sv", 1)])
        P.add("s", lambda e, S: e.activation(rowa[0:1, :], rowb[0:1, :], AF.Ln,
                                             bias=cstf[0:1, 0:1]),
              waits=[P.tok("sv"), consts_ok])
        P.add("s", lambda e, S: e.activation(rowb[0:1, :], rowa[0:1, :], AF.Exp,
                                             scale=cstf[0:1, 1:2]),
              incs=[("ss", 1)])
        rstd_ok = P.tok("ss")
        for tb in range(NTB):
            sl = slice(tb * TB, (tb + 1) * TB)
            pX = PS[tb % 2]
            P.add("t", lambda e, S, sl=sl, pX=pX: e.matmul(
                pX[:], oner_f[:], rowb[0:1, sl], start=True, stop=True),
                waits=[rstd_ok], incs=[("st", 1)])
            P.add("v", lambda e, S, sl=sl, pX=pX: e.tensor_copy(rrep[:, sl], pX[:]),
                  waits=[P.tok("st")], incs=[("sv", 1)])
        P.add("t", None, waits=[P.tok("sv")])
        for tc in range(NC):
            P.add("t", lambda e, S, tc=tc: e.matmul(
                pA[:, tc:tc + 1], rowb[0:1, tc * 128:(tc + 1) * 128], oner_f[0:1, 0:1],
                start=True, stop=True))
        P.add("t", None, incs=[("st", 1)])
        P.add("v", lambda e, S: e.tensor_copy(rstd_cols[:], pA[:, 0:NC]),
              waits=[P.tok("st")], incs=[("sv", 1)])
        P.barrier(("t", "v", "s"))

        bankfree = [P.tok("sv"), P.tok("sv")]
        attnT_free = P.tok("st")
        zb_stored = [(s("dzs0"), 0), (s("dzs1"), 0)]

        w_consumed = {}
        vt4_used = ("st", 0)
        vt4_used_next = ("st", 0)
        for h in range(NH):
            buf = h % 2
            wq, wk, wv = wqb[buf], wkb[buf], wvb[0]
            if h > 0:
                wwait = [w_consumed[h - 2]] if h >= 2 else []
                wld[h] = [dma("g", wq[:], wq_d[h], waits=wwait, sem=s(f"dw{buf}")),
                          dma("g", wk[:], wk_d[h], sem=s(f"dw{buf}"))] + (
                    [dma("g", wvb[0][:],
                         wv_d[h:h + 4].rearrange("h p c d -> p c h d"),
                         waits=[vt4_used_next] if h >= 4 else [],
                         sem=s(f"dw{buf}"))] if h % 4 == 0 else [])
            w_ok = wld[h][-1]

            P.add("t", None, waits=[w_ok, bankfree[0]])
            for j, wt in enumerate((wq, wk)):
                for dc in range(NC):
                    P.add("t", lambda e, S, j=j, wt=wt, dc=dc: e.matmul(
                        pA[0:1, j * 128:(j + 1) * 128], onec_b[:], wt[:, dc, :],
                        start=(dc == 0), stop=(dc == NC - 1)))
            P.add("t", None, incs=[("st", 1)])
            P.add("s", lambda e, S: e.copy(wbar[0:1, 0:256], pA[0:1, 0:256]),
                  waits=[P.tok("st")], incs=[("ss", 1)])
            bankfree[0] = P.tok("ss")
            wbar_ok = P.tok("ss")

            for wb_i, wt, dst in ((0, wq, q_s), (1, wk, k_s)):
                for tb in range(NTB):
                    sl = slice(tb * TB, (tb + 1) * TB)
                    bk = tb % 2
                    pX = PS[bk]
                    P.add("t", None, waits=[bankfree[bk], wbar_ok])
                    for dc in range(NC):
                        P.add("t", lambda e, S, dc=dc, sl=sl, wt=wt, pX=pX: e.matmul(
                            pX[:], wt[:, dc, :], rb[:, dc, sl],
                            start=(dc == 0), stop=False))
                    P.add("t", lambda e, S, sl=sl, pX=pX, wb_i=wb_i: e.matmul(
                        pX[:], wbar[0:1, wb_i * 128:(wb_i + 1) * 128], nmu_bf[0:1, sl],
                        start=False, stop=True), incs=[("st", 1)])
                    P.add("v", lambda e, S, sl=sl, pX=pX, dst=dst: e.tensor_tensor(
                        dst[:, sl], pX[:], rrep[:, sl], ALU.mult),
                        waits=[P.tok("st")], incs=[("sv", 1)])
                    bankfree[bk] = P.tok("sv")
                    if getattr(g, "fill", None):
                        g.fill(5)

            vt_s = vt4[:, :, h % 4, :]
            if h % 4 == 0:
                pass
            if h % 4 == 0:
                P.add("t", None, waits=[bankfree[1]])
                for hh in range(4):
                    for dc in range(NC):
                        P.add("t", lambda e, S, hh=hh, dc=dc, wv=wv: e.matmul(
                            pB[0:1, hh * 128:(hh + 1) * 128], onec_b[:],
                            wv[:, dc, hh, :],
                            start=(dc == 0), stop=(dc == NC - 1)))
                P.add("t", None, incs=[("st", 1)])
                P.add("s", lambda e, S: e.copy(wbar4[0:1, :], pB[0:1, 0:512]),
                      waits=[P.tok("st")], incs=[("ss", 1)])
                bankfree[1] = P.tok("ss")
                wbar4_ok = P.tok("ss")
                for tc in range(NC):
                    csl = slice(tc * 128, (tc + 1) * 128)
                    bk = tc % 2
                    pX = PS[bk]
                    P.add("t", None, waits=[bankfree[bk], wbar4_ok] +
                          ([vt4_used] if tc == 0 else []))
                    for dc in range(NC):
                        P.add("t", lambda e, S, dc=dc, csl=csl, pX=pX, wv=wv: e.matmul(
                            pX[:], rb[:, dc, csl],
                            wv[:, dc, :, :].rearrange("p h d -> p (h d)"),
                            start=(dc == 0), stop=False))
                    P.add("t", lambda e, S, csl=csl, pX=pX: e.matmul(
                        pX[:], nmu_bf[0:1, csl], wbar4[0:1, :],
                        start=False, stop=True), incs=[("st", 1)])
                    P.add("v", lambda e, S, tc=tc, pX=pX: e.tensor_scalar(
                        vt4[:, tc, :, :].rearrange("p h d -> p (h d)"), pX[:],
                        rstd_cols[:, tc:tc + 1], None, ALU.mult),
                        waits=[P.tok("st")], incs=[("sv", 1)])
                    bankfree[bk] = P.tok("sv")
                    if getattr(g, "fill", None):
                        g.fill(5)
                vt4_used_next = P.tok("st")
            w_consumed[h] = P.tok("st")

            for src, row in ((q_s, rowa), (k_s, rowb)):
                for tb in range(NTB):
                    sl = slice(tb * TB, (tb + 1) * TB)
                    bk = tb % 2
                    pX = PS[bk]
                    P.add("v", lambda e, S, sl=sl, src=src: e.tensor_tensor(
                        scf[0][:], src[:, sl], src[:, sl], ALU.mult), incs=[("sv", 1)])
                    P.add("t", lambda e, S, pX=pX: e.matmul(
                        pX[0:1, :], onec_f[:], scf[0][:], start=True, stop=True),
                        waits=[P.tok("sv"), bankfree[bk]], incs=[("st", 1)])
                    P.add("s", lambda e, S, sl=sl, pX=pX, row=row: e.copy(
                        row[0:1, sl], pX[0:1, :]),
                        waits=[P.tok("st")], incs=[("ss", 1)])
                    bankfree[bk] = P.tok("ss")
                    P.add("v", None, waits=[P.tok("st")])
            P.barrier(("t", "v", "s"))
            P.add("v", lambda e, S: e.reduce_max(
                kmax_s[0:1, 0:1], rowb[0:1, :], axis=AX.X), incs=[("sv", 1)])
            P.add("s", lambda e, S: e.activation(
                kmax_s[0:1, 1:2], kmax_s[0:1, 0:1], AF.Ln, scale=cstf[0:1, 5:6]),
                waits=[P.tok("sv")])
            P.add("s", lambda e, S: e.activation(
                kmax_s[0:1, 2:3], kmax_s[0:1, 1:2], AF.Exp, scale=cstf[0:1, 6:7]))
            P.add("s", lambda e, S: e.activation(rowa[0:1, :], rowa[0:1, :], AF.Ln))
            P.add("s", lambda e, S: e.activation(
                rowa[0:1, :], rowa[0:1, :], AF.Exp, scale=cstf[0:1, 6:7]),
                incs=[("ss", 1)])
            P.add("v", lambda e, S: e.tensor_scalar(
                mhat_bf[0:1, :], rowa[0:1, :], kmax_s[0:1, 2:3], 0.05, ALU.mult, ALU.add),
                waits=[P.tok("ss")], incs=[("sv", 1)])
            mhat_ok = P.tok("sv")

            for qb in range(NTB):
                sl = slice(qb * TB, (qb + 1) * TB)
                nkc = 4 * qb + 4
                P.add("t", lambda e, S, sl=sl: e.matmul(
                    pB[:], oner_b[:], mhat_bf[0:1, sl], start=True, stop=True),
                    waits=[mhat_ok, bankfree[1]], incs=[("st", 1)])
                P.add("s", lambda e, S: e.copy(mrep[:], pB[:]),
                      waits=[P.tok("st")], incs=[("ss", 1)])
                bankfree[1] = P.tok("ss")
                mrep_ok = P.tok("ss")
                scf_free = [P.tok("ss"), P.tok("ss")]
                for kc in range(nkc):
                    bk = kc % 2
                    pX = PS[bk]
                    P.add("t", lambda e, S, kc=kc, sl=sl, pX=pX: e.matmul(
                        pX[:], k_s[:, kc * 128:(kc + 1) * 128], q_s[:, sl],
                        start=True, stop=True),
                        waits=[bankfree[bk]], incs=[("st", 1)])
                    mm_tok = P.tok("st")
                    sb = scf[kc % 2]
                    P.add("v", lambda e, S, pX=pX, sb=sb: e.scalar_tensor_tensor(
                        sb[:], pX[:], ISQRT_DH, mrep[:], ALU.mult, ALU.subtract),
                        waits=[mm_tok, mrep_ok, scf_free[kc % 2]], incs=[("sv", 1)])
                    bankfree[bk] = P.tok("sv")
                    dd = kc - 4 * qb
                    if dd >= 0:
                        P.add("v", lambda e, S, sb=sb, dd=dd: e.tensor_tensor(
                            sb[:], sb[:], cm_s[:, dd, :], ALU.add),
                            waits=[cm_ok], incs=[("sv", 1)])
                    P.add("s", lambda e, S, kc=kc, sb=sb: e.activation(
                        attnT[:, kc, :], sb[:], AF.Exp),
                        waits=[P.tok("sv"), attnT_free], incs=[("ss", 1)])
                    scf_free[kc % 2] = P.tok("ss")
                exp_ok = P.tok("ss")
                P.add("t", None, waits=[exp_ok, bankfree[0], bankfree[1]])
                for kc in range(nkc):
                    P.add("t", lambda e, S, kc=kc, nkc=nkc, vt_s=vt_s: e.matmul(
                        pA[:], vt_s[:, kc, :], attnT[:, kc, :],
                        start=(kc == 0), stop=(kc == nkc - 1)))
                for kc in range(nkc):
                    P.add("t", lambda e, S, kc=kc, nkc=nkc: e.matmul(
                        pB[0:1, :], onec_b[:], attnT[:, kc, :],
                        start=(kc == 0), stop=(kc == nkc - 1)))
                P.add("t", None, incs=[("st", 1)])
                zl_ok = P.tok("st")
                attnT_free = zl_ok
                P.add("v", lambda e, S: e.reciprocal(rowb[0:1, 0:TB], pB[0:1, :]),
                      waits=[zl_ok])
                P.add("v", lambda e, S: e.tensor_copy(lrec_bf[0:1, :], rowb[0:1, 0:TB]),
                      incs=[("sv", 1)])
                P.add("t", lambda e, S: e.matmul(
                    pB[:], oner_b[:], lrec_bf[0:1, :], start=True, stop=True),
                    waits=[P.tok("sv")], incs=[("st", 1)])
                P.add("s", lambda e, S: e.copy(rep_sb[:], pB[:]),
                      waits=[P.tok("st")], incs=[("ss", 1)])
                bankfree[1] = P.tok("ss")
                zb = zbs[qb % 2]
                P.add("v", lambda e, S, zb=zb: e.tensor_tensor(
                    zb[:], pA[:], rep_sb[:], ALU.mult),
                    waits=[P.tok("ss"), zb_stored[qb % 2]], incs=[("sv", 1)])
                bankfree[0] = P.tok("sv")
                zb_stored[qb % 2] = dma("y", zscr_d[h][:, sl], zb[:],
                                        waits=[P.tok("sv")], sem=s(f"dzs{qb % 2}"))
            if h % 4 == 3:
                vt4_used = P.tok("st")
            P.barrier(("t", "v", "s"))

        zq_used = {}
        wo_last = [("st", 0), ("st", 0)]
        r0_last = [("sv", 0), ("sv", 0)]
        r1_last = [(s("ds0"), 0), (s("ds1"), 0)]
        for qb in range(NTB):
            sl = slice(qb * TB, (qb + 1) * TB)
            zbuf = zqb[0]
            zsrc = zscr_d[:, :, sl].rearrange("h p t -> p h t")
            zw = [dma("g", zbuf[:, 4 * i:4 * (i + 1), :], zsrc[:, 4 * i:4 * (i + 1), :],
                      waits=[P.tok(s("dzs0")), P.tok(s("dzs1")), P.tok("st")] +
                            ([zq_used[qb - 1]] if qb >= 1 else []),
                      sem=s("dzq"))
                  for i in range(4)]
            z_ok = zw[-1]
            for dc in range(NC):
                par = dc % 2
                wbuf = wob[par]
                wo_ok = dma("g", wbuf[:], wo_d[dc], waits=[wo_last[par]], sem=s(f"dwo{par}"))
                r0buf = r0t[par]
                r0_ok = dma("y", r0buf[:], xin[:, dc, sl], waits=[r0_last[par]], sem=s(f"da{par}"))
                pX = PS[par]
                P.add("t", None, waits=[z_ok, wo_ok, bankfree[par]])
                for hh in range(NH):
                    P.add("t", lambda e, S, hh=hh, wbuf=wbuf, zbuf=zbuf, pX=pX: e.matmul(
                        pX[:], wbuf[:, hh, :], zbuf[:, hh, :],
                        start=(hh == 0), stop=(hh == NH - 1)))
                P.add("t", None, incs=[("st", 1)])
                wo_last[par] = P.tok("st")
                r1b = r1t[par]
                P.add("v", lambda e, S, pX=pX, r0buf=r0buf, r1b=r1b: e.tensor_tensor(
                    r1b[:], pX[:], r0buf[:], ALU.add),
                    waits=[P.tok("st"), r0_ok, r1_last[par]], incs=[("sv", 1)])
                bankfree[par] = P.tok("sv")
                r0_last[par] = P.tok("sv")
                r1_last[par] = dma("y", r1_d[:, dc, sl], r1b[:], waits=[P.tok("sv")], sem=s(f"ds{par}"))
                if getattr(g, "fill", None):
                    g.fill(4)
            zq_used[qb] = P.tok("st")

def emit_l0ffn(g, dram, after=()):
    nc, P, C = g.nc, g.P, g.C
    s = g.s
    pA, pB = g.pA, g.pB
    PS = [g.pA, g.pB]
    onec_b, onec_f, oner_b, oner_f = g.onec_b, g.onec_f, g.oner_b, g.oner_f
    cstf, h128p, h128n = g.cstf, g.h128p, g.h128n
    consts_ok = g.consts_ok
    for _eng in ENGS:
        P.pending[_eng] = P.pending[_eng] + list(after)
    r1i = dram["r1i"]
    h128_d = dram["h128"]
    suug_d = dram["su_ug"]
    sudp_d = dram["su_dp"]
    wscu_d = dram["wsc_up"]
    wscg_d = dram["wsc_gt"]
    wscd_d = dram["wsc_dn"]
    wup_d = dram["wup"]
    wgt_d = dram["wgt"]
    wdn_d = dram["wdn"]
    r2_d = dram["r2"]


    S1 = float(1.0 / np.sqrt(2048.0))
    S2 = float(1.0 / np.sqrt(8192.0))

    if True:
        r1c = [C.alloc([128, TB], F32) for i in range(2)]
        xln = C.alloc([128, NC, TB], BF16)
        t1 = C.alloc([128, NC, TB], BF16)
        hbuf = C.alloc([128, 64, TB], BF16)
        wub = [C.alloc([128, NC, 128], BF16) for i in range(2)]
        wgb = [C.alloc([128, NC, 128], BF16) for i in range(2)]
        wdb = [C.alloc([128, 64, 128], BF16) for i in range(2)]
        su_ug = C.alloc([128, NC], F32)
        su_dp = C.alloc([128, 64], F32)
        wscu = C.alloc([128, 64], F32)
        wscg = C.alloc([128, 64], F32)
        wscd = C.alloc([128, NC], F32)
        sqs = C.alloc([128, TB], BF16)
        rowm = C.alloc([1, TB], F32)
        rowe = C.alloc([1, TB], F32)
        nmurep = C.alloc([128, TB], F32)
        rstdrep = C.alloc([128, TB], F32)
        dAb = [nmurep, rstdrep]
        tmpf = C.alloc([128, TB], F32)
        upt = [C.alloc([128, TB], F32) for i in range(2)]
        silt = [C.alloc([128, TB], F32) for i in range(2)]
        btmp = C.alloc([128, TB], BF16)
        r2t = [C.alloc([128, TB], F32) for i in range(2)]

        def dma(q, out_ap, in_ap, sem, waits=()):
            P.add(q, lambda e, S, o=out_ap, i=in_ap: e.dma_start(out=o, in_=i),
                  waits=waits, incs=[(sem, 16)])
            return (sem, P.now(sem))

        for t_s, t_d in ((h128p, h128_d), (su_ug, suug_d), (su_dp, sudp_d),
                         (wscu, wscu_d), (wscg, wscg_d), (wscd, wscd_d)):
            cst_ok = dma("g", t_s[:], t_d[:], s("dcst"))
        P.add("v", lambda e, S: e.tensor_scalar_mul(wscu[:], wscu[:], S1),
              waits=[cst_ok, consts_ok])
        P.add("v", lambda e, S: e.tensor_scalar_mul(wscg[:], wscg[:], S1))
        P.add("v", lambda e, S: e.tensor_scalar_mul(wscd[:], wscd[:], S2),
              incs=[("sv", 1)])
        P.barrier(("v",))

        bankfree = [P.tok("sv"), P.tok("sv")]
        wu_last = [("st", 0), ("st", 0)]
        wg_last = [("st", 0), ("st", 0)]
        wd_last = [("st", 0), ("st", 0)]
        r2_last = [(s("ds0"), 0), (s("ds1"), 0)]

        r1c_last = [("sv", 0), ("sv", 0)]
        for ts in range(NTB):
            sl = slice(ts * TB, (ts + 1) * TB)

            for dc in range(NC):
                par = dc % 2
                r1_ok = dma("g", r1c[par][:], r1i[:, dc, sl], s(f"dr1{par}"),
                            waits=[r1c_last[par]])
                P.add("v", lambda e, S, par=par: e.tensor_tensor(
                    sqs[:], r1c[par][:], r1c[par][:], ALU.mult),
                    waits=[r1_ok, P.tok("st")], incs=[("sv", 1)])
                P.add("t", lambda e, S, par=par, dc=dc: e.matmul(
                    pA[0:1, :], onec_f[:], r1c[par][:],
                    start=(dc == 0), stop=(dc == NC - 1)),
                    waits=[P.tok("sv")] + ([bankfree[0], bankfree[1]] if dc == 0 else []))
                P.add("t", lambda e, S, dc=dc: e.matmul(
                    pB[0:1, :], onec_b[:], sqs[:],
                    start=(dc == 0), stop=(dc == NC - 1)), incs=[("st", 1)])
                r1c_last[par] = P.tok("st")
            sq_mm_ok = P.tok("st")
            sx_ok = P.tok("st")
            P.add("s", lambda e, S: e.mul(rowm[0:1, :], pA[0:1, :], 1.0 / D),
                  waits=[sx_ok])
            P.add("s", lambda e, S: e.mul(rowe[0:1, :], pB[0:1, :], 1.0 / D),
                  incs=[("ss", 1)])
            st_ok = P.tok("ss")
            P.add("v", lambda e, S: e.tensor_tensor(
                tmpf[0:1, 0:TB], rowm[0:1, :], rowm[0:1, :], ALU.mult), waits=[st_ok])
            P.add("v", lambda e, S: e.tensor_tensor(
                rowe[0:1, :], rowe[0:1, :], tmpf[0:1, 0:TB], ALU.subtract))
            P.add("v", lambda e, S: e.tensor_scalar_mul(rowm[0:1, :], rowm[0:1, :], -1.0),
                  incs=[("sv", 1)])
            P.add("s", lambda e, S: e.activation(rowe[0:1, :], rowe[0:1, :], AF.Ln,
                                                 bias=cstf[0:1, 0:1]), waits=[P.tok("sv")])
            P.add("s", lambda e, S: e.activation(rowe[0:1, :], rowe[0:1, :], AF.Exp,
                                                 scale=cstf[0:1, 1:2]), incs=[("ss", 1)])
            P.add("t", lambda e, S: e.matmul(pA[:], oner_f[:], rowm[0:1, :],
                                             start=True, stop=True),
                  waits=[P.tok("ss"), bankfree[0]], incs=[("st", 1)])
            P.add("v", lambda e, S: e.tensor_copy(nmurep[:], pA[:]),
                  waits=[P.tok("st")], incs=[("sv", 1)])
            bankfree[0] = P.tok("sv")
            P.add("t", lambda e, S: e.matmul(pB[:], oner_f[:], rowe[0:1, :],
                                             start=True, stop=True),
                  waits=[P.tok("ss"), bankfree[1], sq_mm_ok], incs=[("st", 1)])
            P.add("v", lambda e, S: e.tensor_copy(rstdrep[:], pB[:]),
                  waits=[P.tok("st")], incs=[("sv", 1)])
            bankfree[1] = P.tok("sv")
            rep_ok = P.tok("sv")

            for dc in range(NC):
                par = dc % 2
                r1_ok = dma("g", r1c[par][:], r1i[:, dc, sl], s(f"dr1{par}"),
                            waits=[r1c_last[par]])
                P.add("v", lambda e, S, dc=dc, par=par: e.tensor_tensor(
                    tmpf[:], r1c[par][:], nmurep[:], ALU.add),
                    waits=[r1_ok] + ([rep_ok] if dc == 0 else []))
                P.add("v", lambda e, S, dc=dc: e.scalar_tensor_tensor(
                    xln[:, dc, :], tmpf[:], su_ug[:, dc:dc + 1], rstdrep[:],
                    ALU.mult, ALU.mult), incs=[("sv", 1)])
                r1c_last[par] = P.tok("sv")
            xln_ok = P.tok("sv")

            for ec in range(NC):
                bk = ec % 2
                pX = PS[bk]
                P.add("t", None, waits=[xln_ok, bankfree[bk]])
                jcs = sorted(range(NC), key=lambda j: _h16sign(ec, j) < 0)
                for jj, jc in enumerate(jcs):
                    hs = h128p if _h16sign(ec, jc) > 0 else h128n
                    P.add("t", lambda e, S, jc=jc, hs=hs, pX=pX, jj=jj: e.matmul(
                        pX[:], hs[:], xln[:, jc, :],
                        start=(jj == 0), stop=(jj == NC - 1)))
                P.add("t", None, incs=[("st", 1)])
                P.add("v", lambda e, S, ec=ec, pX=pX: e.tensor_copy(t1[:, ec, :], pX[:]),
                      waits=[P.tok("st")], incs=[("sv", 1)])
                bankfree[bk] = P.tok("sv")
                if getattr(g, "fill", None):
                    g.fill(6)
            t1_ok = P.tok("sv")

            hw_done = {}
            for oc in range(64):
                par = oc % 2
                wu_ok = dma("g", wub[par][:], wup_d[oc], s(f"dwu{par}"), waits=[wu_last[par]])
                wg_ok = dma("g", wgb[par][:], wgt_d[oc], s(f"dwg{par}"), waits=[wg_last[par]])
                upb = upt[oc % 2]
                slb = silt[oc % 2]
                for dc in range(NC):
                    P.add("t", lambda e, S, dc=dc, par=par: e.matmul(
                        pA[:], wub[par][:, dc, :], t1[:, dc, :],
                        start=(dc == 0), stop=(dc == NC - 1)),
                        waits=([t1_ok, wu_ok, bankfree[0]] if dc == 0 else []))
                P.add("t", None, incs=[("st", 1)])
                up_mm = P.tok("st")
                for dc in range(NC):
                    P.add("t", lambda e, S, dc=dc, par=par: e.matmul(
                        pB[:], wgb[par][:, dc, :], t1[:, dc, :],
                        start=(dc == 0), stop=(dc == NC - 1)),
                        waits=([wg_ok, bankfree[1]] if dc == 0 else []))
                P.add("t", None, incs=[("st", 1)])
                gt_mm = P.tok("st")
                wu_last[par] = gt_mm
                wg_last[par] = gt_mm
                P.add("v", lambda e, S, oc=oc, upb=upb: e.tensor_scalar(
                    upb[:], pA[:], wscu[:, oc:oc + 1], None, ALU.mult),
                    waits=[up_mm, hw_done.get(oc - 2, ("sv", 0))], incs=[("sv", 1)])
                bankfree[0] = P.tok("sv")
                P.add("s", lambda e, S, oc=oc, slb=slb: e.activation(
                    slb[:], pB[:], AF.Silu, scale=wscg[:, oc:oc + 1]),
                    waits=[gt_mm, hw_done.get(oc - 2, ("sv", 0))], incs=[("ss", 1)])
                bankfree[1] = P.tok("ss")
                P.add("v", lambda e, S, oc=oc, slb=slb, upb=upb: e.scalar_tensor_tensor(
                    hbuf[:, oc, :], slb[:], su_dp[:, oc:oc + 1], upb[:],
                    ALU.mult, ALU.mult),
                    waits=[P.tok("ss")], incs=[("sv", 1)])
                hw_done[oc] = P.tok("sv")
                if getattr(g, "fill", None):
                    g.fill(6)
            h_ok = P.tok("sv")

            for stg in range(6):
                stride = 32 >> stg
                for base in range(0, 64, 2 * stride):
                    for i in range(base, base + stride):
                        j = i + stride
                        P.add("v", lambda e, S, i=i, j=j: e.tensor_tensor(
                            btmp[:], hbuf[:, i, :], hbuf[:, j, :], ALU.subtract))
                        P.add("v", lambda e, S, i=i, j=j: e.tensor_tensor(
                            hbuf[:, i, :], hbuf[:, i, :], hbuf[:, j, :], ALU.add))
                        P.add("v", lambda e, S, j=j: e.tensor_copy(
                            hbuf[:, j, :], btmp[:]))
            P.add("v", None, incs=[("sv", 1)])
            bfl_ok = P.tok("sv")
            for oc in range(64):
                bk = oc % 2
                pX = PS[bk]
                P.add("t", lambda e, S, oc=oc, pX=pX: e.matmul(
                    pX[:], h128p[:], hbuf[:, oc, :], start=True, stop=True),
                    waits=[bfl_ok, bankfree[bk]], incs=[("st", 1)])
                P.add("v", lambda e, S, oc=oc, pX=pX: e.tensor_copy(
                    hbuf[:, oc, :], pX[:]),
                    waits=[P.tok("st")], incs=[("sv", 1)])
                bankfree[bk] = P.tok("sv")
            t2_ok = P.tok("sv")

            for dco in range(NC):
                par = dco % 2
                wd_ok = dma("g", wdb[par][:], wdn_d[dco], s(f"dwd{par}"), waits=[wd_last[par]])
                pX = PS[par]
                P.add("t", None, waits=[t2_ok, wd_ok, bankfree[par]])
                for ic in range(64):
                    P.add("t", lambda e, S, ic=ic, par=par, pX=pX: e.matmul(
                        pX[:], wdb[par][:, ic, :], hbuf[:, ic, :],
                        start=(ic == 0), stop=(ic == 63)))
                P.add("t", None, incs=[("st", 1)])
                wd_last[par] = P.tok("st")
                rb2 = r2t[par]
                r1_ok = dma("g", r1c[par][:], r1i[:, dco, sl], s(f"dr1{par}"),
                            waits=[r1c_last[par]])
                P.add("v", lambda e, S, dco=dco, pX=pX, rb2=rb2, par=par: e.scalar_tensor_tensor(
                    rb2[:], pX[:], wscd[:, dco:dco + 1], r1c[par][:],
                    ALU.mult, ALU.add),
                    waits=[P.tok("st"), r2_last[par], r1_ok], incs=[("sv", 1)])
                r1c_last[par] = P.tok("sv")
                bankfree[par] = P.tok("sv")
                r2_last[par] = dma("y", r2_d[:, dco, sl], rb2[:], s(f"ds{par}"),
                                   waits=[P.tok("sv")])
                if getattr(g, "fill", None):
                    g.fill(20)

def emit_l1attn(g, dram, after=()):
    nc, P, C = g.nc, g.P, g.C
    s = g.s
    pA, pB = g.pA, g.pB
    PS = [g.pA, g.pB]
    onec_b, onec_f, oner_b, oner_f = g.onec_b, g.onec_f, g.oner_b, g.oner_f
    cstf, h128p, h128n = g.cstf, g.h128p, g.h128n
    consts_ok = g.consts_ok
    for _eng in ENGS:
        P.pending[_eng] = P.pending[_eng] + list(after)
    r2i = dram["r2i"]
    win_d = dram["win"]
    convw_d = dram["convw"]
    convb_d = dram["convb"]
    wxdt_d = dram["wxdt"]
    wxB_d = dram["wxB"]
    wxC_d = dram["wxC"]
    wdt_d = dram["wdt"]
    dtb_d = dram["dtb"]
    alog_d = dram["alog"]
    dvec_d = dram["dvec"]
    wout_d = dram["wout"]
    h128_d = dram["h128"]
    e16_d = dram["e16"]
    r3_d = dram["r3"]


    if True:
        scn = C.alloc([128, 16, TB], BF16)
        xln = C.alloc([128, NC, TB], BF16)
        xconv = C.alloc([128, DIC, 3 + TB], BF16)
        halo3 = C.alloc([128, DIC, 3], BF16)
        dA_all = xconv.rearrange("p c t -> p (c t)").bitcast(F32)[
            :, 0:16 * TB].rearrange("p (n t) -> p n t", t=TB)
        zsil = C.alloc([128, DIC, TB], BF16)
        ubuf = C.alloc([128, DIC, TB], BF16)
        crep = C.alloc([128, 16, TB], BF16)
        brep = xln
        winb = [C.alloc([128, NC, 128], BF16) for i in range(2)]
        woutb = C.alloc([128, DIC, 128], BF16)
        wxw = C.alloc([128, DIC, 128], BF16)
        wxB = C.alloc([128, DIC, 16], BF16)
        wxC = C.alloc([128, DIC, 16], BF16)
        convw = C.alloc([128, DIC, 4], F32)
        convb = C.alloc([128, DIC], F32)
        dtb = C.alloc([128, DIC], F32)
        negA = C.alloc([128, DIC, 16], F32)
        dvec = C.alloc([128, DIC], F32)
        carry = C.alloc([128, DIC, 16], F32)
        e16 = C.alloc([16, 16 * 128], BF16)
        bc_s = C.alloc([16, TB], BF16)
        dtr = C.alloc([128, TB], BF16)
        delta = C.alloc([128, TB], F32)
        dub = C.alloc([128, TB], BF16)
        delb = C.alloc([128, TB], BF16)
        r2c = [C.alloc([128, TB], F32) for i in range(2)]
        rowm = C.alloc([1, TB], F32)
        rowe = C.alloc([1, TB], F32)
        nmurep = C.alloc([128, TB], F32)
        rstdrep = C.alloc([128, TB], F32)
        dAb = [nmurep, rstdrep]
        tmpf = C.alloc([128, TB], F32)
        tmpf2 = C.alloc([128, TB], F32)
        btmp = C.alloc([128, TB], BF16)
        sqs = btmp
        r3t = [C.alloc([128, TB], F32) for i in range(2)]

        def dma(q, out_ap, in_ap, sem, waits=()):
            P.add(q, lambda e, S, o=out_ap, i=in_ap: e.dma_start(out=o, in_=i),
                  waits=waits, incs=[(sem, 16)])
            return (sem, P.now(sem))

        for t_s, t_d in ((wxB, wxB_d), (wxC, wxC_d),
                         (convw, convw_d), (convb, convb_d), (dtb, dtb_d),
                         (negA, alog_d), (dvec, dvec_d), (e16, e16_d),
                         (h128p, h128_d)):
            cst_ok = dma("g", t_s[:], t_d[:], s("dcst"))
        P.add("g", lambda e, S: e.memset(carry[:], 0.0))
        for dc in range(DIC):
            P.add("g", lambda e, S, dc=dc: e.memset(xconv[:, dc, 0:3], 0.0))
        P.add("s", lambda e, S: e.activation(negA[:], negA[:], AF.Exp),
              waits=[cst_ok, consts_ok], incs=[("ss", 1)])
        P.add("v", lambda e, S: e.tensor_scalar_mul(negA[:], negA[:], -1.0),
              waits=[P.tok("ss")], incs=[("sv", 1)])
        P.barrier(("t", "v", "s"))

        bankfree = [P.tok("sv"), P.tok("sv")]
        win_last = [("st", 0), ("st", 0)]
        wxw_last = ("st", 0)
        wout_last = ("st", 0)
        r2c_last = [("sv", 0), ("sv", 0)]
        r3_last = [(s("ds0"), 0), (s("ds1"), 0)]

        for ts in range(NTB):
            sl = slice(ts * TB, (ts + 1) * TB)

            if ts > 0:
                for dc in range(DIC):
                    P.add("v", lambda e, S, dc=dc: e.tensor_copy(
                        xconv[:, dc, 0:3], halo3[:, dc, :]),
                        waits=[P.tok("ss")] if dc == 0 else [])
                P.add("v", None, incs=[("sv", 1)])
            halo_ok = P.tok("sv")

            for dc in range(NC):
                par = dc % 2
                r2_ok = dma("y", r2c[par][:], r2i[:, dc, sl], s(f"drc{par}"),
                            waits=[r2c_last[par]])
                P.add("v", lambda e, S, par=par: e.tensor_tensor(
                    sqs[:], r2c[par][:], r2c[par][:], ALU.mult),
                    waits=[r2_ok, P.tok("st")], incs=[("sv", 1)])
                P.add("t", lambda e, S, par=par, dc=dc: e.matmul(
                    pA[0:1, :], onec_f[:], r2c[par][:],
                    start=(dc == 0), stop=(dc == NC - 1)),
                    waits=[P.tok("sv")] + ([bankfree[0], bankfree[1]] if dc == 0 else []))
                P.add("t", lambda e, S, dc=dc: e.matmul(
                    pB[0:1, :], onec_b[:], sqs[:],
                    start=(dc == 0), stop=(dc == NC - 1)), incs=[("st", 1)])
                r2c_last[par] = P.tok("st")
            P.add("s", lambda e, S: e.mul(rowm[0:1, :], pA[0:1, :], 1.0 / D),
                  waits=[P.tok("st")])
            P.add("s", lambda e, S: e.mul(rowe[0:1, :], pB[0:1, :], 1.0 / D),
                  incs=[("ss", 1)])
            P.add("v", lambda e, S: e.tensor_tensor(
                tmpf[0:1, 0:TB], rowm[0:1, :], rowm[0:1, :], ALU.mult),
                waits=[P.tok("ss")])
            P.add("v", lambda e, S: e.tensor_tensor(
                rowe[0:1, :], rowe[0:1, :], tmpf[0:1, 0:TB], ALU.subtract))
            P.add("v", lambda e, S: e.tensor_scalar_mul(rowm[0:1, :], rowm[0:1, :], -1.0),
                  incs=[("sv", 1)])
            P.add("s", lambda e, S: e.activation(rowe[0:1, :], rowe[0:1, :], AF.Ln,
                                                 bias=cstf[0:1, 0:1]), waits=[P.tok("sv")])
            P.add("s", lambda e, S: e.activation(rowe[0:1, :], rowe[0:1, :], AF.Exp,
                                                 scale=cstf[0:1, 1:2]), incs=[("ss", 1)])
            P.add("t", lambda e, S: e.matmul(pA[:], oner_f[:], rowm[0:1, :],
                                             start=True, stop=True),
                  waits=[P.tok("ss")], incs=[("st", 1)])
            P.add("v", lambda e, S: e.tensor_copy(nmurep[:], pA[:]),
                  waits=[P.tok("st")], incs=[("sv", 1)])
            P.add("t", lambda e, S: e.matmul(pB[:], oner_f[:], rowe[0:1, :],
                                             start=True, stop=True), incs=[("st", 1)])
            P.add("v", lambda e, S: e.tensor_copy(rstdrep[:], pB[:]),
                  waits=[P.tok("st")], incs=[("sv", 1)])
            rep_ok = P.tok("sv")
            bankfree = [rep_ok, rep_ok]

            for dc in range(NC):
                par = dc % 2
                r2_ok = dma("y", r2c[par][:], r2i[:, dc, sl], s(f"drc{par}"),
                            waits=[r2c_last[par]])
                P.add("v", lambda e, S, dc=dc, par=par: e.tensor_tensor(
                    tmpf[:], r2c[par][:], nmurep[:], ALU.add),
                    waits=[r2_ok] + ([rep_ok] if dc == 0 else []))
                P.add("v", lambda e, S, dc=dc: e.tensor_tensor(
                    xln[:, dc, :], tmpf[:], rstdrep[:], ALU.mult), incs=[("sv", 1)])
                r2c_last[par] = P.tok("sv")
            xln_ok = P.tok("sv")

            for oc in range(64):
                par = oc % 2
                wi_ok = dma("g", winb[par][:], win_d[oc], s(f"dwi{par}"),
                            waits=[win_last[par]])
                pX = PS[par]
                P.add("t", None, waits=[xln_ok, wi_ok, bankfree[par]] +
                      ([halo_ok] if oc < DIC else []))
                for dc in range(NC):
                    P.add("t", lambda e, S, dc=dc, par=par, pX=pX: e.matmul(
                        pX[:], winb[par][:, dc, :], xln[:, dc, :],
                        start=(dc == 0), stop=(dc == NC - 1)))
                P.add("t", None, incs=[("st", 1)])
                mm_ok = P.tok("st")
                win_last[par] = mm_ok
                if oc < DIC:
                    P.add("v", lambda e, S, oc=oc, pX=pX: e.tensor_copy(
                        xconv[:, oc, 3:3 + TB], pX[:]),
                        waits=[mm_ok], incs=[("sv", 1)])
                    bankfree[par] = P.tok("sv")
                    if getattr(g, "fill", None):
                        g.fill(5)
                    if oc == DIC - 1:
                        xpart_ok = P.tok("sv")
                else:
                    P.add("s", lambda e, S, oc=oc, pX=pX: e.activation(
                        zsil[:, oc - DIC, :], pX[:], AF.Silu),
                        waits=[mm_ok], incs=[("ss", 1)])
                    bankfree[par] = P.tok("ss")
            inproj_ok = [P.tok("sv"), P.tok("ss")]

            conv_sil = {}
            for dc in range(DIC):
                tb_ = tmpf if dc % 2 == 0 else tmpf2
                P.add("v", lambda e, S, dc=dc, tb_=tb_: e.tensor_scalar(
                    tb_[:], xconv[:, dc, 0:TB], convw[:, dc, 0:1], None, ALU.mult),
                    waits=([xpart_ok] if dc == 0 else []) +
                          ([conv_sil[dc - 2]] if dc >= 2 else []))
                for k2 in range(1, 4):
                    P.add("v", lambda e, S, dc=dc, k2=k2, tb_=tb_: e.scalar_tensor_tensor(
                        tb_[:], xconv[:, dc, k2:k2 + TB], convw[:, dc, k2:k2 + 1],
                        tb_[:], ALU.mult, ALU.add))
                P.add("v", None, incs=[("sv", 1)])
                P.add("s", lambda e, S, dc=dc, tb_=tb_: e.activation(
                    ubuf[:, dc, :], tb_[:], AF.Silu, bias=convb[:, dc:dc + 1]),
                    waits=[P.tok("sv")], incs=[("ss", 1)])
                conv_sil[dc] = P.tok("ss")
                P.add("v", lambda e, S, dc=dc: e.tensor_copy(
                    halo3[:, dc, :], xconv[:, dc, TB:TB + 3]))
            P.add("v", None, incs=[("sv", 1)])
            halo_saved = P.tok("sv")
            u_ok = P.tok("ss")

            wx_ok = dma("g", wxw[:], wxdt_d[:], s("dwx"), waits=[wxw_last])
            P.add("t", None, waits=[u_ok, bankfree[0], wx_ok])
            for dc in range(DIC):
                P.add("t", lambda e, S, dc=dc: e.matmul(
                    pA[:], wxw[:, dc, :], ubuf[:, dc, :],
                    start=(dc == 0), stop=(dc == DIC - 1)))
            P.add("t", None, incs=[("st", 1)])
            wxw_last = P.tok("st")
            P.add("v", lambda e, S: e.tensor_copy(dtr[:], pA[:]),
                  waits=[P.tok("st")], incs=[("sv", 1)])
            bankfree[0] = P.tok("sv")
            for nm_w, rep in ((wxB, brep), (wxC, crep)):
                P.add("t", None, waits=[bankfree[1]])
                for dc in range(DIC):
                    P.add("t", lambda e, S, dc=dc, nm_w=nm_w: e.matmul(
                        pB[0:16, :], nm_w[:, dc, :], ubuf[:, dc, :],
                        start=(dc == 0), stop=(dc == DIC - 1)))
                P.add("t", None, incs=[("st", 1)])
                P.add("s", lambda e, S: e.copy(bc_s[:], pB[0:16, :]),
                      waits=[P.tok("st")], incs=[("ss", 1)])
                bankfree[1] = P.tok("ss")
                for n in range(16):
                    bk = n % 2
                    pX = PS[bk]
                    P.add("t", lambda e, S, n=n, pX=pX: e.matmul(
                        pX[:], e16[:, n * 128:(n + 1) * 128], bc_s[:],
                        start=True, stop=True),
                        waits=[P.tok("ss"), bankfree[bk]], incs=[("st", 1)])
                    P.add("v", lambda e, S, n=n, pX=pX, rep=rep: e.tensor_copy(
                        rep[:, n, :], pX[:]),
                        waits=[P.tok("st")], incs=[("sv", 1)])
                    bankfree[bk] = P.tok("sv")
            bc_ok = P.tok("sv")

            if ts == 0 and getattr(g, "drain_build", None):
                g.drain_build()
            wd_ok = dma("g", wxw[:], wdt_d[:], s("dwx"), waits=[P.tok("st")])
            for dc in range(DIC):
                P.add("t", lambda e, S, dc=dc: e.matmul(
                    pA[:], wxw[:, dc, :], dtr[:], start=True, stop=True),
                    waits=([bankfree[0], bc_ok, wd_ok] if dc == 0 else [bankfree[0]]),
                    incs=[("st", 1)])
                dmm = P.tok("st")
                P.add("v", lambda e, S, dc=dc: e.tensor_scalar(
                    tmpf[:], pA[:], dtb[:, dc:dc + 1], None, ALU.add),
                    waits=[dmm], incs=[("sv", 1)])
                bankfree[0] = P.tok("sv")
                P.add("s", lambda e, S: e.activation(tmpf2[:], tmpf[:], AF.Abs),
                      waits=[P.tok("sv")])
                P.add("s", lambda e, S: e.activation(tmpf2[:], tmpf2[:], AF.Exp,
                                                     scale=cstf[:, 2:3]))
                P.add("s", lambda e, S: e.activation(tmpf2[:], tmpf2[:], AF.Ln,
                                                     bias=cstf[:, 3:4]), incs=[("ss", 1)])
                P.add("v", lambda e, S: e.tensor_scalar_max(delta[:], tmpf[:], 0.0),
                      waits=[P.tok("ss")])
                P.add("v", lambda e, S: e.tensor_tensor(
                    delta[:], delta[:], tmpf2[:], ALU.add), incs=[("sv", 1)])
                delta_ok = P.tok("sv")
                P.add("v", lambda e, S: e.tensor_copy(delb[:], delta[:]))
                P.add("v", lambda e, S, dc=dc: e.tensor_tensor(
                    dub[:], delb[:], ubuf[:, dc, :], ALU.mult))
                for nn in range(16):
                    P.add("v", lambda e, S, nn=nn: e.tensor_tensor(
                        scn[:, nn, :], dub[:], brep[:, nn, :], ALU.mult),
                        waits=(getattr(g, "build_wait", [])
                               if (ts == 0 and dc == 0 and nn == 0) else []))
                P.add("v", None, incs=[("sv", 1)])
                dbu_ok = P.tok("sv")
                half_toks = []
                for half in range(2):
                    ew = []
                    if half == 0:
                        ew = [delta_ok, halo_saved, u_ok]
                    for n in range(8 * half, 8 * half + 8):
                        P.add("s", lambda e, S, dc=dc, n=n: e.activation(
                            dA_all[:, n, :], delta[:], AF.Exp,
                            scale=negA[:, dc, n:n + 1]),
                            waits=(ew if n == 8 * half else []))
                    P.add("s", None, incs=[("ss", 1)])
                    half_toks.append(P.tok("ss"))
                for n in range(16):
                    P.add("v", lambda e, S, dc=dc, n=n: e.tensor_tensor_scan(
                        scn[:, n, :], dA_all[:, n, :], scn[:, n, :],
                        carry[:, dc, n:n + 1], ALU.mult, ALU.add),
                        waits=([half_toks[0], dbu_ok] if n == 0 else
                               ([half_toks[1]] if n == 8 else [])))
                P.add("v", None, incs=[("sv", 1)])
                scan_half_prev = P.tok("sv")
                scan_ok = P.tok("sv")
                P.add("v", lambda e, S, dc=dc: e.tensor_copy(
                    carry[:, dc, :], scn[:, :, TB - 1]))
                P.add("v", lambda e, S: e.tensor_tensor(scn[:], scn[:], crep[:], ALU.mult))
                P.add("v", lambda e, S: e.tensor_tensor(
                    scn[:, 0:8, :], scn[:, 0:8, :], scn[:, 8:16, :], ALU.add))
                P.add("v", lambda e, S: e.tensor_tensor(
                    scn[:, 0:4, :], scn[:, 0:4, :], scn[:, 4:8, :], ALU.add))
                P.add("v", lambda e, S: e.tensor_tensor(
                    scn[:, 0:2, :], scn[:, 0:2, :], scn[:, 2:4, :], ALU.add))
                P.add("v", lambda e, S: e.tensor_tensor(
                    tmpf[:], scn[:, 0, :], scn[:, 1, :], ALU.add))
                P.add("v", lambda e, S, dc=dc: e.scalar_tensor_tensor(
                    tmpf[:], ubuf[:, dc, :], dvec[:, dc:dc + 1], tmpf[:],
                    ALU.mult, ALU.add))
                P.add("v", lambda e, S, dc=dc: e.scalar_tensor_tensor(
                    zsil[:, dc, :], tmpf[:], cstf[:, 4:5], zsil[:, dc, :],
                    ALU.mult, ALU.mult), incs=[("sv", 1)])
            y_ok = P.tok("sv")

            for stg in range(5):
                stride = 16 >> stg
                for base in range(0, DIC, 2 * stride):
                    for i in range(base, base + stride):
                        j = i + stride
                        P.add("v", lambda e, S, i=i, j=j: e.tensor_tensor(
                            btmp[:], zsil[:, i, :], zsil[:, j, :], ALU.subtract),
                            waits=[y_ok] if (stg == 0 and i == 0) else [])
                        P.add("v", lambda e, S, i=i, j=j: e.tensor_tensor(
                            zsil[:, i, :], zsil[:, i, :], zsil[:, j, :], ALU.add))
                        P.add("v", lambda e, S, j=j: e.tensor_copy(
                            zsil[:, j, :], btmp[:]))
            P.add("v", None, incs=[("sv", 1)])
            bfl_ok = P.tok("sv")
            for ec in range(DIC):
                bk = ec % 2
                pX = PS[bk]
                P.add("t", lambda e, S, ec=ec, pX=pX: e.matmul(
                    pX[:], h128p[:], zsil[:, ec, :], start=True, stop=True),
                    waits=[bfl_ok, bankfree[bk]], incs=[("st", 1)])
                P.add("v", lambda e, S, ec=ec, pX=pX: e.tensor_copy(
                    zsil[:, ec, :], pX[:]),
                    waits=[P.tok("st")], incs=[("sv", 1)])
                bankfree[bk] = P.tok("sv")
            t2_ok = P.tok("sv")

            for dco in range(NC):
                par = dco % 2
                wo_ok = dma("g", woutb[:], wout_d[dco], s("dwo0"), waits=[wout_last])
                r2_ok = dma("y", r2c[par][:], r2i[:, dco, sl], s(f"drc{par}"),
                            waits=[r2c_last[par]])
                pX = PS[par]
                P.add("t", None, waits=[t2_ok, wo_ok, bankfree[par]])
                for ec in range(DIC):
                    P.add("t", lambda e, S, ec=ec, pX=pX: e.matmul(
                        pX[:], woutb[:, ec, :], zsil[:, ec, :],
                        start=(ec == 0), stop=(ec == DIC - 1)))
                P.add("t", None, incs=[("st", 1)])
                wout_last = P.tok("st")
                rb3 = r3t[par]
                P.add("v", lambda e, S, pX=pX, par=par, rb3=rb3: e.tensor_tensor(
                    rb3[:], pX[:], r2c[par][:], ALU.add),
                    waits=[P.tok("st"), r2_ok, r3_last[par]], incs=[("sv", 1)])
                bankfree[par] = P.tok("sv")
                r2c_last[par] = P.tok("sv")
                r3_last[par] = dma("y", r3_d[:, dco, sl], rb3[:], s(f"ds{par}"),
                                   waits=[P.tok("sv")])

def emit_l1ffn(g, dram, after=()):
    nc, P, C = g.nc, g.P, g.C
    s = g.s
    pA, pB = g.pA, g.pB
    PS = [g.pA, g.pB]
    onec_b, onec_f, oner_b, oner_f = g.onec_b, g.onec_f, g.oner_b, g.oner_f
    cstf, h128p, h128n = g.cstf, g.h128p, g.h128n
    consts_ok = g.consts_ok
    for _eng in ENGS:
        P.pending[_eng] = P.pending[_eng] + list(after)
    r3i = dram["r3i"]
    gidx_d = dram["gidx"]
    uidx_d = dram["uidx"]
    didx_d = dram["didx"]
    lut_d = dram["lut"]
    y_d = dram["yout"]
    wgu_scr = dram["wgu_scr"]
    wdn_scr = dram["wdn_scr"]


    if True:
        xln = C.alloc([128, NC, TB], BF16)
        hbuf = C.alloc([128, GOC, TB], BF16)
        wgt = [C.alloc([128, NC, 128], BF16) for i in range(2)]
        wut = [C.alloc([128, NC, 128], BF16) for i in range(2)]
        wdt = [C.alloc([128, GOC, 128], BF16) for i in range(2)]
        r3c = [C.alloc([128, TB], F32) for i in range(2)]
        rowm = C.alloc([1, TB], F32)
        rowe = C.alloc([1, TB], F32)
        nmurep = C.alloc([128, TB], F32)
        rstdrep = C.alloc([128, TB], F32)
        tmpf = C.alloc([128, TB], F32)
        silt = C.alloc([128, TB], F32)
        sqs = C.alloc([128, TB], BF16)
        yt = [C.alloc([128, TB], F32) for i in range(2)]

        def dma(q, out_ap, in_ap, sem, waits=()):
            P.add(q, lambda e, S, o=out_ap, i=in_ap: e.dma_start(out=o, in_=i),
                  waits=waits, incs=[(sem, 16)])
            return (sem, P.now(sem))

        build_done = list(getattr(g, "build_wait", [])) or [("sv", 0), ("sv", 0)]

        bankfree = [P.tok("sv"), P.tok("sv")]
        r3c_last = [("sv", 0), ("sv", 0)]
        wg_last = [("st", 0), ("st", 0)]
        wu_last = [("st", 0), ("st", 0)]
        wd_last = [("st", 0), ("st", 0)]
        y_last = [(s("ds0"), 0), (s("ds1"), 0)]
        for ts in range(NTB):
            sl = slice(ts * TB, (ts + 1) * TB)
            for dc in range(NC):
                par = dc % 2
                r3_ok = dma("y", r3c[par][:], r3i[:, dc, sl], s(f"drc{par}"),
                            waits=[r3c_last[par]])
                P.add("v", lambda e, S, par=par: e.tensor_tensor(
                    sqs[:], r3c[par][:], r3c[par][:], ALU.mult),
                    waits=[r3_ok, P.tok("st")], incs=[("sv", 1)])
                P.add("t", lambda e, S, par=par, dc=dc: e.matmul(
                    pA[0:1, :], onec_f[:], r3c[par][:],
                    start=(dc == 0), stop=(dc == NC - 1)),
                    waits=[P.tok("sv")] + ([bankfree[0], bankfree[1]] if dc == 0 else []))
                P.add("t", lambda e, S, dc=dc: e.matmul(
                    pB[0:1, :], onec_b[:], sqs[:],
                    start=(dc == 0), stop=(dc == NC - 1)), incs=[("st", 1)])
                r3c_last[par] = P.tok("st")
            P.add("s", lambda e, S: e.mul(rowm[0:1, :], pA[0:1, :], 1.0 / D),
                  waits=[P.tok("st")])
            P.add("s", lambda e, S: e.mul(rowe[0:1, :], pB[0:1, :], 1.0 / D),
                  incs=[("ss", 1)])
            P.add("v", lambda e, S: e.tensor_tensor(
                tmpf[0:1, 0:TB], rowm[0:1, :], rowm[0:1, :], ALU.mult),
                waits=[P.tok("ss")])
            P.add("v", lambda e, S: e.tensor_tensor(
                rowe[0:1, :], rowe[0:1, :], tmpf[0:1, 0:TB], ALU.subtract))
            P.add("v", lambda e, S: e.tensor_scalar_mul(rowm[0:1, :], rowm[0:1, :], -1.0),
                  incs=[("sv", 1)])
            P.add("s", lambda e, S: e.activation(rowe[0:1, :], rowe[0:1, :], AF.Ln,
                                                 bias=cstf[0:1, 0:1]), waits=[P.tok("sv")])
            P.add("s", lambda e, S: e.activation(rowe[0:1, :], rowe[0:1, :], AF.Exp,
                                                 scale=cstf[0:1, 1:2]), incs=[("ss", 1)])
            P.add("t", lambda e, S: e.matmul(pA[:], oner_f[:], rowm[0:1, :],
                                             start=True, stop=True),
                  waits=[P.tok("ss")], incs=[("st", 1)])
            P.add("v", lambda e, S: e.tensor_copy(nmurep[:], pA[:]),
                  waits=[P.tok("st")], incs=[("sv", 1)])
            P.add("t", lambda e, S: e.matmul(pB[:], oner_f[:], rowe[0:1, :],
                                             start=True, stop=True), incs=[("st", 1)])
            P.add("v", lambda e, S: e.tensor_copy(rstdrep[:], pB[:]),
                  waits=[P.tok("st")], incs=[("sv", 1)])
            rep_ok = P.tok("sv")
            bankfree = [rep_ok, rep_ok]
            for dc in range(NC):
                par = dc % 2
                r3_ok = dma("y", r3c[par][:], r3i[:, dc, sl], s(f"drc{par}"),
                            waits=[r3c_last[par]])
                P.add("v", lambda e, S, dc=dc, par=par: e.tensor_tensor(
                    tmpf[:], r3c[par][:], nmurep[:], ALU.add),
                    waits=[r3_ok] + ([rep_ok] if dc == 0 else []))
                P.add("v", lambda e, S, dc=dc: e.tensor_tensor(
                    xln[:, dc, :], tmpf[:], rstdrep[:], ALU.mult), incs=[("sv", 1)])
                r3c_last[par] = P.tok("sv")
            xln_ok = P.tok("sv")

            for oc in range(GOC):
                par = oc % 2
                wg_ok = dma("y", wgt[par][:], wgu_scr[0, oc], s(f"dwg{par}"),
                            waits=[wg_last[par], build_done[0], build_done[1]])
                wu_ok = dma("y", wut[par][:], wgu_scr[1, oc], s(f"dwu{par}"),
                            waits=[wu_last[par]])
                for hc in range(NC):
                    P.add("t", lambda e, S, hc=hc, par=par: e.matmul(
                        pA[:], wgt[par][:, hc, :], xln[:, hc, :],
                        start=(hc == 0), stop=(hc == NC - 1)),
                        waits=([xln_ok, wg_ok, bankfree[0]] if hc == 0 else []))
                P.add("t", None, incs=[("st", 1)])
                gmm_ok = P.tok("st")
                for hc in range(NC):
                    P.add("t", lambda e, S, hc=hc, par=par: e.matmul(
                        pB[:], wut[par][:, hc, :], xln[:, hc, :],
                        start=(hc == 0), stop=(hc == NC - 1)),
                        waits=([wu_ok, bankfree[1]] if hc == 0 else []))
                P.add("t", None, incs=[("st", 1)])
                mm_ok = P.tok("st")
                wg_last[par] = mm_ok
                wu_last[par] = mm_ok
                P.add("s", lambda e, S: e.activation(silt[:], pA[:], AF.Silu),
                      waits=[gmm_ok], incs=[("ss", 1)])
                bankfree[0] = P.tok("ss")
                P.add("v", lambda e, S, oc=oc: e.tensor_tensor(
                    hbuf[:, oc, :], pB[:], silt[:], ALU.mult),
                    waits=[P.tok("ss"), mm_ok], incs=[("sv", 1)])
                bankfree[1] = P.tok("sv")
            h_ok = P.tok("sv")

            for dco in range(NC):
                par = dco % 2
                wd_ok = dma("y", wdt[par][:], wdn_scr[dco], s(f"dwd{par}"),
                            waits=[wd_last[par]])
                r3_ok = dma("y", r3c[par][:], r3i[:, dco, sl], s(f"drc{par}"),
                            waits=[r3c_last[par]])
                pX = PS[par]
                P.add("t", None, waits=[h_ok, wd_ok, bankfree[par]])
                for ic in range(GOC):
                    P.add("t", lambda e, S, ic=ic, par=par, pX=pX: e.matmul(
                        pX[:], wdt[par][:, ic, :], hbuf[:, ic, :],
                        start=(ic == 0), stop=(ic == GOC - 1)))
                P.add("t", None, incs=[("st", 1)])
                wd_last[par] = P.tok("st")
                yb = yt[par]
                P.add("v", lambda e, S, pX=pX, par=par, yb=yb: e.tensor_tensor(
                    yb[:], pX[:], r3c[par][:], ALU.add),
                    waits=[P.tok("st"), r3_ok, y_last[par]], incs=[("sv", 1)])
                bankfree[par] = P.tok("sv")
                r3c_last[par] = P.tok("sv")
                y_last[par] = dma("y", y_d[:, dco, sl], yb[:], s(f"ds{par}"),
                                  waits=[P.tok("sv")])




def mk_lut_build(g, dram):
    P, C = g.P, g.C
    idxb = [C.alloc([128, 8, 128], I32)] * 2
    idxf = C.alloc([128, 1024], BF16)
    contrib = C.alloc([128, 1024], BF16)
    wacc = [C.alloc([128, 1024], BF16) for _ in range(2)]
    lut_s = C.alloc([1, 16], F32)
    lutrep = C.alloc([128, 16], F32)
    P.sem("b_di0", "b_di1", "b_ws0", "b_ws1", "b_dl")

    class BB:
        pass
    bb = BB()
    jobs = [(kind, ic, half) for kind in range(3) for ic in range(GOC)
            for half in range(2)]
    bb.jobs = jobs
    bb.i = 0
    bb.cast_tok = [("sv", 0), ("sv", 0)]
    bb.ws_last = [("b_ws0", 0), ("b_ws1", 0)]

    def prelude():
        P.add("g", lambda e, S: e.dma_start(out=lut_s, in_=dram["lut"]),
              incs=[("b_dl", 16)])
        P.add("t", lambda e, S: e.matmul(g.pA[:, 0:16], g.oner_f, lut_s,
                                         start=True, stop=True),
              waits=[("b_dl", 16), g.consts_ok], incs=[("st", 1)])
        P.add("v", lambda e, S: e.tensor_copy(lutrep, g.pA[:, 0:16]),
              waits=[P.tok("st")], incs=[("sv", 1)])
        bb.lutrep_ok = P.tok("sv")
    bb.prelude = prelude

    bb.op = 0

    def _emit_one():
        if bb.i >= len(jobs):
            return False
        kind, ic, half = jobs[bb.i]
        par = bb.i % 2
        wa = wacc[par]
        op = bb.op
        if op == 0:
            src_d = (dram["gidx"], dram["uidx"], dram["didx"])[kind]
            in_ap = src_d[ic][:, 8 * half:8 * half + 8, :]
            ib = idxb[0]
            P.add("g", lambda e, S, ib=ib, in_ap=in_ap: e.dma_start(
                out=ib[:], in_=in_ap),
                waits=[bb.cast_tok[0]], incs=[("b_di0", 16)])
            i_ok = ("b_di0", P.now("b_di0"))
            P.add("v", lambda e, S, ib=ib: e.tensor_copy(idxf[:], ib[:]),
                  waits=[i_ok, bb.lutrep_ok], incs=[("sv", 1)])
            bb.cast_tok[0] = P.tok("sv")
        elif op == 1:
            P.add("v", lambda e, S, wa=wa: e.tensor_scalar(
                wa[:], idxf[:], 0.0, lutrep[:, 0:1],
                ALU.is_equal, ALU.mult), waits=[bb.ws_last[par]])
        else:
            v, phase = divmod(op - 2, 2)
            v += 1
            if phase == 0:
                P.add("v", lambda e, S, v=v: e.tensor_scalar(
                    contrib[:], idxf[:], float(v), lutrep[:, v:v + 1],
                    ALU.is_equal, ALU.mult))
            else:
                P.add("v", lambda e, S, wa=wa: e.tensor_tensor(
                    wa[:], wa[:], contrib[:], ALU.add))
        bb.op += 1
        if bb.op == 32:
            P.add("v", None, incs=[("sv", 1)])
            bdone = P.tok("sv")
            if kind < 2:
                out_ap = dram["wgu_scr"][kind, ic][:, 8 * half:8 * half + 8, :]
            else:
                out_ap = dram["wdn_scr"][:, :, ic, :].transpose([1, 0, 2])[
                    :, 8 * half:8 * half + 8, :]
            P.add("y", lambda e, S, out_ap=out_ap, wa=wa: e.dma_start(
                out=out_ap, in_=wa[:].rearrange("p (c j) -> p c j", j=128)),
                waits=[bdone], incs=[(f"b_ws{par}", 16)])
            bb.ws_last[par] = (f"b_ws{par}", P.now(f"b_ws{par}"))
            bb.op = 0
            bb.i += 1
        return True

    def fill(k):
        for _ in range(k):
            if not _emit_one():
                return False
        return True
    bb.fill = fill

    def emit_job():
        return fill(32)
    bb.emit_job = emit_job
    return bb



def build_fused():
    nc = bass.Bass("TRN2", target_bir_lowering=False, debug=False)
    d = {}
    d["xin"] = nc.dram_tensor("xin", [128, NC, T], F32, kind="ExternalInput").ap()
    d["wq"] = nc.dram_tensor("wq", [NH, 128, NC, DH], F32, kind="ExternalInput").ap()
    d["wk"] = nc.dram_tensor("wk", [NH, 128, NC, DH], F32, kind="ExternalInput").ap()
    d["wv"] = nc.dram_tensor("wv", [NH, 128, NC, DH], F32, kind="ExternalInput").ap()
    d["wo"] = nc.dram_tensor("wo", [NC, 128, NH, DH], F32, kind="ExternalInput").ap()
    d["cm"] = nc.dram_tensor("cm", [128, 4, TB], F32, kind="ExternalInput").ap()
    d["h128"] = nc.dram_tensor("h128", [128, 128], F32, kind="ExternalInput").ap()
    d["su_ug"] = nc.dram_tensor("su_ug", [128, NC], F32, kind="ExternalInput").ap()
    d["su_dp"] = nc.dram_tensor("su_dp", [128, 64], F32, kind="ExternalInput").ap()
    d["wsc_up"] = nc.dram_tensor("wsc_up", [128, 64], F32, kind="ExternalInput").ap()
    d["wsc_gt"] = nc.dram_tensor("wsc_gt", [128, 64], F32, kind="ExternalInput").ap()
    d["wsc_dn"] = nc.dram_tensor("wsc_dn", [128, NC], F32, kind="ExternalInput").ap()
    d["wup"] = nc.dram_tensor("wup", [64, 128, NC, 128], F32, kind="ExternalInput").ap()
    d["wgt"] = nc.dram_tensor("wgt", [64, 128, NC, 128], F32, kind="ExternalInput").ap()
    d["wdn"] = nc.dram_tensor("wdn", [NC, 128, 64, 128], F32, kind="ExternalInput").ap()
    d["win"] = nc.dram_tensor("win", [64, 128, NC, 128], F32, kind="ExternalInput").ap()
    d["convw"] = nc.dram_tensor("convw", [128, DIC, 4], F32, kind="ExternalInput").ap()
    d["convb"] = nc.dram_tensor("convb", [128, DIC], F32, kind="ExternalInput").ap()
    d["wxdt"] = nc.dram_tensor("wxdt", [128, DIC, 128], F32, kind="ExternalInput").ap()
    d["wxB"] = nc.dram_tensor("wxB", [128, DIC, 16], F32, kind="ExternalInput").ap()
    d["wxC"] = nc.dram_tensor("wxC", [128, DIC, 16], F32, kind="ExternalInput").ap()
    d["wdt"] = nc.dram_tensor("wdt", [128, DIC, 128], F32, kind="ExternalInput").ap()
    d["dtb"] = nc.dram_tensor("dtb", [128, DIC], F32, kind="ExternalInput").ap()
    d["alog"] = nc.dram_tensor("alog", [128, DIC, 16], F32, kind="ExternalInput").ap()
    d["dvec"] = nc.dram_tensor("dvec", [128, DIC], F32, kind="ExternalInput").ap()
    d["wout"] = nc.dram_tensor("wout", [NC, 128, DIC, 128], F32, kind="ExternalInput").ap()
    d["e16"] = nc.dram_tensor("e16", [16, 16 * 128], F32, kind="ExternalInput").ap()
    d["gidx"] = nc.dram_tensor("gidx", [GOC, 128, NC, 128], I32, kind="ExternalInput").ap()
    d["uidx"] = nc.dram_tensor("uidx", [GOC, 128, NC, 128], I32, kind="ExternalInput").ap()
    d["didx"] = nc.dram_tensor("didx", [GOC, 128, NC, 128], I32, kind="ExternalInput").ap()
    d["lut"] = nc.dram_tensor("lut", [1, 16], F32, kind="ExternalInput").ap()
    d["yout"] = nc.dram_tensor("yout", [128, NC, T], F32, kind="ExternalOutput").ap()
    d["zscr"] = nc.dram_tensor("zscr", [NH, 128, T], BF16, kind="Internal").ap()
    d["r1"] = nc.dram_tensor("r1", [128, NC, T], F32, kind="Internal").ap()
    d["r2"] = nc.dram_tensor("r2", [128, NC, T], F32, kind="Internal").ap()
    d["r3"] = nc.dram_tensor("r3", [128, NC, T], F32, kind="Internal").ap()
    d["r1i"] = d["r1"]
    d["r2i"] = d["r2"]
    d["r3i"] = d["r3"]
    d["wgu_scr"] = nc.dram_tensor("wgu_scr", [2, GOC, 128, NC, 128], BF16, kind="Internal").ap()
    d["wdn_scr"] = nc.dram_tensor("wdn_scr", [NC, 128, GOC, 128], BF16, kind="Internal").ap()

    g = GCtx()
    g.nc = nc
    P = g.P = Plan()
    P.sem("st", "sv", "ss", "sg")
    prefix_holder = {"px": ""}

    def s(nm):
        if nm in ("st", "sv", "ss", "sg"):
            return nm
        full = prefix_holder["px"] + nm
        P.sem(full)
        return full
    g.s = s

    with contextlib.ExitStack() as ctx:
        E = ctx.enter_context
        arena_t = E(nc.sbuf_tensor("arena", [128, ARENA_BYTES], mybir.dt.int8))
        g.C = Carver(arena_t[:])
        g.onec_b = E(nc.sbuf_tensor("g_onec_b", [128, 1], BF16))[:]
        g.onec_f = E(nc.sbuf_tensor("g_onec_f", [128, 1], F32))[:]
        g.oner_b = E(nc.sbuf_tensor("g_oner_b", [1, 128], BF16))[:]
        g.oner_f = E(nc.sbuf_tensor("g_oner_f", [1, 128], F32))[:]
        g.cstf = E(nc.sbuf_tensor("g_cstf", [128, 8], F32))[:]
        g.h128p = E(nc.sbuf_tensor("g_h128p", [128, 128], BF16))[:]
        g.h128n = E(nc.sbuf_tensor("g_h128n", [128, 128], BF16))[:]
        g.pA = E(nc.psum_tensor("pA", [128, TB], F32))
        g.pB = E(nc.psum_tensor("pB", [128, TB], F32))

        P.add("g", lambda e, S: e.memset(g.onec_b, 1.0))
        P.add("g", lambda e, S: e.memset(g.onec_f, 1.0))
        P.add("g", lambda e, S: e.memset(g.oner_b, 1.0))
        P.add("g", lambda e, S: e.memset(g.oner_f, 1.0))
        for slot, val in ((0, LN_EPS), (1, -0.5), (2, -1.0), (3, 1.0),
                          (4, float(2.0 ** -6)), (5, 1.0 / 128.0), (6, 0.5)):
            P.add("g", lambda e, S, slot=slot, val=val: e.memset(
                g.cstf[:, slot:slot + 1], val))
        P.add("g", lambda e, S: e.dma_start(out=g.h128p, in_=d["h128"]),
              incs=[("sg_h", 16)] if P.sem("sg_h") or True else [])
        P.add("v", lambda e, S: e.tensor_scalar_mul(g.h128n, g.h128p, -1.0),
              waits=[("sg_h", 16)], incs=[("sv", 1)])
        P.add("g", lambda e, S: e.nop())
        P.add("g", None, incs=[("sg", 1)])
        g.consts_ok = P.tok("sg")

        bb = mk_lut_build(g, d)
        g.C.floor = g.C.off
        bb.prelude()
        g.fill = bb.fill
        prefix_holder["px"] = "a_"
        emit_l0attn(g, d, after=[P.tok("sv")])
        P.barrier(("t", "v", "s"))
        after = [P.tok("a_ds0"), P.tok("a_ds1")]
        g.C.reset()
        prefix_holder["px"] = "f_"
        emit_l0ffn(g, d, after=after)
        def drain_build():
            while bb.emit_job():
                pass
            g.build_wait = [bb.ws_last[0], bb.ws_last[1]]
            g.fill = None
        g.drain_build = drain_build
        P.barrier(("t", "v", "s"))
        after = [P.tok("f_ds0"), P.tok("f_ds1")]
        g.C.floor = 0
        g.C.reset()
        prefix_holder["px"] = "m_"
        emit_l1attn(g, d, after=after)
        P.barrier(("t", "v", "s"))
        after = [P.tok("m_ds0"), P.tok("m_ds1")]
        g.C.reset()
        prefix_holder["px"] = "g_"
        emit_l1ffn(g, d, after=after)
        P.add("y", None, waits=[P.tok("g_ds0"), P.tok("g_ds1")])

        sems = {n: E(nc.semaphore(n)) for n in P.names}
        P.emit(nc, sems)
    return nc


def kernel_fused(**inputs):
    I = {k: np.asarray(v) for k, v in inputs.items()}
    x = I["L0_attn_x"]
    w = {}
    w.update(prep_l0attn_host(x[0], I["L0_attn_W_K"], I["L0_attn_W_Q"],
                              I["L0_attn_W_V"], I["L0_attn_W_O"]))
    del w["xin"]
    w.update(prep_l0ffn_host(I["L0_ffn_SU_ug"], I["L0_ffn_SU_dp"],
                             I["L0_ffn_Wscale_ug"], I["L0_ffn_Wscale_dp"],
                             I["L0_ffn_W_up"], I["L0_ffn_W_gate"], I["L0_ffn_W_down"]))
    w.update(prep_l1attn_host(I["L1_attn_W_in"], I["L1_attn_conv_w"],
                              I["L1_attn_conv_b"], I["L1_attn_W_x"],
                              I["L1_attn_W_dt"], I["L1_attn_dt_bias"],
                              I["L1_attn_A_log"], I["L1_attn_D"], I["L1_attn_W_out"]))
    w.update(prep_l1ffn_host(I["L1_ffn_lut"], I["L1_ffn_gate_idx"],
                             I["L1_ffn_up_idx"], I["L1_ffn_down_idx"]))
    in_maps = []
    for b in range(NCORES):
        hp = dict(w)
        hp["xin"] = pack_fm(x[b])
        in_maps.append(hp)
    nc = build_fused()
    res = run_bass_kernel_spmd(nc, in_maps, core_ids=list(range(NCORES)))
    out = np.empty((NCORES, T, D), np.float32)
    for b in range(NCORES):
        out[b] = unpack_fm(res.results[b]["yout"])
    return out
